# version 02
# baseline (speedup 1.0000x reference)
import numpy as np
from contextlib import ExitStack

import concourse.bass as bass
import concourse.mybir as mb
from concourse.bass import AP
from concourse.bass_utils import run_bass_kernel_spmd

F32 = mb.dt.float32
BF16 = mb.dt.bfloat16
I32 = mb.dt.int32
OP = mb.AluOpType
AF = mb.ActivationFunctionType
AX = mb.AxisListType.X

NCORES = 8
DM = 1024
GD = DM // 128
DIN = 2048
GI = DIN // 128
NST = 16
DTR = 64
HID0 = 3072
G0 = HID0 // 128
NH1 = 16
HD = 64
NH2 = 8
INNER2 = 512
HID2 = 2730
F1 = 5460
WIN = 512
LNEPS = 1e-5

ENGINES = ("sync", "vector", "scalar", "tensor", "gpsimd")


def ts(i, n=128):
    return slice(i * n, (i + 1) * n)


class Sem:
    __slots__ = ("h", "n", "name")

    def __init__(self, h, name):
        self.h = h
        self.n = 0
        self.name = name


class Ctx:

    def __init__(self, nc, es, L):
        self.nc = nc
        self.es = es
        self.L = L
        self.streams = {e: [] for e in ENGINES}
        self.sems = {}

    _CANON = None

    def _canon(self, name):
        if name.startswith("bar_"):
            return "bar"
        for suf, canon in (("_v", "g_v"), ("_s", "g_s"), ("_t", "g_t"),
                           ("_ng", "g_dgB"), ("_rt", "g_dgB"), ("_mk", "g_dsB"),
                           ("_wv", "g_dgC"), ("_wo", "g_dgD"),
                           ("_kh", "g_dsC"), ("_khr", "g_khr"),

                           ("_pre", "g_dsE"), ("_preg", "g_dgE"),
                           ("_p", "g_dsE"), ("_pg", "g_dgE")):
            if name.endswith(suf):
                return canon
        return name

    def sem(self, name):
        name = self._canon(name)
        if name not in self.sems:
            h = self.es.enter_context(self.nc.semaphore(name))
            self.sems[name] = Sem(h, name)
        return self.sems[name]

    def emit(self, eng, fn):
        self.streams[eng].append(fn)

    def rec(self, eng, kind, sem, val):
        if not hasattr(self, "trace"):
            self.trace = {e: [] for e in ENGINES}
        self.trace[eng].append((kind, sem, val))

    def wait(self, eng, sem, val=None):
        v = sem.n if val is None else val
        if v <= 0:
            return
        self.rec(eng, "wait", sem.name, v)
        self.emit(eng, lambda e, s=sem.h, vv=v: e.wait_ge(s, vv))

    def op(self, eng, fn, inc=None):
        if inc is None:
            self.emit(eng, fn)
        else:
            sem, k = inc
            sem.n += k
            self.rec(eng, "inc", sem.name, k)
            self.emit(eng, lambda e, f=fn, s=sem.h, kk=k: f(e).then_inc(s, kk))

    def dma(self, eng, dst, src, sem):
        sem.n += 16
        self.rec(eng, "inc", sem.name, 16)
        self.emit(eng, lambda e, dd=dst, ss=src, h=sem.h: e.dma_start(dd, ss).then_inc(h, 16))

    def di(self, eng, sem, k=1):
        sem.n += k
        self.rec(eng, "inc", sem.name, k)
        self.emit(eng, lambda e, s=sem.h, kk=k: (e.drain(), e.sem_inc(s, kk)))

    def barrier(self, tag):
        s = self.sem("bar_" + tag)
        for e in ENGINES:
            s.n += 1
            self.rec(e, "inc", s.name, 1)
            self.emit(e, lambda en, h=s.h: (en.drain(), en.sem_inc(h, 1)))
        for e in ENGINES:
            self.wait(e, s)


class Ring:

    def __init__(self, c, name, bufs):
        self.c = c
        self.bufs = bufs
        self.sd = [c.sem(f"ring_d{i}") for i in range(len(bufs))]
        self.sr = c.sem("ring_r")
        self.iprod = 0
        self.icons = 0
        self.targets = []
        self.rbase = self.sr.n

    def load(self, src_ap, ncols, rows=128):
        i = self.iprod
        nb = len(self.bufs)
        if i >= nb:
            self.c.wait("gpsimd", self.sr, self.rbase + i - nb + 1)
            self.c.wait("gpsimd", self.sd[i % nb], self.targets[i - nb])
        self.c.dma("gpsimd", self.bufs[i % nb][0:rows, 0:ncols], src_ap, self.sd[i % nb])
        self.targets.append(self.sd[i % nb].n)
        self.iprod += 1

    def take(self, eng="tensor"):
        i = self.icons
        nb = len(self.bufs)
        self.c.wait(eng, self.sd[i % nb], self.targets[i])
        self.icons += 1
        return self.bufs[i % nb]

    def release(self, eng="tensor"):
        self.c.di(eng, self.sr)



def host_constants(L):
    import ml_dtypes
    c = {}
    c["ident"] = np.eye(128, dtype=np.float32)
    nkc = L // 128
    alibi = np.zeros((128, NH1 * nkc), np.float32)
    p = np.arange(128, dtype=np.float32)[:, None]
    for h in range(NH1):
        slope = 0.5 ** h
        for kc in range(nkc):
            alibi[:, h * nkc + kc : h * nkc + kc + 1] = slope * (kc * 128 + p)
    c["alibi"] = alibi
    half = np.arange(0, HD, 2, dtype=np.float32)
    inv_freq = 1.0 / 10000.0 ** (half / HD)
    tt = np.arange(L, dtype=np.float32)
    fr = tt[:, None] * inv_freq[None, :]
    fr = np.concatenate([fr, fr], axis=-1)
    base = (half + 0.4 * HD) / (1.4 * HD)
    power = (tt - L // 2) / float(WIN // 2)
    sc = base[None, :] ** power[:, None]
    sc = np.concatenate([sc, sc], axis=-1)
    cos, sin = np.cos(fr), np.sin(fr)
    t1q = (cos * sc).T.astype(np.float32)
    t2q = (sin * sc).T.astype(np.float32)
    t1k = (cos / sc).T.astype(np.float32)
    t2k = (sin / sc).T.astype(np.float32)
    for t2 in (t2q, t2k):
        t2[0:32, :] *= -1.0

    def dup(a):
        return np.ascontiguousarray(np.concatenate([a, a], axis=0)).astype(ml_dtypes.bfloat16)

    c["rotq1"], c["rotq2"] = dup(t1q), dup(t2q)
    c["rotk1"], c["rotk2"] = dup(t1k), dup(t2k)
    NEG = np.float32(-3.0e38)
    a = np.arange(128)
    c["mask_own_kq"] = np.where(a[:, None] > a[None, :], NEG, 0).astype(np.float32)
    c["mask_prev_kq"] = np.where(a[:, None] < a[None, :], NEG, 0).astype(np.float32)
    c["mask_own_qk"] = np.ascontiguousarray(c["mask_own_kq"].T)
    c["mask_prev_qk"] = np.ascontiguousarray(c["mask_prev_kq"].T)
    return c



def ln_full(ctx, t, src_col, dst_col, tag, gamma=None, beta=None,
            update_x=False):
    c = ctx
    L = c.L
    T = min(512, L)
    sv = c.sem(tag + "_v")
    ss = c.sem(tag + "_s")
    st = c.sem(tag + "_t")
    for tc in range(L // T):
        sl = slice(tc * T, (tc + 1) * T)
        ps_sum = t.PW0[:, 0:T]
        ps_sq = t.PW0[:, 512 : 512 + T]
        c.wait("vector", st)
        for g in range(GD):
            c.op("vector", lambda v, g=g, sl=sl: v.tensor_copy(t.xb_col(g, sl), src_col(g, sl)))
        c.di("vector", sv)
        c.wait("tensor", sv)
        for g in range(GD):
            pp = t.ln_r2a if g % 2 == 0 else t.ln_r2b
            c.wait("scalar", st, st.n - 1 if g >= 2 else st.n)
            c.op("scalar", lambda s, g=g, pp=pp, sl=sl: s.activation(
                pp[:, 0:T], src_col(g, sl), AF.Square), inc=None)
            c.di("scalar", ss)
            c.wait("tensor", ss)
            first, last = g == 0, g == GD - 1
            c.op("tensor", lambda e, g=g, sl=sl, f=first, l=last: e.matmul(
                ps_sum, t.ones_b[:], t.xb_col(g, sl), start=f, stop=l))
            c.op("tensor", lambda e, g=g, pp=pp, f=first, l=last: e.matmul(
                ps_sq, t.ones_b[:], pp[:, 0:T], start=f, stop=l),
                inc=(st, 1))
        c.wait("vector", st)
        c.op("vector", lambda v: v.tensor_scalar_mul(t.ln_mu[:, 0:T], ps_sum, 1.0 / DM))
        c.di("vector", sv)
        c.wait("scalar", sv)
        c.op("scalar", lambda s: s.activation(t.ln_m2[:, 0:T], t.ln_mu[:, 0:T], AF.Square))
        c.di("scalar", ss)
        c.wait("vector", ss)
        c.op("vector", lambda v: v.scalar_tensor_tensor(
            t.ln_var[:, 0:T], ps_sq, 1.0 / DM, t.ln_m2[:, 0:T], OP.mult, OP.subtract))
        c.di("vector", sv)
        c.wait("scalar", sv)
        c.op("scalar", lambda s: s.activation(t.ln_var[:, 0:T], t.ln_var[:, 0:T], AF.Ln, bias=t.ln_eps_ap()))
        c.emit("scalar", lambda s: s.drain())
        c.op("scalar", lambda s: s.activation(t.ln_rs[:, 0:T], t.ln_var[:, 0:T], AF.Exp, scale=-0.5))
        c.di("scalar", ss)
        c.wait("vector", ss)
        c.op("vector", lambda v: v.tensor_mul(t.ln_m2[:, 0:T], t.ln_mu[:, 0:T], t.ln_rs[:, 0:T]))
        c.emit("vector", lambda v: v.drain())
        for g in range(GD):
            c.op("vector", lambda v, g=g, sl=sl: v.tensor_mul(
                t.ln_t1[:, 0:T], src_col(g, sl), t.ln_rs[:, 0:T]))
            c.emit("vector", lambda v: v.drain())
            if gamma is None:
                c.op("vector", lambda v, g=g, sl=sl: v.tensor_sub(
                    dst_col(g, sl), t.ln_t1[:, 0:T], t.ln_m2[:, 0:T]))
                c.emit("vector", lambda v: v.drain())
            else:
                c.op("vector", lambda v, g=g: v.tensor_sub(
                    t.ln_t1[:, 0:T], t.ln_t1[:, 0:T], t.ln_m2[:, 0:T]))
                c.di("vector", sv)
                c.wait("scalar", sv)
                c.op("scalar", lambda s, g=g, sl=sl: s.activation(
                    dst_col(g, sl), t.ln_t1[:, 0:T], AF.Identity,
                    bias=beta[:, g : g + 1], scale=gamma[:, g : g + 1]))
                c.di("scalar", ss)
                c.wait("vector", ss)
        if update_x:
            for g in range(GD):
                c.op("vector", lambda v, g=g, sl=sl: v.tensor_copy(
                    t.xb_col(g, sl), t.x_col(g, sl)))
            c.emit("vector", lambda v: v.drain())
        c.di("vector", sv)
        c.wait("tensor", sv)
    c.barrier(tag)



def ph_mamba(ctx, d, t, m):
    c = ctx
    L = c.L
    T = min(512, L)
    NTC = L // T
    sv = c.sem("mb_v")
    ss = c.sem("mb_s")
    stt = c.sem("mb_t")
    sdw = [c.sem("ring_d0"), c.sem("ring_d1")]
    s_scan = c.sem("mb_scan")
    svw = c.sem("ring_r")
    svw_base = svw.n
    sdw_targets = []

    s_mp = c.sem("mb_pre")
    s_mpg = c.sem("mb_preg")
    for g in range(GI):
        c.dma("sync", m.convw[:, g * 4 : g * 4 + 4], d["convw"][ts(g), :], s_mp)
        c.dma("sync", m.convb[:, g : g + 1], d["convb"][ts(g), :], s_mp)
        c.dma("sync", m.dtpb[:, g : g + 1], d["dtprojb"][ts(g), :], s_mp)
        c.dma("sync", m.negA[:, g * NST : (g + 1) * NST], d["alog"][ts(g), :], s_mp)
        c.dma("sync", m.D[:, g : g + 1], d["dvec"][ts(g), :], s_mp)
    for g in range(GD):
        c.dma("sync", m.lng[:, g : g + 1], d["mlng"][ts(g), :], s_mp)
        c.dma("sync", m.lnb[:, g : g + 1], d["mlnb"][ts(g), :], s_mp)
    for g in range(GI):
        c.dma("gpsimd", m.xproj[:, g * 96 : (g + 1) * 96], d["xproj"][ts(g), :], s_mpg)
    c.dma("gpsimd", m.dtw[:], d["dtproj"][:], s_mpg)
    c.wait("scalar", s_mp)
    c.wait("scalar", s_mpg)
    c.op("scalar", lambda s: s.activation(m.negA[:], m.negA[:], AF.Exp))
    c.di("scalar", ss)
    c.wait("vector", ss)
    c.op("vector", lambda v: v.tensor_scalar_mul(m.negA[:], m.negA[:], -1.0))
    c.op("vector", lambda v: v.memset(m.states[:], 0.0))
    c.op("vector", lambda v: v.memset(m.hist[:], 0.0))
    c.di("vector", sv)
    c.wait("tensor", sv)
    c.wait("tensor", s_mp)
    c.wait("tensor", s_mpg)

    slab_emit_i = [0]

    def emit_slab_load(kind, ob):
        i = slab_emit_i[0]
        slab_emit_i[0] += 1
        if i >= 2:
            c.wait("gpsimd", svw, svw_base + i - 1)
            c.wait("gpsimd", sdw[i % 2], sdw_targets[i - 2])
        buf = m.wslab[i % 2]
        sdwi = sdw[i % 2]
        if kind == "in":
            srcap = AP(d["inproj"].tensor, ob * 512,
                       [[2 * DIN, 128], [128 * 2 * DIN, GD], [1, 512]])
            dstap = buf[:, 0 : GD * 512]
        else:
            srcap = AP(d["outproj"].tensor, ob * 256,
                       [[DM, 128], [128 * DM, GI], [1, 256]])
            dstap = buf[:, 0 : GI * 256]
        c.dma("gpsimd", dstap, srcap, sdwi)
        sdw_targets.append(sdwi.n)

    slab_i = [0]

    def next_slab():
        i = slab_i[0]
        slab_i[0] += 1
        c.wait("tensor", sdw[i % 2], sdw_targets[i])
        return m.wslab[i % 2], i

    def release_slab(i):
        c.di("tensor", svw)

    for tc in range(NTC):
        sl = slice(tc * T, (tc + 1) * T)
        for ob in range(8):
            emit_slab_load("in", ob)

        ps_sum = t.PW0[:, 0:T]
        ps_sq = t.PW0[:, 512 : 512 + T]
        for g in range(GD):
            pp = t.ln_r2a if g % 2 == 0 else t.ln_r2b
            c.wait("scalar", stt)
            c.op("scalar", lambda s, g=g, pp=pp, sl=sl: s.activation(
                pp[:, 0:T], t.x_col(g, sl), AF.Square))
            c.di("scalar", ss)
            c.wait("tensor", ss)
            first, last = g == 0, g == GD - 1
            c.op("tensor", lambda e, g=g, sl=sl, f=first, l=last: e.matmul(
                ps_sum, t.ones_b[:], t.xb_col(g, sl), start=f, stop=l))
            c.op("tensor", lambda e, pp=pp, f=first, l=last: e.matmul(
                ps_sq, t.ones_b[:], pp[:, 0:T], start=f, stop=l), inc=(stt, 1))
        c.wait("vector", stt)
        c.op("vector", lambda v: v.tensor_scalar_mul(t.ln_mu[:, 0:T], ps_sum, 1.0 / DM))
        c.di("vector", sv)
        c.wait("scalar", sv)
        c.op("scalar", lambda s: s.activation(t.ln_m2[:, 0:T], t.ln_mu[:, 0:T], AF.Square))
        c.di("scalar", ss)
        c.wait("vector", ss)
        c.op("vector", lambda v: v.scalar_tensor_tensor(
            t.ln_var[:, 0:T], ps_sq, 1.0 / DM, t.ln_m2[:, 0:T], OP.mult, OP.subtract))
        c.di("vector", sv)
        c.wait("scalar", sv)
        c.op("scalar", lambda s: s.activation(t.ln_var[:, 0:T], t.ln_var[:, 0:T], AF.Ln, bias=t.ln_eps_ap()))
        c.emit("scalar", lambda s: s.drain())
        c.op("scalar", lambda s: s.activation(t.ln_rs[:, 0:T], t.ln_var[:, 0:T], AF.Exp, scale=-0.5))
        c.di("scalar", ss)
        c.wait("vector", ss)
        c.op("vector", lambda v: v.tensor_mul(t.ln_m2[:, 0:T], t.ln_mu[:, 0:T], t.ln_rs[:, 0:T]))
        c.emit("vector", lambda v: v.drain())
        for g in range(GD):
            c.wait("vector", ss)
            c.op("vector", lambda v, g=g, sl=sl: v.tensor_mul(
                t.ln_t1[:, 0:T], t.x_col(g, sl), t.ln_rs[:, 0:T]))
            c.emit("vector", lambda v: v.drain())
            c.op("vector", lambda v, g=g: v.tensor_sub(
                t.ln_t1[:, 0:T], t.ln_t1[:, 0:T], t.ln_m2[:, 0:T]))
            c.di("vector", sv)
            c.wait("scalar", sv)
            c.op("scalar", lambda s, g=g: s.activation(
                m.xn[:, ts(g, T)], t.ln_t1[:, 0:T], AF.Identity,
                bias=m.lnb[:, g : g + 1], scale=m.lng[:, g : g + 1]))
            c.di("scalar", ss)
        c.wait("tensor", ss)
        c.wait("vector", ss)

        for ob in range(8):
            buf, si = next_slab()
            for oc in range(4):
                o_global = ob * 4 + oc
                ps = t.PW1[:, (oc % 2) * 512 : (oc % 2) * 512 + T]
                c.wait("tensor", sv)
                c.wait("tensor", ss)
                for g in range(GD):
                    first, last = g == 0, g == GD - 1
                    c.op("tensor", lambda e, ps=ps, buf=buf, g=g, oc=oc, f=first, l=last:
                         e.matmul(ps, buf[:, g * 512 + oc * 128 : g * 512 + oc * 128 + 128],
                                  m.xn[:, ts(g, T)], start=f, stop=l),
                         inc=(stt, 1) if last else None)
                dc = o_global if ob < 4 else o_global - GI
                if ob < 4:
                    c.wait("vector", stt)
                    c.op("vector", lambda v, dc=dc: v.tensor_copy(m.upre[:, 0:3], m.hist[:, dc * 3 : dc * 3 + 3]))
                    c.op("vector", lambda v, ps=ps: v.tensor_copy(m.upre[:, 3 : 3 + T], ps))
                    c.emit("vector", lambda v: v.drain())
                    c.op("vector", lambda v, dc=dc: v.tensor_copy(m.hist[:, dc * 3 : dc * 3 + 3], m.upre[:, T : T + 3]))
                    c.emit("vector", lambda v: v.drain())
                    c.op("vector", lambda v, dc=dc: v.tensor_scalar_mul(
                        m.cacc[:, 0:T], m.upre[:, 0:T], m.convw[:, dc * 4 : dc * 4 + 1]))
                    c.emit("vector", lambda v: v.drain())
                    for j in (1, 2, 3):
                        src_acc = m.cacc if j % 2 == 1 else m.cacc2
                        dst_acc = m.cacc2 if j % 2 == 1 else m.cacc
                        c.op("vector", lambda v, dc=dc, j=j, sa=src_acc, da=dst_acc:
                             v.scalar_tensor_tensor(
                                 da[:, 0:T], m.upre[:, j : j + T],
                                 m.convw[:, dc * 4 + j : dc * 4 + j + 1],
                                 sa[:, 0:T], OP.mult, OP.add))
                        c.emit("vector", lambda v: v.drain())
                    c.di("vector", sv)
                    c.wait("scalar", sv)
                    c.op("scalar", lambda s, dc=dc: s.activation(
                        m.u[:, ts(dc, T)], m.cacc2[:, 0:T], AF.Silu,
                        bias=m.convb[:, dc : dc + 1]))
                    c.di("scalar", ss)
                else:
                    c.wait("scalar", stt)
                    c.op("scalar", lambda s, ps=ps, dc=dc: s.activation(
                        m.sz[:, ts(dc, T)], ps, AF.Silu))
                    c.di("scalar", ss)
            release_slab(si)
        c.wait("tensor", ss)

        psd = t.PW0[0:96, 1024 : 1024 + T]
        for g in range(GI):
            first, last = g == 0, g == GI - 1
            c.op("tensor", lambda e, g=g, f=first, l=last: e.matmul(
                psd, m.xproj[:, g * 96 : (g + 1) * 96], m.u[:, ts(g, T)],
                start=f, stop=l), inc=(stt, 1) if last else None)
        c.wait("scalar", stt)
        c.op("scalar", lambda s: s.activation(m.dtb[0:64, 0:T], t.PW0[0:64, 1024 : 1024 + T], AF.Copy))
        c.di("scalar", ss)
        c.wait("vector", stt)
        c.op("vector", lambda v: v.tensor_copy(m.sbc[0:32, 0:T], t.PW0[64:96, 1024 : 1024 + T]))
        c.di("vector", sv)
        sbcd = c.sem("mb_sbcd")
        c.wait("sync", sv)
        c.dma("sync", d["scr_bc"][tc * 32 : (tc + 1) * 32, :][:, 0:T], m.sbc[0:32, 0:T], sbcd)
        c.wait("gpsimd", sbcd, sbcd.n)
        for ph in range(3):
            cnt = len(range(ph, 32, 3))
            srcap = AP(d["scr_bc"].tensor, (tc * 32 + ph) * 512, [[3 * 512, cnt], [1, T]])
            c.dma("gpsimd", m.sbc2[ph * 32 : ph * 32 + 1, 0 : cnt * T], srcap, c.sem("mb_sbcg"))
        sbcg = c.sem("mb_sbcg")
        for ob in range(4):
            emit_slab_load("out", ob)
        c.wait("tensor", sbcd)
        c.wait("tensor", c.sem("mb_sbcg"))
        c.wait("tensor", sv)
        c.wait("tensor", ss)

        for dc in range(GI):
            pdt = t.PW0[:, 1536 : 1536 + T]
            c.op("tensor", lambda e, dc=dc: e.matmul(
                pdt, m.dtw[:, ts(dc)], m.dtb[0:64, 0:T], start=True, stop=True),
                inc=(stt, 1))
            c.wait("scalar", stt)
            c.op("scalar", lambda s, dc=dc: s.activation(
                m.delta[:, 0:T], pdt, AF.Exp, bias=m.dtpb[:, dc : dc + 1]))
            c.emit("scalar", lambda s: s.drain())
            c.op("scalar", lambda s: s.activation(m.delta[:, 0:T], m.delta[:, 0:T], AF.Ln, bias=t.one_ap()))
            c.di("scalar", ss)
            c.wait("vector", ss)
            c.op("vector", lambda v, dc=dc: v.tensor_mul(m.du[:, 0:T], m.delta[:, 0:T], m.u[:, ts(dc, T)]))
            c.di("vector", sv)
            c.wait("tensor", sv)
            for n in range(NST):
                idx = dc * NST + n
                slot = idx % 2
                pb = t.PW1[:, 1024 + slot * 512 : 1024 + slot * 512 + T]
                pcc = t.PW0[:, slot * 512 : slot * 512 + T]
                c.wait("tensor", s_scan, s_scan.n - 1)
                rb, rc = n, NST + n
                pb_b, pc_b = (rb % 3) * 32, (rc % 3) * 32
                bap = m.sbc2[pb_b : pb_b + 1, (rb // 3) * T : (rb // 3) * T + T]
                cap = m.sbc2[pc_b : pc_b + 1, (rc // 3) * T : (rc // 3) * T + T]
                c.op("tensor", lambda e, pb=pb, bap=bap, pb_b=pb_b: e.matmul(
                    pb, t.ones_b[pb_b : pb_b + 1, 0:128], bap, start=True, stop=True))
                c.op("tensor", lambda e, pcc=pcc, cap=cap, pc_b=pc_b: e.matmul(
                    pcc, t.ones_b[pc_b : pc_b + 1, 0:128], cap, start=True, stop=True), inc=(stt, 1))
                c.wait("scalar", s_scan, s_scan.n - 1)
                c.op("scalar", lambda s, dc=dc, n=n, slot=slot: s.activation(
                    m.dA[:, slot * T : slot * T + T], m.delta[:, 0:T], AF.Exp,
                    scale=m.negA[:, dc * NST + n : dc * NST + n + 1]))
                c.di("scalar", ss)
                c.wait("vector", stt)
                c.wait("vector", ss)
                c.op("vector", lambda v, pb=pb: v.tensor_mul(m.dBu[:, 0:T], m.du[:, 0:T], pb))
                c.emit("vector", lambda v: v.drain())
                c.op("vector", lambda v, dc=dc, n=n, slot=slot: v.tensor_tensor_scan(
                    m.h[:, 0:T], m.dA[:, slot * T : slot * T + T], m.dBu[:, 0:T],
                    m.states[:, dc * NST + n : dc * NST + n + 1], OP.mult, OP.add),
                    inc=(s_scan, 1))
                c.emit("vector", lambda v: v.drain())
                c.op("vector", lambda v, dc=dc, n=n: v.tensor_copy(
                    m.states[:, dc * NST + n : dc * NST + n + 1], m.h[:, T - 1 : T]))
                c.op("vector", lambda v, pcc=pcc: v.tensor_mul(m.hC[:, 0:T], m.h[:, 0:T], pcc))
                c.di("vector", sv)
                c.wait("tensor", sv)
                psy = t.PW1[:, 0:T]
                c.op("tensor", lambda e, psy=psy, n=n: e.matmul(
                    psy, t.ident_b[:], m.hC[:, 0:T], start=(n == 0), stop=(n == NST - 1)),
                    inc=(stt, 1) if n == NST - 1 else None)
            c.wait("vector", stt)
            c.op("vector", lambda v, dc=dc: v.scalar_tensor_tensor(
                m.y1[:, 0:T], m.u[:, ts(dc, T)], m.D[:, dc : dc + 1], t.PW1[:, 0:T],
                OP.mult, OP.add))
            c.emit("vector", lambda v: v.drain())
            c.op("vector", lambda v, dc=dc: v.tensor_mul(
                m.u[:, ts(dc, T)], m.y1[:, 0:T], m.sz[:, ts(dc, T)]))
            c.di("vector", sv)
            c.wait("tensor", sv)

        for ob in range(4):
            buf, si = next_slab()
            for oc in range(2):
                mg = ob * 2 + oc
                ps = t.PW0[:, (oc % 2) * 512 : (oc % 2) * 512 + T]
                c.wait("tensor", sv)
                for g in range(GI):
                    first, last = g == 0, g == GI - 1
                    c.op("tensor", lambda e, ps=ps, buf=buf, g=g, oc=oc, f=first, l=last:
                         e.matmul(ps, buf[:, g * 256 + oc * 128 : g * 256 + oc * 128 + 128],
                                  m.u[:, ts(g, T)], start=f, stop=l),
                         inc=(stt, 1) if last else None)
                c.wait("vector", stt)
                c.op("vector", lambda v, ps=ps, mg=mg, sl=sl: v.scalar_tensor_tensor(
                    t.x_col(mg, sl), t.x_col(mg, sl), 2.0, ps, OP.mult, OP.add))
                c.di("vector", sv)
            release_slab(si)
        c.wait("tensor", sv)



def ph_ffn0(ctx, d, t, f):
    c = ctx
    L = c.L
    TH = min(1024, L)
    sv = c.sem("f0_v")
    ss = c.sem("f0_s")
    stt = c.sem("f0_t")
    ring = Ring(c, "f0", f.wslab)

    for th in range(L // TH):
        for m in range(G0):
            for wname in ("wgT", "wuT"):
                srcap = AP(d[wname].tensor, m * 128,
                           [[HID0, 128], [128 * HID0, GD], [1, 128]])
                ring.load(srcap, GD * 128)
        for mc in range(GD):
            srcap = AP(d["wdT"].tensor, mc * 128,
                       [[DM, 128], [128 * DM, G0], [1, 128]])
            ring.load(srcap, G0 * 128)

    for th in range(L // TH):
        for m in range(G0):
            wg = ring.take()
            wu = ring.take()
            for t2 in range(TH // 512):
                sl = slice(th * TH + t2 * 512, th * TH + t2 * 512 + 512)
                hcol = m * TH + t2 * 512
                psg = t.PW1[:, 0:512]
                psu = t.PW1[:, 512:1024]
                c.wait("tensor", ss)
                c.wait("tensor", sv)
                for g in range(GD):
                    fi, la = g == 0, g == GD - 1
                    c.op("tensor", lambda e, psg=psg, wg=wg, g=g, sl=sl, fi=fi, la=la:
                         e.matmul(psg, wg[:, g * 128 : g * 128 + 128], t.xb_col(g, sl),
                                  start=fi, stop=la))
                for g in range(GD):
                    fi, la = g == 0, g == GD - 1
                    c.op("tensor", lambda e, psu=psu, wu=wu, g=g, sl=sl, fi=fi, la=la:
                         e.matmul(psu, wu[:, g * 128 : g * 128 + 128], t.xb_col(g, sl),
                                  start=fi, stop=la),
                         inc=(stt, 1) if la else None)
                c.wait("scalar", stt)
                c.op("scalar", lambda s, psg=psg, t2=t2: s.activation(
                    f.gl[:, t2 * 512 : t2 * 512 + 512], psg, AF.Silu, scale=1.0 / 127.0))
                c.di("scalar", ss)
                c.wait("vector", ss)
                c.op("vector", lambda v, psu=psu, m=m, t2=t2, hcol=hcol: v.tensor_mul(
                    f.h[:, hcol : hcol + 512], f.gl[:, t2 * 512 : t2 * 512 + 512], psu))
                c.di("vector", sv)
            ring.release()
            ring.release()
        c.wait("tensor", sv)
        for mc in range(GD):
            wd = ring.take()
            for t2 in range(TH // 512):
                sl = slice(th * TH + t2 * 512, th * TH + t2 * 512 + 512)
                ps = t.PW0[:, (t2 % 2) * 512 : (t2 % 2) * 512 + 512]
                c.wait("tensor", sv)
                for hm in range(G0):
                    fi, la = hm == 0, hm == G0 - 1
                    c.op("tensor", lambda e, ps=ps, wd=wd, hm=hm, t2=t2, fi=fi, la=la:
                         e.matmul(ps, wd[:, hm * 128 : hm * 128 + 128],
                                  f.h[:, hm * TH + t2 * 512 : hm * TH + t2 * 512 + 512],
                                  start=fi, stop=la),
                         inc=(stt, 1) if la else None)
                c.wait("vector", stt)
                c.op("vector", lambda v, ps=ps, mc=mc, sl=sl: v.scalar_tensor_tensor(
                    t.x_col(mc, sl), ps, 1.0 / 16129.0, t.x_col(mc, sl), OP.mult, OP.add))
                c.di("vector", sv)
            ring.release()
        c.wait("tensor", sv)




def ph_attn1(ctx, d, t, a):
    c = ctx
    import os as _os0
    delta = int(_os0.environ.get("A1_DELTA", "99"))
    if getattr(c, "a1_stop", 99) <= -1:
        if delta >= 1:
            c.di("vector", c.sem("a1_v"))
        if delta >= 2:
            c.wait("gpsimd", c.sem("a1_v"))
        if delta >= 3:
            c.wait("tensor", c.sem("a1_v"))
        return
    L = c.L
    NKC = L // 128
    NQB = L // 512
    nkc = NKC
    sv = c.sem("a1_v")
    ss = c.sem("a1_s")
    stt = c.sem("a1_t")
    sp = c.sem("a1_p")
    spg = c.sem("a1_pg")
    ring = Ring(c, "a1", a.wslab)

    import os
    mask = int(os.environ.get("A1_PRE_MASK", "127"))
    if mask & 1:
        for nm, dst in (("bq", a.bq), ("bk", a.bk), ("bo", a.bo)):
            c.dma("sync", dst[:], d[nm][:], sp)
        c.dma("sync", a.alibi[:], d["alibi"][:], sp)
    if mask & 2:
        c.dma("gpsimd", a.rez[:], AP(d["rezero"].tensor, 0, [[0, 128], [1, 1]]), spg)
    if mask & 8:
        c.dma("gpsimd", a.Aq[:], AP(d["AqT"].tensor, 0, [[8, 128], [128 * 8, GD], [1, 8]]), spg)
    import os as _os
    if int(_os.environ.get("A1_MS", "1")):
        c.op("vector", lambda v: v.memset(a.Akv[:], 0.0))
        c.op("vector", lambda v: v.memset(a.Ao[:], 0.0))
    c.di("vector", sv)
    c.wait("gpsimd", sv)
    if mask & 16:
        c.dma("gpsimd", AP(a.Akv, 0, [[GD * 72, 128], [72, GD], [1, 8]]),
              AP(d["AkT"].tensor, 0, [[8, 128], [128 * 8, GD], [1, 8]]), spg)
        c.dma("gpsimd", AP(a.Akv, 64, [[GD * 72, 128], [72, GD], [1, 8]]),
              AP(d["AvT"].tensor, 0, [[8, 128], [128 * 8, GD], [1, 8]]), spg)
        c.dma("gpsimd", AP(a.Ao, 64, [[NH1 * 72, 64], [72, NH1], [1, 8]]),
              AP(d["AoT"].tensor, 0, [[8, 64], [64 * 8, NH1], [1, 8]]), spg)
    if mask & 32:
        for nm, r0, c0 in (("BkT", 0, 0), ("BvT", 64, 0), ("BqT", 0, DM), ("BoT", 64, DM)):
            c.dma("gpsimd", a.Bpack[r0 : r0 + 8, c0 : c0 + DM], d[nm][:], spg)
        c.dma("gpsimd", a.Bpack[72:73, 0:DM], d["bvrow"][:], spg)
    c.wait("vector", sp)
    if mask & 64:
        c.op("vector", lambda v: v.tensor_scalar_mul(a.bk8[:], a.bk[:], 0.125))
    if int(_os.environ.get("A1_MS", "1")):
        c.op("vector", lambda v: v.memset(a.negm[:], 0.0))
    c.op("vector", lambda v: v.memset(a.lrap[64:73, :], 1.0))
    c.di("vector", sv)

    def wslab_src(wname, oc):
        return AP(d[wname].tensor, oc * 128, [[DM, 128], [128 * DM, GD], [1, 128]])

    swv = c.sem("a1_wv")

    if getattr(c, "a1_stop", 99) <= 0:
        c.wait("tensor", spg)
        c.wait("tensor", sp)
        c.wait("tensor", sv)
        return
    c.wait("tensor", spg)
    c.wait("tensor", sv)
    for i2 in range(L // 512):
        sl = slice(i2 * 512, (i2 + 1) * 512)
        ps = t.PW0[0:72, 1536:2048]
        for g in range(GD):
            fi, la = g == 0, g == GD - 1
            c.op("tensor", lambda e, ps=ps, g=g, sl=sl, fi=fi, la=la:
                 e.matmul(ps, a.Akv[:, g * 72 : g * 72 + 72], t.xb_col(g, sl), start=fi, stop=la),
                 inc=(stt, 1) if la else None)
        c.wait("scalar", stt)
        c.op("scalar", lambda s, ps=ps, sl=sl: s.activation(
            a.lrap[0:72, sl], ps, AF.Identity, scale=0.125))
        c.di("scalar", ss)
        c.wait("tensor", ss)

    if getattr(c, "a1_stop", 99) <= 1:
        return
    skd = [c.sem("spill_d0"), c.sem("spill_d1")]
    skd_t = []
    for oc in range(GD):
        ring.load(wslab_src("WkT", oc), GD * 128)
    for oc in range(GD):
        wk = ring.take()
        for i2 in range(L // 512):
            sl = slice(i2 * 512, (i2 + 1) * 512)
            ps = t.PW1[:, (i2 % 2) * 512 : (i2 % 2) * 512 + 512]
            ik = oc * (L // 512) + i2
            kst = a.kst[ik % 2]
            c.wait("tensor", ss)
            for g in range(GD):
                fi = g == 0
                c.op("tensor", lambda e, ps=ps, wk=wk, g=g, sl=sl, fi=fi:
                     e.matmul(ps, wk[:, g * 128 : g * 128 + 128], t.xb_col(g, sl),
                              start=fi, stop=False))
            c.op("tensor", lambda e, ps=ps, oc=oc, sl=sl: e.matmul(
                ps, a.Bpack[0:8, oc * 128 : oc * 128 + 128], a.lrap[0:8, sl],
                start=False, stop=True), inc=(stt, 1))
            c.wait("scalar", stt)
            if ik >= 2:
                c.wait("scalar", skd[ik % 2], skd_t[ik - 2])
            c.op("scalar", lambda s, ps=ps, oc=oc, kst=kst: s.activation(
                kst[:], ps, AF.Identity, bias=a.bk8[:, oc : oc + 1], scale=0.125))
            c.di("scalar", ss)
            c.wait("sync", ss)
            c.dma("sync", AP(d["kspill"].tensor, oc * 128 * L + sl.start,
                             [[L, 128], [1, 512]]), kst[:], skd[ik % 2])
            skd_t.append(skd[ik % 2].n)
            c.wait("scalar", ss, ss.n)
            c.op("scalar", lambda s, kst=kst: s.activation(a.sq[:, 0:512], kst[:], AF.Square))
            c.di("scalar", ss)
            c.wait("tensor", ss)
            psn = t.PW0[:, 1536:2048]
            for par in range(2):
                p0 = par * 64
                c.op("tensor", lambda e, psn=psn, p0=p0: e.matmul(
                    psn, t.ones_b[p0 : p0 + 64, 0:128], a.sq[p0 : p0 + 64, 0:512],
                    start=True, stop=True), inc=(stt, 1))
                c.wait("vector", stt)
                h = 2 * oc + par
                c.op("vector", lambda v, psn=psn, i2=i2: v.reduce_max(
                    a.mx[:, 0:1], psn, AX))
                c.emit("vector", lambda v: v.drain())
                if i2 == 0:
                    c.op("vector", lambda v, h=h: v.tensor_copy(a.kn2[:, h : h + 1], a.mx[:, 0:1]))
                else:
                    c.op("vector", lambda v, h=h: v.tensor_max(
                        a.kn2[:, h : h + 1], a.kn2[:, h : h + 1], a.mx[:, 0:1]))
                c.di("vector", sv)
                c.wait("tensor", sv)
        ring.release()
    c.wait("scalar", sv)
    c.op("scalar", lambda s: s.activation(a.negmk[:], a.kn2[:], AF.Sqrt))
    c.di("scalar", ss)
    c.wait("vector", ss)
    c.op("vector", lambda v: v.tensor_scalar_mul(a.negmk[:], a.negmk[:], -1.0))
    c.di("vector", sv)

    if getattr(c, "a1_stop", 99) <= 2:
        return
    c.dma("gpsimd", a.wv_ao[:, 0 : GD * 1024],
          AP(d["WvT"].tensor, 0, [[DM, 128], [128 * DM, GD], [1, DM]]), swv)
    c.wait("tensor", swv)
    c.wait("tensor", sv)
    c.op("vector", lambda v: v.memset(a.vt[:], 0.0))
    c.emit("vector", lambda v: v.drain())
    for kt in range(NKC):
        for h in range(NH1):
            c.op("vector", lambda v, kt=kt, h=h: v.memset(
                a.vt[:, kt * 1040 + h * 65 + 64 : kt * 1040 + h * 65 + 65], 1.0))
    c.di("vector", sv)
    c.wait("tensor", sv)
    for kt in range(NKC):
        for ob2 in range(2):
            ps = t.PW1[:, (ob2 % 2) * 512 : (ob2 % 2) * 512 + 512]
            c.wait("tensor", sv)
            for g in range(GD):
                c.op("tensor", lambda e, ps=ps, kt=kt, g=g, ob2=ob2: e.matmul(
                    ps, t.xb_col(g, slice(kt * 128, kt * 128 + 128)),
                    a.wv_ao[:, g * 1024 + ob2 * 512 : g * 1024 + ob2 * 512 + 512],
                    start=(g == 0), stop=False))
            c.op("tensor", lambda e, ps=ps, kt=kt, ob2=ob2: e.matmul(
                ps, a.lrap[64:73, kt * 128 : kt * 128 + 128],
                a.Bpack[64:73, ob2 * 512 : ob2 * 512 + 512],
                start=False, stop=True), inc=(stt, 1))
            c.wait("vector", stt)
            for h2 in range(8):
                h = ob2 * 8 + h2
                c.op("vector", lambda v, ps=ps, kt=kt, h=h, h2=h2: v.tensor_copy(
                    a.vt[:, kt * 1040 + h * 65 : kt * 1040 + h * 65 + 64],
                    ps[:, h2 * 64 : h2 * 64 + 64]))
            c.di("vector", sv)
    c.wait("tensor", sv)

    if getattr(c, "a1_stop", 99) <= 3:
        return
    for qb in range(NQB):
        qsl = slice(qb * 512, (qb + 1) * 512)
        ps = t.PW0[0:8, 1536:2048]
        c.wait("tensor", ss)
        c.wait("tensor", sv)
        for g in range(GD):
            fi, la = g == 0, g == GD - 1
            c.op("tensor", lambda e, ps=ps, g=g, fi=fi, la=la, qsl=qsl: e.matmul(
                ps, a.Aq[:, g * 8 : g * 8 + 8], t.xb_col(g, qsl), start=fi, stop=la),
                inc=(stt, 1) if la else None)
        c.wait("scalar", stt)
        c.op("scalar", lambda s, ps=ps: s.activation(a.lraq[0:8, 0:512], ps, AF.Identity, scale=0.125))
        c.di("scalar", ss)
        c.wait("tensor", ss)
        for oc in range(GD):
            ring.load(wslab_src("WqT", oc), GD * 128)
        for oc in range(GD):
            wq = ring.take()
            ps = t.PW1[:, (oc % 2) * 512 : (oc % 2) * 512 + 512]
            c.wait("tensor", ss)
            for g in range(GD):
                fi = g == 0
                c.op("tensor", lambda e, ps=ps, wq=wq, g=g, fi=fi, qsl=qsl: e.matmul(
                    ps, wq[:, g * 128 : g * 128 + 128], t.xb_col(g, qsl), start=fi, stop=False))
            c.op("tensor", lambda e, ps=ps, oc=oc: e.matmul(
                ps, a.Bpack[0:8, DM + oc * 128 : DM + oc * 128 + 128], a.lraq[0:8, 0:512],
                start=False, stop=True), inc=(stt, 1))
            ring.release()
            c.wait("scalar", stt)
            c.op("scalar", lambda s, ps=ps, oc=oc: s.activation(
                a.qf[:, oc * 512 : oc * 512 + 512], ps, AF.Identity,
                bias=a.bq[:, oc : oc + 1]))
            c.di("scalar", ss)
        c.wait("tensor", ss)
        if getattr(c, "a1_stop", 99) <= 4:
            continue
        c.wait("sync", skd[0])
        c.wait("sync", skd[1])
        skh = [c.sem("pf_d0"), c.sem("pf_d1")]
        if not hasattr(c, "pf_targets"):
            c.pf_targets = {}
        kh_targets = []
        khr_base = c.sem("a1_khr").n
        for h in range(NH1):
            p0h = (h % 2) * 64
            if khr_base + h - 1 > 0:
                c.wait("sync", c.sem("a1_khr"), khr_base + h - 1)
            prev = c.pf_targets.get(h % 2)
            if prev is not None:
                c.wait("sync", skh[h % 2], prev)
            c.dma("sync", a.khb[h % 2][p0h : p0h + 64, :],
                  AP(d["kspill"].tensor, h * 64 * L, [[L, 64], [1, L]]), skh[h % 2])
            kh_targets.append(skh[h % 2].n)
            c.pf_targets[h % 2] = skh[h % 2].n
        for h in range(NH1):
            g = h // 2
            p0 = (h % 2) * 64
            p1 = 64 - p0
            c.wait("tensor", skh[h % 2], kh_targets[h])
            c.wait("scalar", sv)
            c.wait("scalar", stt)
            c.op("scalar", lambda s, g=g, p0=p0: s.activation(
                a.sq[p0 : p0 + 64, 0:512], a.qf[p0 : p0 + 64, g * 512 : g * 512 + 512], AF.Square))
            c.di("scalar", ss)
            c.wait("tensor", ss)
            psn = t.PW0[:, 1536:2048]
            c.op("tensor", lambda e, psn=psn, p0=p0: e.matmul(
                psn, t.ones_b[p0 : p0 + 64, 0:128], a.sq[p0 : p0 + 64, 0:512],
                start=True, stop=True), inc=(stt, 1))
            c.wait("scalar", stt)
            c.op("scalar", lambda s, psn=psn: s.activation(a.sqr[:, 0:512], psn, AF.Sqrt))
            c.di("scalar", ss)
            c.wait("vector", ss)
            c.wait("vector", stt)
            slope2047 = -(0.5 ** h) * (L - 1.0)
            c.op("vector", lambda v, h=h, p1=p1, s2=slope2047: v.tensor_scalar(
                a.negm[p1 : p1 + 1, 0:512], a.sqr[p1 : p1 + 1, 0:512],
                a.negmk[p1 : p1 + 1, h : h + 1], s2, OP.mult, OP.add))
            c.di("vector", sv)
            c.wait("tensor", sv)
            for kc in range(NKC):
                slot = kc % 2
                pss = t.PW0[:, slot * 512 : slot * 512 + 512]
                c.wait("tensor", ss, ss.n - 1)
                c.op("tensor", lambda e, pss=pss, h=h, p0=p0, kc=kc: e.matmul(
                    pss, a.khb[h % 2][p0 : p0 + 64, kc * 128 : kc * 128 + 128],
                    a.qf[p0 : p0 + 64, (h // 2) * 512 : (h // 2) * 512 + 512],
                    start=True, stop=False))
                c.op("tensor", lambda e, pss=pss, p1=p1: e.matmul(
                    pss, t.ones_b[p1 : p1 + 1, 0:128], a.negm[p1 : p1 + 1, 0:512],
                    start=False, stop=True), inc=(stt, 1))
                c.wait("scalar", stt)
                c.op("scalar", lambda s, pss=pss, h=h, kc=kc, slot=slot: s.activation(
                    a.pt[:, slot * 512 : slot * 512 + 512], pss, AF.Exp,
                    bias=a.alibi[:, h * nkc + kc : h * nkc + kc + 1]), inc=(ss, 1))
                c.wait("tensor", ss)
                pso = t.PW1[0:65, 1024:1536]
                c.op("tensor", lambda e, pso=pso, kc=kc, h=h, slot=slot: e.matmul(
                    pso, a.vt[:, kc * 1040 + h * 65 : kc * 1040 + h * 65 + 65],
                    a.pt[:, slot * 512 : slot * 512 + 512],
                    start=(kc == 0), stop=(kc == NKC - 1)),
                    inc=(stt, 1) if kc == NKC - 1 else None)
            c.di("tensor", c.sem("a1_khr"))
            c.wait("vector", stt)
            c.op("vector", lambda v: v.tensor_scalar_max(a.lrow,
                                                         t.PW1[64:65, 1024:1536], 1e-30))
            c.emit("vector", lambda v: v.drain())
            c.op("vector", lambda v: v.reciprocal(a.linv, a.lrow))
            c.di("vector", sv)
            c.wait("tensor", sv)
            psl = t.PW1[:, 1536:2048]
            c.op("tensor", lambda e, psl=psl: e.matmul(
                psl, t.ones_f[64:65, 0:128], a.linv, start=True, stop=True),
                inc=(stt, 1))
            c.wait("vector", stt)
            c.op("vector", lambda v, h=h: v.tensor_copy(
                a.ao2[0:64, h * 512 : h * 512 + 512], t.PW1[0:64, 1024:1536]))
            c.emit("vector", lambda v: v.drain())
            c.op("vector", lambda v, h=h: v.tensor_mul(
                a.ao2[0:64, h * 512 : h * 512 + 512], a.ao2[0:64, h * 512 : h * 512 + 512],
                t.PW1[0:64, 1536:2048]))
            c.di("vector", sv)
            c.wait("tensor", sv)
        if getattr(c, "a1_stop", 99) <= 5:
            continue
        ps = t.PW0[0:72, 1536:2048]
        for h in range(NH1):
            fi, la = h == 0, h == NH1 - 1
            c.op("tensor", lambda e, ps=ps, h=h, fi=fi, la=la: e.matmul(
                ps, a.Ao[0:64, h * 72 : h * 72 + 72], a.ao2[0:64, h * 512 : h * 512 + 512],
                start=fi, stop=la), inc=(stt, 1) if la else None)
        c.wait("scalar", stt)
        c.op("scalar", lambda s, ps=ps: s.activation(a.lraq[64:72, 0:512],
                                                     t.PW0[64:72, 1536:2048], AF.Identity, scale=0.125))
        c.di("scalar", ss)
        c.wait("tensor", ss)
        for oc in range(GD):
            srcap = AP(d["WoT"].tensor, oc * 128, [[DM, 64], [64 * DM, NH1], [1, 128]])
            ring.load(srcap, NH1 * 128, rows=64)
        for oc in range(GD):
            wo = ring.take()
            ps = t.PW0[:, (oc % 2) * 512 : (oc % 2) * 512 + 512]
            c.wait("tensor", sv)
            c.wait("tensor", ss)
            for h in range(NH1):
                fi = h == 0
                c.op("tensor", lambda e, ps=ps, wo=wo, h=h, fi=fi: e.matmul(
                    ps, wo[0:64, h * 128 : h * 128 + 128],
                    a.ao2[0:64, h * 512 : h * 512 + 512], start=fi, stop=False))
            c.op("tensor", lambda e, ps=ps, oc=oc: e.matmul(
                ps, a.Bpack[64:72, DM + oc * 128 : DM + oc * 128 + 128], a.lraq[64:72, 0:512],
                start=False, stop=True), inc=(stt, 1))
            ring.release()
            c.wait("vector", stt)
            c.op("vector", lambda v, ps=ps, oc=oc: v.tensor_scalar(
                a.sqr[:, 0:512], ps, a.bo[:, oc : oc + 1], a.rez[:, 0:1], OP.add, OP.mult))
            c.emit("vector", lambda v: v.drain())
            c.op("vector", lambda v, oc=oc, qsl=qsl: v.scalar_tensor_tensor(
                t.x_col(oc, qsl), t.x_col(oc, qsl), 2.0, a.sqr[:, 0:512], OP.mult, OP.add))
            c.di("vector", sv)
        c.wait("tensor", sv)




def ph_ffn1_mix(ctx, d, t, b, l):
    c = ctx
    L = c.L
    NKC = L // 128
    NW = L // WIN
    tag = f"m{l}"
    sv = c.sem(tag + "_v")
    ss = c.sem(tag + "_s")
    stt = c.sem(tag + "_t")
    sng = c.sem(tag + "_ng")
    ring = Ring(c, tag, b.wslab)

    c.wait("vector", c.sem("b_pre"))
    s_rt = c.sem(tag + "_rt")
    for nm, dst in (("rotq1", b.rotq1), ("rotq2", b.rotq2),
                    ("rotk1", b.rotk1), ("rotk2", b.rotk2)):
        c.dma("gpsimd", dst[:], d[nm][:], s_rt)
    s_mk = c.sem(tag + "_mk")
    for nm, dst in (("mask_own_kq", b.mokq), ("mask_prev_kq", b.mpkq),
                    ("mask_own_qk", b.moqk), ("mask_prev_qk", b.mpqk)):
        c.dma("sync", dst[:], d[nm][:], s_mk)
    c.wait("vector", s_rt)
    c.wait("vector", s_mk)
    for g in range(GD):
        c.op("vector", lambda v, g=g: v.tensor_copy(t.XB[:, ts(g, L)], t.X[:, ts(g, L)]))
    c.emit("vector", lambda v: v.drain())
    for g in range(GD // 2, GD):
        gq = g - GD // 2
        c.op("vector", lambda v, gq=gq: v.memset(b.xsh0[:, gq * 512 : gq * 512 + 1], 0.0))
        c.op("vector", lambda v, g=g, gq=gq: v.tensor_copy(
            b.xsh0[:, gq * 512 + 1 : gq * 512 + 512], t.XB[:, g * L : g * L + 511]))
    c.di("vector", sv)

    def qkv_src(oc):
        return AP(d["wqkv"].tensor, l * DM * 1536 + oc * 128,
                  [[1536, 128], [128 * 1536, GD], [1, 128]])

    for oc in range(8):
        ring.load(qkv_src(oc), GD * 128)
    swv = c.sem(tag + "_wv")
    c.dma("gpsimd", b.wv[:, 0 : GD * 512],
          AP(d["wqkv"].tensor, l * DM * 1536 + 1024, [[1536, 128], [128 * 1536, GD], [1, 512]]), swv)
    swo = c.sem(tag + "_wo")
    c.dma("gpsimd", b.wo_all[0:64, 0:8192],
          AP(d["wo"].tensor, l * INNER2 * DM,
             [[DM, 64], [128, GD], [64 * DM, NH2], [1, 128]]), swo)

    def shifted_mm(e, ps, g, sl, fi, la, stationary_ap):
        if g < GD // 2:
            return e.matmul(ps[:, 0 : sl.stop - sl.start], stationary_ap, t.xb_col(g, sl), start=fi, stop=la)
        elif sl.start == 0:
            gq = g - GD // 2
            return e.matmul(ps[:, 0 : sl.stop], stationary_ap,
                            b.xsh0[:, gq * 512 : gq * 512 + sl.stop], start=fi, stop=la)
        else:
            return e.matmul(ps[:, 0 : sl.stop - sl.start], stationary_ap,
                            t.XB[:, g * L + sl.start - 1 : g * L + sl.stop - 1], start=fi, stop=la)

    sqk = [c.sem("spill_d0"), c.sem("spill_d1")]
    sqk_t = []
    qk_i = [0]
    for part, dst in (("q", None), ("k", None)):
        base_oc = 0 if part == "q" else 4
        for oc2 in range(4):
            oc = base_oc + oc2
            wslab = ring.take()
            for i2 in range(L // 512):
                sl = slice(i2 * 512, (i2 + 1) * 512)
                ps = t.PW1[:, (i2 % 2) * 512 : (i2 % 2) * 512 + 512]
                c.wait("tensor", ss)
                c.wait("tensor", sv)
                for g in range(GD):
                    fi, la = g == 0, g == GD - 1
                    stat = wslab[:, g * 128 : g * 128 + 128]
                    c.op("tensor", lambda e, ps=ps, g=g, sl=sl, fi=fi, la=la, stat=stat:
                         shifted_mm(e, ps, g, sl, fi, la, stat),
                         inc=(stt, 1) if la else None)
                if getattr(c, "dbg_raw", False):
                    c.wait("scalar", stt)
                    c.op("scalar", lambda s, ps=ps, dst=dst, oc2=oc2, sl=sl: s.activation(
                        dst[:, oc2 * L + sl.start : oc2 * L + sl.stop], ps, AF.Identity))
                    c.di("scalar", ss)
                    c.wait("tensor", ss)
                    continue
                c.wait("scalar", stt)
                c.op("scalar", lambda s, ps=ps: s.activation(b.sq[:, 0:512], ps, AF.Square))
                c.emit("scalar", lambda s: s.drain())
                c.op("scalar", lambda s, ps=ps: s.activation(b.raw[:, 0:512], ps, AF.Identity))
                c.di("scalar", ss)
                c.wait("tensor", ss)
                psn = t.PW0[:, 1024:1536]
                c.op("tensor", lambda e, psn=psn: e.matmul(
                    psn[0:64, :], t.ones_b[0:64, 0:64], b.sq[0:64, 0:512],
                    start=True, stop=True))
                c.op("tensor", lambda e, psn=psn: e.matmul(
                    psn[64:128, :], t.ones_b[64:128, 64:128], b.sq[64:128, 0:512],
                    start=True, stop=True), inc=(stt, 1))
                c.wait("vector", stt)
                c.op("vector", lambda v, psn=psn: v.tensor_scalar_max(b.rs[:, 0:512], psn, 1e-24))
                c.di("vector", sv)
                c.wait("scalar", sv)
                c.op("scalar", lambda s: s.activation(b.rs[:, 0:512], b.rs[:, 0:512], AF.Ln))
                c.emit("scalar", lambda s: s.drain())
                c.op("scalar", lambda s: s.activation(b.rs[:, 0:512], b.rs[:, 0:512], AF.Exp, scale=-0.5))
                c.di("scalar", ss)
                c.wait("vector", ss)
                qsc = b.qs8 if part == "q" else b.ks
                c.op("vector", lambda v, qsc=qsc: v.tensor_scalar_mul(
                    b.rs[:, 0:512], b.rs[:, 0:512], qsc[:, l : l + 1]))
                c.emit("vector", lambda v: v.drain())
                c.op("vector", lambda v: v.tensor_mul(b.t0[:, 0:512], b.raw[:, 0:512], b.rs[:, 0:512]))
                c.emit("vector", lambda v: v.drain())
                if getattr(c, "dbg_norot", False):
                    c.op("vector", lambda v, dst=dst, oc2=oc2, sl=sl: v.tensor_copy(
                        dst[:, oc2 * L + sl.start : oc2 * L + sl.stop], b.t0[:, 0:512]))
                    c.di("vector", sv)
                    c.wait("tensor", sv)
                    continue
                for (dp, sp) in ((0, 32), (32, 0), (64, 96), (96, 64)):
                    c.op("vector", lambda v, dp=dp, sp=sp: v.tensor_copy(
                        b.t2s[dp : dp + 32, 0:512], b.t0[sp : sp + 32, 0:512]))
                t1t = b.rotq1 if part == "q" else b.rotk1
                t2t = b.rotq2 if part == "q" else b.rotk2
                c.op("vector", lambda v, oc2=oc2, sl=sl, t1t=t1t: v.tensor_mul(
                    b.t1o[:, 0:512], b.t0[:, 0:512], t1t[:, sl]))
                c.emit("vector", lambda v: v.drain())
                c.op("vector", lambda v, oc2=oc2, sl=sl, t2t=t2t: v.tensor_mul(
                    b.t2o[:, 0:512], b.t2s[:, 0:512], t2t[:, sl]))
                c.emit("vector", lambda v: v.drain())
                iq = qk_i[0]
                qk_i[0] += 1
                qkst = b.qkst[iq % 2]
                if iq >= 2:
                    c.wait("vector", sqk[iq % 2], sqk_t[iq - 2])
                c.op("vector", lambda v, qkst=qkst: v.tensor_add(
                    qkst[:], b.t1o[:, 0:512], b.t2o[:, 0:512]))
                c.di("vector", sv)
                c.wait("sync", sv)
                rowbase = (512 if part == "q" else 0) + oc2 * 128
                c.dma("sync", AP(d["kspill"].tensor, rowbase * L + sl.start,
                                 [[L, 128], [1, 512]]), qkst[:], sqk[iq % 2])
                sqk_t.append(sqk[iq % 2].n)
                c.wait("tensor", sv)
            ring.release()
    c.wait("tensor", sv)
    c.wait("tensor", ss)

    if getattr(c, "dbg_qk", False):
        sdbg = c.sem("dbg_qk")
        c.wait("gpsimd", sv)
        c.wait("gpsimd", ss)
        for g2 in range(4):
            c.dma("gpsimd", d["outT"][ts(g2), :], b.qh[:, g2 * L : (g2 + 1) * L], sdbg)
            c.dma("gpsimd", d["outT"][ts(g2 + 4), :], b.kh[:, g2 * L : (g2 + 1) * L], sdbg)
        c.wait("sync", sdbg)
        c.dbg_stop = True
        return

    c.wait("tensor", swv)
    c.op("vector", lambda v: v.memset(b.vt2[:], 0.0))
    c.emit("vector", lambda v: v.drain())
    for kt in range(NKC):
        for h in range(NH2):
            c.op("vector", lambda v, kt=kt, h=h: v.memset(
                b.vt2[:, kt * 520 + h * 65 + 64 : kt * 520 + h * 65 + 65], 1.0))
    c.di("vector", sv)
    c.wait("tensor", sv)
    for kt in range(NKC):
        ps = t.PW1[:, (kt % 2) * 512 : (kt % 2) * 512 + 512]
        c.wait("tensor", sv)
        for g in range(GD):
            fi, la = g == 0, g == GD - 1
            if g < GD // 2:
                stat = t.XB[:, g * L + kt * 128 : g * L + kt * 128 + 128]
            elif kt == 0:
                stat = b.xsh0[:, (g - GD // 2) * 512 : (g - GD // 2) * 512 + 128]
            else:
                stat = t.XB[:, g * L + kt * 128 - 1 : g * L + kt * 128 + 127]
            c.op("tensor", lambda e, ps=ps, stat=stat, g=g, fi=fi, la=la: e.matmul(
                ps, stat, b.wv[:, g * 512 : g * 512 + 512], start=fi, stop=la),
                inc=(stt, 1) if la else None)
        c.wait("vector", stt)
        for h in range(NH2):
            c.op("vector", lambda v, ps=ps, kt=kt, h=h: v.tensor_copy(
                b.vt2[:, kt * 520 + h * 65 : kt * 520 + h * 65 + 64],
                ps[:, h * 64 : h * 64 + 64]))
        c.di("vector", sv)
    c.wait("tensor", sv)

    skh = [c.sem("pf_d0"), c.sem("pf_d1")]
    skhr = c.sem(tag + "_khr")
    c.wait("sync", sqk[0])
    c.wait("sync", sqk[1])
    for w in range(NW):
        kbase_ch = max(0, 4 * w - 4)
        nkw = (4 * w + 4 - kbase_ch) * 128
        if not hasattr(c, "pf_targets"):
            c.pf_targets = {}
        kh2_targets = []
        khr_base2 = skhr.n
        for h in range(NH2):
            p0h = (h % 2) * 64
            if w * NH2 + h >= 2:
                c.wait("sync", skhr, skhr.n + (h - 2) + 1 if False else khr_base2 + h - 1)
            prev = c.pf_targets.get(h % 2)
            if prev is not None:
                c.wait("sync", skh[h % 2], prev)
            c.dma("sync", b.khb[h % 2][p0h : p0h + 64, 0:nkw],
                  AP(d["kspill"].tensor, h * 64 * L + kbase_ch * 128,
                     [[L, 64], [1, nkw]]), skh[h % 2])
            c.dma("sync", b.qhb[h % 2][p0h : p0h + 64, 0:512],
                  AP(d["kspill"].tensor, (512 + h * 64) * L + w * WIN,
                     [[L, 64], [1, 512]]), skh[h % 2])
            kh2_targets.append(skh[h % 2].n)
            c.pf_targets[h % 2] = skh[h % 2].n
        for h in range(NH2):
            g2 = h // 2
            p0 = (h % 2) * 64
            p1 = 64 - p0
            c.wait("tensor", skh[h % 2], kh2_targets[h])
            khb = b.khb[h % 2]
            qhb = b.qhb[h % 2]
            for i4 in range(4):
                i = w * 4 + i4
                c0 = max(0, i - 4)
                ncc = i - c0 + 1
                ps1 = t.PW0[:, 0:1024]
                c.wait("tensor", sv)
                c.wait("tensor", ss)
                nk = ncc * 128
                nmm = (nk + 511) // 512
                i_l = (i - 4 * w) * 128
                c0_l = (c0 - kbase_ch) * 128
                for mi in range(nmm):
                    w0 = mi * 512
                    wlen = min(512, nk - w0)
                    fi, la = mi == 0, mi == nmm - 1
                    c.op("tensor", lambda e, ps1=ps1, p0=p0, i_l=i_l, c0_l=c0_l, w0=w0,
                         wlen=wlen, fi=fi, la=la, khb=khb, qhb=qhb:
                         e.matmul(ps1[:, w0 : w0 + wlen],
                                  qhb[p0 : p0 + 64, i_l : i_l + 128],
                                  khb[p0 : p0 + 64, c0_l + w0 : c0_l + w0 + wlen],
                                  start=True, stop=True),
                         inc=(stt, 1) if la else None)
                c.wait("vector", stt)
                if i >= 4:
                    c.op("vector", lambda v, ps1=ps1: v.tensor_add(
                        ps1[:, 0:128], ps1[:, 0:128], b.mpqk[:]))
                c.op("vector", lambda v, ps1=ps1, nk=nk: v.tensor_add(
                    ps1[:, nk - 128 : nk], ps1[:, nk - 128 : nk], b.moqk[:]))
                c.emit("vector", lambda v: v.drain())
                c.op("vector", lambda v, ps1=ps1, nk=nk, i4=i4: v.reduce_max(
                    b.mcols[:, i4 : i4 + 1], ps1[:, 0:nk], AX, negate=True))
                c.di("vector", sv)
                c.wait("tensor", sv)
            pstr = t.PW0[0:4, 1536:1664]
            c.op("tensor", lambda e, pstr=pstr: e.transpose(pstr, b.mcols[:, 0:4], t.ident[:]),
                 inc=(stt, 1))
            c.wait("vector", stt)
            c.op("vector", lambda v, pstr=pstr: v.tensor_copy(b.mrow[0:4, 0:128], pstr))
            c.di("vector", sv)
            c.wait("gpsimd", sv)
            c.dma("gpsimd", b.negm[0:1, 0:512],
                  AP(b.mrow, 0, [[128, 4], [1, 128]]), sng)
            c.dma("gpsimd", b.negm[64:65, 0:512],
                  AP(b.mrow, 0, [[128, 4], [1, 128]]), sng)
            c.wait("tensor", sng)
            cc0 = max(0, 4 * w - 4)
            for cc in range(cc0, 4 * w + 4):
                ilo = max(cc, 4 * w)
                ihi = min(cc + 4, 4 * w + 3)
                nq = (ihi - ilo + 1) * 128
                q0 = ilo * 128
                ps2 = t.PW0[:, 0:1024]
                c.wait("tensor", ss)
                c.wait("tensor", sv)
                nmm = (nq + 511) // 512
                cc_l = (cc - kbase_ch) * 128
                q0_l = q0 - w * WIN
                for mi in range(nmm):
                    w0 = mi * 512
                    wlen = min(512, nq - w0)
                    c.op("tensor", lambda e, ps2=ps2, p0=p0, cc_l=cc_l, q0_l=q0_l, w0=w0,
                         wlen=wlen, khb=khb, qhb=qhb:
                         e.matmul(ps2[:, w0 : w0 + wlen],
                                  khb[p0 : p0 + 64, cc_l : cc_l + 128],
                                  qhb[p0 : p0 + 64, q0_l + w0 : q0_l + w0 + wlen],
                                  start=True, stop=False))
                c.op("tensor", lambda e, ps2=ps2, p1=p1, nq=nq, q0=q0, w=w: e.matmul(
                    ps2[:, 0:nq], t.ones_b[p1 : p1 + 1, 0:128],
                    b.negm[p1 : p1 + 1, q0 - w * WIN : q0 - w * WIN + nq],
                    start=False, stop=True), inc=(stt, 1))
                c.wait("vector", stt)
                if cc == ilo:
                    c.op("vector", lambda v, ps2=ps2: v.tensor_add(
                        ps2[:, 0:128], ps2[:, 0:128], b.mokq[:]))
                if cc + 4 == ihi:
                    c.op("vector", lambda v, ps2=ps2, nq=nq: v.tensor_add(
                        ps2[:, nq - 128 : nq], ps2[:, nq - 128 : nq], b.mpkq[:]))
                c.di("vector", sv)
                c.wait("scalar", sv)
                c.op("scalar", lambda s, ps2=ps2, nq=nq: s.activation(
                    b.pt[:, 0:nq], ps2[:, 0:nq], AF.Exp))
                c.di("scalar", ss)
                c.wait("tensor", ss)
                for i in range(ilo, ihi + 1):
                    fi = cc == max(0, i - 4)
                    la = cc == i
                    i4 = i - 4 * w
                    c.op("tensor", lambda e, cc=cc, h=h, i=i, i4=i4, ilo=ilo, fi=fi, la=la:
                         e.matmul(t.PW1[0:65, i4 * 512 : i4 * 512 + 128],
                                  b.vt2[:, cc * 520 + h * 65 : cc * 520 + h * 65 + 65],
                                  b.pt[:, (i - ilo) * 128 : (i - ilo) * 128 + 128],
                                  start=fi, stop=la),
                         inc=(stt, 1) if la else None)
            c.wait("vector", stt)
            for i4 in range(4):
                c.op("vector", lambda v, i4=i4: v.tensor_scalar_max(
                    b.lrow[0:1, i4 * 128 : i4 * 128 + 128],
                    t.PW1[64:65, i4 * 512 : i4 * 512 + 128], 1e-30))
            c.emit("vector", lambda v: v.drain())
            c.op("vector", lambda v: v.reciprocal(b.linv[0:1, 0:512], b.lrow[0:1, 0:512]))
            c.di("vector", sv)
            c.wait("tensor", sv)
            psl = t.PW0[:, 1024:1536]
            c.op("tensor", lambda e, psl=psl: e.matmul(
                psl, t.ones_f[0:1, 0:128], b.linv[0:1, 0:512], start=True, stop=True),
                inc=(stt, 1))
            c.wait("vector", stt)
            for i4 in range(4):
                c.op("vector", lambda v, h=h, i4=i4: v.tensor_copy(
                    b.aol[0:64, h * 512 + i4 * 128 : h * 512 + i4 * 128 + 128],
                    t.PW1[0:64, i4 * 512 : i4 * 512 + 128]))
            c.emit("vector", lambda v: v.drain())
            c.op("vector", lambda v, h=h: v.tensor_mul(
                b.aol[0:64, h * 512 : h * 512 + 512], b.aol[0:64, h * 512 : h * 512 + 512],
                t.PW0[0:64, 1024:1536]))
            c.di("vector", sv)
            c.wait("tensor", sv)
            c.di("tensor", skhr)
        c.wait("tensor", swo)
        for oc in range(GD):
            ps = t.PW0[:, (oc % 2) * 512 : (oc % 2) * 512 + 512]
            c.wait("tensor", sv)
            for h in range(NH2):
                fi, la = h == 0, h == NH2 - 1
                c.op("tensor", lambda e, ps=ps, oc=oc, h=h, fi=fi, la=la: e.matmul(
                    ps, b.wo_all[0:64, oc * 1024 + h * 128 : oc * 1024 + h * 128 + 128],
                    b.aol[0:64, h * 512 : h * 512 + 512], start=fi, stop=la),
                    inc=(stt, 1) if la else None)
            c.wait("vector", stt)
            c.op("vector", lambda v, ps=ps, oc=oc, w=w: v.tensor_add(
                t.x_col(oc, slice(w * WIN, (w + 1) * WIN)),
                t.x_col(oc, slice(w * WIN, (w + 1) * WIN)), ps))
            c.di("vector", sv)
        c.wait("tensor", sv)


def ph_ffn1_ffn(ctx, d, t, b2, l):
    c = ctx
    L = c.L
    tag = f"g{l}"
    sv = c.sem(tag + "_v")
    ss = c.sem(tag + "_s")
    stt = c.sem(tag + "_t")
    ring = Ring(c, tag, b2.wslab)
    NCH = 22

    for i2 in range(L // 512):
        for j in range(NCH):
            for wnm in ("w1a", "w1g"):
                srcap = AP(d[wnm].tensor, l * DM * 2816 + j * 128,
                           [[2816, 128], [128 * 2816, GD], [1, 128]])
                ring.load(srcap, GD * 128)
        for mc in range(GD):
            srcap = AP(d["w2"].tensor, l * 2816 * DM + mc * 128,
                       [[DM, 128], [128 * DM, NCH], [1, 128]])
            ring.load(srcap, NCH * 128)

    for i2 in range(L // 512):
        sl = slice(i2 * 512, (i2 + 1) * 512)
        ps_sum = t.PW0[:, 0:512]
        ps_sq = t.PW0[:, 512:1024]
        c.wait("vector", stt)
        c.wait("vector", ss)
        for g in range(GD):
            c.op("vector", lambda v, g=g, sl=sl: v.tensor_copy(t.xb_col(g, sl), t.x_col(g, sl)))
        c.emit("vector", lambda v: v.drain())
        if sl.start == 0:
            for g in range(GD // 2, GD):
                gq = g - GD // 2
                c.op("vector", lambda v, gq=gq: v.memset(b2.fsh0[:, gq * 512 : gq * 512 + 1], 0.0))
                c.op("vector", lambda v, g=g, gq=gq: v.tensor_copy(
                    b2.fsh0[:, gq * 512 + 1 : gq * 512 + 512], t.XB[:, g * L : g * L + 511]))
        c.di("vector", sv)
        c.wait("scalar", sv)
        for g in range(GD):
            pp = t.ln_r2a if g % 2 == 0 else t.ln_r2b
            if g < GD // 2:
                srcap = t.x_col(g, sl)
                sumap = t.xb_col(g, sl)
            elif sl.start == 0:
                gq = g - GD // 2
                srcap = b2.fsh0[:, gq * 512 : gq * 512 + 512]
                sumap = srcap
            else:
                srcap = t.XB[:, g * L + sl.start - 1 : g * L + sl.stop - 1]
                sumap = srcap
            c.wait("scalar", stt, stt.n - 1 if g >= 2 else stt.n)
            c.op("scalar", lambda s, pp=pp, srcap=srcap: s.activation(pp[:, 0:512], srcap, AF.Square))
            c.di("scalar", ss)
            c.wait("tensor", ss)
            fi, la = g == 0, g == GD - 1
            c.op("tensor", lambda e, sumap=sumap, fi=fi, la=la: e.matmul(
                ps_sum, t.ones_b[:], sumap, start=fi, stop=la))
            c.op("tensor", lambda e, pp=pp, fi=fi, la=la: e.matmul(
                ps_sq, t.ones_b[:], pp[:, 0:512], start=fi, stop=la),
                inc=(stt, 1))
        c.wait("vector", stt)
        c.op("vector", lambda v: v.tensor_scalar_mul(t.ln_mu[:, 0:512], ps_sum, 1.0 / DM))
        c.di("vector", sv)
        c.wait("scalar", sv)
        c.op("scalar", lambda s: s.activation(t.ln_m2[:, 0:512], t.ln_mu[:, 0:512], AF.Square))
        c.di("scalar", ss)
        c.wait("vector", ss)
        c.op("vector", lambda v: v.scalar_tensor_tensor(
            t.ln_var[:, 0:512], ps_sq, 1.0 / DM, t.ln_m2[:, 0:512], OP.mult, OP.subtract))
        c.di("vector", sv)
        c.wait("scalar", sv)
        c.op("scalar", lambda s: s.activation(t.ln_var[:, 0:512], t.ln_var[:, 0:512], AF.Ln, bias=t.ln_eps_ap()))
        c.emit("scalar", lambda s: s.drain())
        c.op("scalar", lambda s: s.activation(t.ln_rs[:, 0:512], t.ln_var[:, 0:512], AF.Exp, scale=-0.5))
        c.di("scalar", ss)
        c.wait("vector", ss)
        c.op("vector", lambda v: v.tensor_mul(t.ln_m2[:, 0:512], t.ln_mu[:, 0:512], t.ln_rs[:, 0:512]))
        c.emit("vector", lambda v: v.drain())
        for g in range(GD):
            c.wait("vector", ss)
            if g < GD // 2:
                srcap = t.x_col(g, sl)
            elif sl.start == 0:
                srcap = b2.fsh0[:, (g - GD // 2) * 512 : (g - GD // 2) * 512 + 512]
            else:
                srcap = t.XB[:, g * L + sl.start - 1 : g * L + sl.stop - 1]
            c.op("vector", lambda v, srcap=srcap: v.tensor_mul(
                t.ln_t1[:, 0:512], srcap, t.ln_rs[:, 0:512]))
            c.emit("vector", lambda v: v.drain())
            c.op("vector", lambda v, g=g: v.tensor_sub(
                t.ln_t1[:, 0:512], t.ln_t1[:, 0:512], t.ln_m2[:, 0:512]))
            c.di("vector", sv)
            c.wait("scalar", sv)
            c.op("scalar", lambda s, g=g: s.activation(
                b2.xnl[:, ts(g, 512)], t.ln_t1[:, 0:512], AF.Identity,
                bias=b2.lnb[:, l * GD + g : l * GD + g + 1],
                scale=b2.lng[:, l * GD + g : l * GD + g + 1]))
            c.di("scalar", ss)
        c.wait("tensor", ss)
        for j in range(NCH):
            wa = ring.take()
            wg = ring.take()
            cj = 128
            psa = t.PW1[:, 0:512]
            psg = t.PW1[:, 512:1024]
            c.wait("tensor", sv)
            c.wait("tensor", ss)
            for g in range(GD):
                fi, la = g == 0, g == GD - 1
                c.op("tensor", lambda e, psa=psa, wa=wa, g=g, cj=cj, fi=fi, la=la: e.matmul(
                    psa[0:cj, :], wa[:, g * cj : g * cj + cj], b2.xnl[:, ts(g, 512)],
                    start=fi, stop=la))
            for g in range(GD):
                fi, la = g == 0, g == GD - 1
                c.op("tensor", lambda e, psg=psg, wg=wg, g=g, cj=cj, fi=fi, la=la: e.matmul(
                    psg[0:cj, :], wg[:, g * cj : g * cj + cj], b2.xnl[:, ts(g, 512)],
                    start=fi, stop=la), inc=(stt, 1) if la else None)
            ring.release()
            ring.release()
            c.wait("scalar", stt)
            c.op("scalar", lambda s, psg=psg, j=j, cj=cj: s.activation(
                b2.gl[0:cj, 0:512], psg[0:cj, :], AF.Gelu, bias=b2.b1g[0:cj, j + 22 * l : j + 22 * l + 1]))
            c.di("scalar", ss)
            c.wait("vector", ss)
            c.op("vector", lambda v, psa=psa, j=j, cj=cj: v.scalar_tensor_tensor(
                b2.h[0:cj, j * 512 : j * 512 + 512], psa[0:cj, :],
                b2.b1a[0:cj, j + 22 * l : j + 22 * l + 1],
                b2.gl[0:cj, 0:512], OP.add, OP.mult))
            c.di("vector", sv)
            c.wait("tensor", sv)
        for mc in range(GD):
            w2s = ring.take()
            ps = t.PW0[:, 1024 + (mc % 2) * 512 : 1024 + (mc % 2) * 512 + 512]
            c.wait("tensor", sv)
            for j in range(NCH):
                fi, la = j == 0, j == NCH - 1
                c.op("tensor", lambda e, ps=ps, w2s=w2s, j=j, fi=fi, la=la: e.matmul(
                    ps, w2s[:, j * 128 : j * 128 + 128],
                    b2.h[:, j * 512 : j * 512 + 512], start=fi, stop=la),
                    inc=(stt, 1) if la else None)
            ring.release()
            c.wait("vector", stt)
            c.op("vector", lambda v, ps=ps, mc=mc, sl=sl: v.scalar_tensor_tensor(
                t.x_col(mc, sl), ps, b2.b2[:, l * GD + mc : l * GD + mc + 1],
                t.x_col(mc, sl), OP.add, OP.add))
            c.di("vector", sv)
        c.wait("tensor", sv)



def build(L=2048, upto=99):
    nc = bass.Bass("TRN2", target_bir_lowering=False)
    es = ExitStack()
    ctx = Ctx(nc, es, L)

    d = {}

    def din(name, shape, dt=F32):
        d[name] = nc.dram_tensor(name, list(shape), dt, kind="ExternalInput").ap()

    din("xT", (DM, L))
    din("inproj", (DM, 2 * DIN))
    din("convw", (DIN, 4))
    din("convb", (DIN, 1))
    din("xproj", (DIN, DTR + 2 * NST))
    din("dtproj", (DTR, DIN))
    din("dtprojb", (DIN, 1))
    din("alog", (DIN, NST))
    din("dvec", (DIN, 1))
    din("outproj", (DIN, DM))
    din("mlng", (DM, 1))
    din("mlnb", (DM, 1))
    din("WqT", (DM, DM)); din("WkT", (DM, DM)); din("WvT", (DM, DM)); din("WoT", (DM, DM))
    din("AqT", (DM, 8)); din("AkT", (DM, 8)); din("AvT", (DM, 8)); din("AoT", (DM, 8))
    din("BqT", (8, DM)); din("BkT", (8, DM)); din("BvT", (8, DM)); din("BoT", (8, DM))
    din("bq", (128, GD)); din("bk", (128, GD)); din("bv", (128, GD)); din("bo", (128, GD))
    din("bvrow", (1, DM))
    din("rezero", (1, 1))
    din("wqkv", (2 * DM, 1536))
    din("wo", (2 * INNER2, DM))
    din("w1a", (2 * DM, 2816))
    din("w1g", (2 * DM, 2816))
    din("w2", (2 * 2816, DM))
    din("qscale", (128, 2))
    din("kscale", (128, 2))
    din("l1lng", (128, 2 * GD))
    din("l1lnb", (128, 2 * GD))
    din("b1a", (128, 2 * 22))
    din("b1g", (128, 2 * 22))
    din("l1b2", (128, 2 * GD))
    din("rotq1", (128, L), BF16)
    din("rotq2", (128, L), BF16)
    din("rotk1", (128, L), BF16)
    din("rotk2", (128, L), BF16)
    din("mask_own_kq", (128, 128))
    din("mask_prev_kq", (128, 128))
    din("mask_own_qk", (128, 128))
    din("mask_prev_qk", (128, 128))
    din("alibi", (128, NH1 * (L // 128)))
    din("wgT", (DM, HID0), I32)
    din("wuT", (DM, HID0), I32)
    din("wdT", (HID0, DM), I32)
    din("ident", (128, 128))
    din("fconst", (128, 8))
    d["outT"] = nc.dram_tensor("outT", [DM, L], F32, kind="ExternalOutput").ap()
    d["scr_x"] = nc.dram_tensor("scr_x", [128, GD * L], F32).ap()
    d["kspill"] = nc.dram_tensor("kspill", [DM, L], BF16).ap()
    d["scr_bc"] = nc.dram_tensor("scr_bc", [(L // 512 if L >= 512 else 1) * 32, 512], F32).ap()

    class T:
        pass

    t = T()
    perm = bass.AutoArena(nc)
    t.X = perm.sbuf_tensor("X", [128, GD * L], F32)
    t.XB = perm.sbuf_tensor("XB", [128, GD * L], BF16)
    t.ident = perm.sbuf_tensor("identS", [128, 128], F32)
    t.ident_b = perm.sbuf_tensor("identB", [128, 128], BF16)
    t.ones_f = perm.sbuf_tensor("ones_f", [128, 128], F32)
    t.ones_b = perm.sbuf_tensor("ones_b", [128, 128], BF16)
    t.fc = perm.sbuf_tensor("fconst", [128, 8], F32)

    t.ln_r2a = perm.sbuf_tensor("ln_r2a", [128, 512], BF16)
    t.ln_r2b = perm.sbuf_tensor("ln_r2b", [128, 512], BF16)
    t.ln_mu = perm.sbuf_tensor("ln_mu", [128, 512], F32)
    t.ln_m2 = perm.sbuf_tensor("ln_m2", [128, 512], F32)
    t.ln_var = perm.sbuf_tensor("ln_var", [128, 512], F32)
    t.ln_rs = perm.sbuf_tensor("ln_rs", [128, 512], F32)
    t.ln_t1 = perm.sbuf_tensor("ln_t1", [128, 512], F32)
    perm.allocate("left")

    t.ln_eps_ap = lambda: t.fc[:, 0:1]
    t.one_ap = lambda: t.fc[:, 1:2]

    def x_col(g, sl):
        return t.X[:, g * L + sl.start : g * L + sl.stop]

    def xb_col(g, sl):
        return t.XB[:, g * L + sl.start : g * L + sl.stop]

    t.x_col = x_col
    t.xb_col = xb_col

    pw0 = es.enter_context(nc.psum_tensor("PW0", [128, 2048], F32))
    pw1 = es.enter_context(nc.psum_tensor("PW1", [128, 2048], F32))
    t.PW0 = pw0
    t.PW1 = pw1

    s_pre = ctx.sem("d_pre")
    ctx.dma("sync", t.ident[:], d["ident"][:], s_pre)
    ctx.dma("sync", t.fc[:], d["fconst"][:], s_pre)

    s_preg = ctx.sem("d_preg")
    ctx.dma("gpsimd", t.ident_b[:], d["ident"][:], s_preg)
    ctx.wait("vector", s_pre)
    ctx.wait("vector", s_preg)
    ctx.op("vector", lambda v: v.memset(t.ones_f[:], 1.0))
    ctx.op("vector", lambda v: v.memset(t.ones_b[:], 1.0))
    ctx.di("vector", ctx.sem("v_init"))

    s_in = ctx.sem("d_xin")
    for g in range(GD):
        ctx.dma("sync", t.X[:, ts(g, L)], d["xT"][ts(g), :], s_in)
    ctx.wait("vector", s_in)
    ctx.wait("vector", ctx.sem("v_init"))
    for g in range(GD):
        ctx.op("vector", lambda v, g=g: v.tensor_copy(t.XB[:, ts(g, L)], t.X[:, ts(g, L)]))
    ctx.di("vector", ctx.sem("v_init"))
    ctx.barrier("load")

    if upto >= 1:
        mar = bass.AutoArena(nc)

        class M:
            pass

        m = M()
        T_ = min(512, L)
        m.xn = mar.sbuf_tensor("m_xn", [128, GD * T_], BF16)
        m.u = mar.sbuf_tensor("m_u", [128, GI * T_], BF16)
        m.sz = mar.sbuf_tensor("m_sz", [128, GI * T_], BF16)
        m.wslab = [mar.sbuf_tensor(f"m_ws{i}", [128, 4096], BF16) for i in range(2)]
        m.upre = mar.sbuf_tensor("m_upre", [128, T_ + 3], F32)
        m.hist = mar.sbuf_tensor("m_hist", [128, GI * 3], F32)
        m.cacc = mar.sbuf_tensor("m_cacc", [128, T_], F32)
        m.dtb = mar.sbuf_tensor("m_dtb", [64, T_], BF16)
        m.sbc = mar.sbuf_tensor("m_sbc", [32, T_], F32)
        m.sbc2 = mar.sbuf_tensor("m_sbc2", [65, 11 * T_], BF16)
        m.delta = mar.sbuf_tensor("m_delta", [128, T_], F32)
        m.du = mar.sbuf_tensor("m_du", [128, T_], F32)
        m.dA = mar.sbuf_tensor("m_dA", [128, 2 * T_], F32)
        m.h = mar.sbuf_tensor("m_h", [128, T_], F32)
        m.hC = mar.sbuf_tensor("m_hC", [128, T_], BF16)
        m.y1 = mar.sbuf_tensor("m_y1", [128, T_], F32)
        m.states = mar.sbuf_tensor("m_states", [128, GI * NST], F32)
        m.cacc2 = m.y1
        m.dBu = m.cacc
        m.convw = mar.sbuf_tensor("mm_convw", [128, GI * 4], F32)
        m.convb = mar.sbuf_tensor("mm_convb", [128, GI], F32)
        m.dtpb = mar.sbuf_tensor("mm_dtpb", [128, GI], F32)
        m.negA = mar.sbuf_tensor("mm_negA", [128, GI * NST], F32)
        m.D = mar.sbuf_tensor("mm_D", [128, GI], F32)
        m.xproj = mar.sbuf_tensor("mm_xproj", [128, GI * 96], BF16)
        m.dtw = mar.sbuf_tensor("mm_dtw", [64, DIN], BF16)
        m.lng = mar.sbuf_tensor("mm_lng", [128, GD], F32)
        m.lnb = mar.sbuf_tensor("mm_lnb", [128, GD], F32)
        mar.allocate("right")
        ph_mamba(ctx, d, t, m)
        ctx.barrier("mamba")
        mar.free()
        ln_full(ctx, t, t.x_col, t.x_col, "pln0", update_x=True)

    if upto >= 2:
        far = bass.AutoArena(nc)

        class F:
            pass

        f = F()
        TH = min(1024, L)
        f.h = far.sbuf_tensor("f_h", [128, G0 * TH], BF16)
        f.gl = far.sbuf_tensor("f_gl", [128, TH], F32)
        f.wslab = [far.sbuf_tensor(f"f_ws{i}", [128, G0 * 128], BF16) for i in range(3)]
        far.allocate("right")
        ph_ffn0(ctx, d, t, f)
        ctx.barrier("ffn0")
        far.free()
        ln_full(ctx, t, t.x_col, t.x_col, "pln1", update_x=True)

    if upto >= 3 or 29 <= upto <= 36:
        if 29 <= upto <= 35:
            ctx.a1_stop = upto - 30
        aar = bass.AutoArena(nc)

        class A:
            pass

        a = A()
        NKC = L // 128
        a.kst = [aar.sbuf_tensor(f"a_kst{i}", [128, 512], BF16) for i in range(2)]
        a.khb = [aar.sbuf_tensor(f"a_khb{i}", [128, L], BF16) for i in range(2)]
        a.vt = aar.sbuf_tensor("a_vt", [128, NKC * 1040], BF16)
        a.wv_ao = aar.sbuf_tensor("a_wvao", [128, 8192], BF16)
        a.ao2 = a.wv_ao
        a.qf = aar.sbuf_tensor("a_qf", [128, GD * 512], BF16)
        a.negm = aar.sbuf_tensor("a_negm", [128, 512], BF16)
        a.lrap = aar.sbuf_tensor("a_lrap", [73, L], BF16)
        a.lraq = aar.sbuf_tensor("a_lraq", [72, 512], BF16)
        a.sq = aar.sbuf_tensor("a_sq", [128, 512], BF16)
        a.sqr = aar.sbuf_tensor("a_sqr", [128, 512], F32)
        a.mx = aar.sbuf_tensor("a_mx", [128, 4], F32)
        a.negmk = aar.sbuf_tensor("a_negmk", [128, NH1], F32)
        a.kn2 = aar.sbuf_tensor("a_kn2", [128, NH1], F32)
        a.pt = aar.sbuf_tensor("a_pt", [128, 1024], BF16)
        a.wslab = [aar.sbuf_tensor(f"a_ws{i}", [128, 2048], BF16) for i in range(2)]
        a.Aq = aar.sbuf_tensor("a_Aq", [128, GD * 8], BF16)
        a.Akv = aar.sbuf_tensor("a_Akv", [128, GD * 72], BF16)
        a.Ao = aar.sbuf_tensor("a_Ao", [64, NH1 * 72], BF16)
        a.Bpack = aar.sbuf_tensor("a_Bp", [73, 2 * DM], BF16)
        a.alibi = aar.sbuf_tensor("a_alibi", [128, NH1 * (L // 128)], F32)
        a.bq = aar.sbuf_tensor("a_bq", [128, GD], F32)
        a.bk = aar.sbuf_tensor("a_bk", [128, GD], F32)
        a.bk8 = aar.sbuf_tensor("a_bk8", [128, GD], F32)
        a.bo = aar.sbuf_tensor("a_bo", [128, GD], F32)
        a.rez = aar.sbuf_tensor("a_rez", [128, 1], F32)
        a.lrow = a.sqr[0:1, :]
        a.linv = a.sqr[64:65, :]
        aar.allocate("right")
        ph_attn1(ctx, d, t, a)
        ctx.barrier("attn1")
        aar.free()
        if upto != 36:
            ln_full(ctx, t, t.x_col, t.x_col, "pln2", update_x=True)

    if upto >= 4 and not (21 <= upto <= 36):
        s_xs = ctx.sem("d_xsave")
        ctx.dma("sync", d["scr_x"][:], t.X[:, 0 : GD * L], s_xs)
        ctx.wait("sync", s_xs)
        ctx.barrier("xsave")

        bpre_g = ctx.sem("b_preg")
        bpre = ctx.sem("b_pred")
        bar0 = bass.AutoArena(nc)

        class BS:
            pass

        bs = BS()
        bs.qs8 = bar0.sbuf_tensor("b_qs8", [128, 2], F32)
        bs.ks = bar0.sbuf_tensor("b_ks", [128, 2], F32)
        bs.lng = bar0.sbuf_tensor("b_lng", [128, 2 * GD], F32)
        bs.lnb = bar0.sbuf_tensor("b_lnb", [128, 2 * GD], F32)
        bs.b1a = bar0.sbuf_tensor("b_b1a", [128, 2 * 22], F32)
        bs.b1g = bar0.sbuf_tensor("b_b1g", [128, 2 * 22], F32)
        bs.b2 = bar0.sbuf_tensor("b_b2", [128, 2 * GD], F32)
        bar0.allocate("left")
        ctx.dma("sync", bs.qs8[:], d["qscale"][:], bpre)
        ctx.dma("sync", bs.ks[:], d["kscale"][:], bpre)
        ctx.dma("sync", bs.lng[:], d["l1lng"][:], bpre)
        ctx.dma("sync", bs.lnb[:], d["l1lnb"][:], bpre)
        ctx.dma("sync", bs.b1a[:], d["b1a"][:], bpre)
        ctx.dma("sync", bs.b1g[:], d["b1g"][:], bpre)
        ctx.dma("sync", bs.b2[:], d["l1b2"][:], bpre)
        sb_pre = ctx.sem("b_pre")
        ctx.wait("vector", bpre)
        ctx.op("vector", lambda v: v.tensor_scalar_mul(bs.qs8[:], bs.qs8[:], 8.0))
        ctx.di("vector", sb_pre)

        if upto == 21:
            ctx.dbg_qk = True
        if upto == 22:
            ctx.dbg_qk = True
            ctx.dbg_raw = True
        if upto == 23:
            ctx.dbg_qk = True
            ctx.dbg_norot = True
        nsub = {11: 1, 12: 2, 13: 3, 21: 1, 22: 1, 23: 1}.get(upto, 4)
        subi = 0
        for l in range(2):
            if subi >= nsub:
                break
            bar = bass.AutoArena(nc)

            class B:
                pass

            b = B()
            NKC = L // 128
            b.qkst = [bar.sbuf_tensor(f"b_qkst{i}", [128, 512], BF16) for i in range(2)]
            b.khb = [bar.sbuf_tensor(f"b_khb{i}", [128, 1024], BF16) for i in range(2)]
            b.qhb = [bar.sbuf_tensor(f"b_qhb{i}", [128, 512], BF16) for i in range(2)]
            b.vt2 = bar.sbuf_tensor("b_vt2", [128, NKC * 520], BF16)
            b.wv = bar.sbuf_tensor("b_wv", [128, GD * 512], BF16)
            b.wo_all = bar.sbuf_tensor("b_woall", [64, 8192], BF16)
            b.xsh0 = bar.sbuf_tensor("b_xsh0", [128, 4 * 512], BF16)
            b.rotq1 = bar.sbuf_tensor("bb_rq1", [128, L], BF16)
            b.rotq2 = bar.sbuf_tensor("bb_rq2", [128, L], BF16)
            b.rotk1 = bar.sbuf_tensor("bb_rk1", [128, L], BF16)
            b.rotk2 = bar.sbuf_tensor("bb_rk2", [128, L], BF16)
            b.mokq = bar.sbuf_tensor("bb_mokq", [128, 128], F32)
            b.mpkq = bar.sbuf_tensor("bb_mpkq", [128, 128], F32)
            b.moqk = bar.sbuf_tensor("bb_moqk", [128, 128], F32)
            b.mpqk = bar.sbuf_tensor("bb_mpqk", [128, 128], F32)
            b.sq = bar.sbuf_tensor("b_sq", [128, 512], BF16)
            b.rs = bar.sbuf_tensor("b_rs", [128, 512], F32)
            b.t0 = bar.sbuf_tensor("b_t0", [128, 512], F32)
            b.t2s = bar.sbuf_tensor("b_t2s", [128, 512], BF16)
            b.t1o = bar.sbuf_tensor("b_t1o", [128, 512], BF16)
            b.t2o = bar.sbuf_tensor("b_t2o", [128, 512], BF16)
            b.mcols = bar.sbuf_tensor("b_mcols", [128, 4], F32)
            b.mrow = bar.sbuf_tensor("b_mrow", [4, 128], F32)
            b.negm = bar.sbuf_tensor("b_negm", [128, 512], BF16)
            b.pt = bar.sbuf_tensor("b_pt", [128, 640], BF16)
            b.lrow = bar.sbuf_tensor("b_lrow", [1, 512], F32)
            b.linv = bar.sbuf_tensor("b_linv", [1, 512], F32)
            b.aol = bar.sbuf_tensor("b_aol", [64, NH2 * 512], BF16)
            b.wslab = [bar.sbuf_tensor(f"b_ws{l}{i}", [128, 1024], BF16) for i in range(2)]
            b.qs8 = bs.qs8
            b.ks = bs.ks
            b.raw = t.ln_t1
            bar.allocate("right")
            ph_ffn1_mix(ctx, d, t, b, l)
            ctx.barrier(f"mix{l}")
            bar.free()
            subi += 1
            if getattr(ctx, "dbg_stop", False):
                break
            if subi >= nsub:
                break

            bar2 = bass.AutoArena(nc)

            class B2:
                pass

            b2 = B2()
            b2.xnl = bar2.sbuf_tensor("b2_xnl", [128, GD * 512], BF16)
            b2.h = bar2.sbuf_tensor("b2_h", [128, 22 * 512], BF16)
            b2.gl = bar2.sbuf_tensor("b2_gl", [128, 512], F32)
            b2.fsh0 = bar2.sbuf_tensor("b2_fsh0", [128, 4 * 512], BF16)
            b2.wslab = [bar2.sbuf_tensor(f"b2_ws{l}{i}", [128, 2816], BF16) for i in range(3)]
            b2.b1a = bs.b1a
            b2.b1g = bs.b1g
            b2.b2 = bs.b2
            b2.lng = bs.lng
            b2.lnb = bs.lnb
            bar2.allocate("right")
            ph_ffn1_ffn(ctx, d, t, b2, l)
            ctx.barrier(f"ffn{l}")
            bar2.free()
            subi += 1
        bar0.free()

        if nsub >= 4:
            s_xr = ctx.sem("d_xread")
            s_rv = ctx.sem("v_xres")
            for g in range(GD):
                for i2 in range(L // 512):
                    sl = slice(i2 * 512, (i2 + 1) * 512)
                    ctx.wait("sync", s_rv)
                    ctx.dma("sync", t.ln_t1[:, 0:512],
                            d["scr_x"][:, g * L + sl.start : g * L + sl.stop], s_xr)
                    ctx.wait("vector", s_xr)
                    ctx.op("vector", lambda v, g=g, sl=sl: v.tensor_add(
                        t.x_col(g, sl), t.x_col(g, sl), t.ln_t1[:, 0:512]))
                    ctx.di("vector", s_rv)
            ctx.barrier("xres")
            ln_full(ctx, t, t.x_col, t.x_col, "pln3", update_x=False)

    if getattr(ctx, "dbg_stop", False):
        with nc.Block() as block:
            for eng in ENGINES:
                def make(eng):
                    def f(e):
                        for fn in ctx.streams[eng]:
                            fn(e)
                    return f
                getattr(block, eng)(make(eng))
        es.close()
        return nc
    s_out = ctx.sem("d_out")
    for g in range(GD):
        ctx.dma("sync", d["outT"][ts(g), :], t.X[:, ts(g, L)], s_out)
    ctx.wait("sync", s_out)

    with nc.Block() as block:
        for eng in ENGINES:
            def make(eng):
                def f(e):
                    for fn in ctx.streams[eng]:
                        fn(e)
                return f
            getattr(block, eng)(make(eng))

    es.close()
    build.last_ctx = ctx
    return nc



_CACHE = {}


def _pad_cols(w):
    out = np.zeros((2, DM, 2816), np.float32)
    out[:, :, : w.shape[2]] = w
    return out.reshape(2 * DM, 2816)


def _pad_rows(w):
    out = np.zeros((2, 2816, DM), np.float32)
    out[:, : w.shape[1], :] = w
    return out.reshape(2 * 2816, DM)


def _pack_b1(b):
    out = np.zeros((128, 2 * 22), np.float32)
    for l in range(2):
        bp = np.zeros(22 * 128, np.float32)
        bp[: b.shape[1]] = b[l]
        out[:, l * 22 : (l + 1) * 22] = bp.reshape(22, 128).T
    return out


def prep_inputs(inputs, L):
    consts = host_constants(L)
    shared = {
        "inproj": np.asarray(inputs["L0_attn_in_proj_w"]),
        "convw": np.asarray(inputs["L0_attn_conv_w"]),
        "convb": np.asarray(inputs["L0_attn_conv_b"]).reshape(DIN, 1),
        "xproj": np.asarray(inputs["L0_attn_x_proj_w"]),
        "dtproj": np.asarray(inputs["L0_attn_dt_proj_w"]),
        "dtprojb": np.asarray(inputs["L0_attn_dt_proj_b"]).reshape(DIN, 1),
        "alog": np.asarray(inputs["L0_attn_A_log"]),
        "dvec": np.asarray(inputs["L0_attn_D"]).reshape(DIN, 1),
        "outproj": np.asarray(inputs["L0_attn_out_proj_w"]),
        "mlng": np.asarray(inputs["L0_attn_ln_g"]).reshape(DM, 1),
        "mlnb": np.asarray(inputs["L0_attn_ln_b"]).reshape(DM, 1),
        "wgT": np.ascontiguousarray(np.asarray(inputs["L0_ffn_w_gate"]).T),
        "wuT": np.ascontiguousarray(np.asarray(inputs["L0_ffn_w_up"]).T),
        "wdT": np.ascontiguousarray(np.asarray(inputs["L0_ffn_w_down"]).T),
        "WqT": np.ascontiguousarray(np.asarray(inputs["L1_attn_Wq"]).T),
        "WkT": np.ascontiguousarray(np.asarray(inputs["L1_attn_Wk"]).T),
        "WvT": np.ascontiguousarray(np.asarray(inputs["L1_attn_Wv"]).T),
        "WoT": np.ascontiguousarray(np.asarray(inputs["L1_attn_Wo"]).T),
        "AqT": np.ascontiguousarray(np.asarray(inputs["L1_attn_Aq"]).T),
        "AkT": np.ascontiguousarray(np.asarray(inputs["L1_attn_Ak"]).T),
        "AvT": np.ascontiguousarray(np.asarray(inputs["L1_attn_Av"]).T),
        "AoT": np.ascontiguousarray(np.asarray(inputs["L1_attn_Ao"]).T),
        "BqT": np.ascontiguousarray(np.asarray(inputs["L1_attn_Bq"]).T),
        "BkT": np.ascontiguousarray(np.asarray(inputs["L1_attn_Bk"]).T),
        "BvT": np.ascontiguousarray(np.asarray(inputs["L1_attn_Bv"]).T),
        "BoT": np.ascontiguousarray(np.asarray(inputs["L1_attn_Bo"]).T),
        "bq": np.ascontiguousarray(np.asarray(inputs["L1_attn_bq"]).reshape(GD, 128).T),
        "bk": np.ascontiguousarray(np.asarray(inputs["L1_attn_bk"]).reshape(GD, 128).T),
        "bv": np.ascontiguousarray(np.asarray(inputs["L1_attn_bv"]).reshape(GD, 128).T),
        "bo": np.ascontiguousarray(np.asarray(inputs["L1_attn_bo"]).reshape(GD, 128).T),
        "bvrow": np.asarray(inputs["L1_attn_bv"]).reshape(1, DM),
        "rezero": np.asarray(inputs["L1_attn_rezero"]).reshape(1, 1),
        "alibi": consts["alibi"],
        "wqkv": np.asarray(inputs["L1_ffn_wqkv"]).reshape(2 * DM, 1536),
        "wo": np.asarray(inputs["L1_ffn_wo"]).reshape(2 * INNER2, DM),
        "w1a": _pad_cols(np.asarray(inputs["L1_ffn_w1"])[:, :, :HID2]),
        "w1g": _pad_cols(np.asarray(inputs["L1_ffn_w1"])[:, :, HID2:]),
        "w2": _pad_rows(np.asarray(inputs["L1_ffn_w2"])),
        "qscale": np.ascontiguousarray(
            np.tile(np.asarray(inputs["L1_ffn_q_scale"]).T, (2, 1))),
        "kscale": np.ascontiguousarray(
            np.tile(np.asarray(inputs["L1_ffn_k_scale"]).T, (2, 1))),
        "l1lng": np.ascontiguousarray(
            np.asarray(inputs["L1_ffn_ln_g"]).reshape(2, GD, 128)
            .transpose(2, 0, 1).reshape(128, 2 * GD)),
        "l1lnb": np.ascontiguousarray(
            np.asarray(inputs["L1_ffn_ln_b"]).reshape(2, GD, 128)
            .transpose(2, 0, 1).reshape(128, 2 * GD)),
        "b1a": _pack_b1(np.asarray(inputs["L1_ffn_b1"])[:, :HID2]),
        "b1g": _pack_b1(np.asarray(inputs["L1_ffn_b1"])[:, HID2:]),
        "l1b2": np.ascontiguousarray(
            np.asarray(inputs["L1_ffn_b2"]).reshape(2, GD, 128)
            .transpose(2, 0, 1).reshape(128, 2 * GD)),
        "rotq1": consts["rotq1"], "rotq2": consts["rotq2"],
        "rotk1": consts["rotk1"], "rotk2": consts["rotk2"],
        "mask_own_kq": consts["mask_own_kq"], "mask_prev_kq": consts["mask_prev_kq"],
        "mask_own_qk": consts["mask_own_qk"], "mask_prev_qk": consts["mask_prev_qk"],
        "ident": consts["ident"],
        "fconst": np.broadcast_to(np.array([LNEPS, 1.0, 0, 0, 0, 0, 0, 0], np.float32), (128, 8)).copy(),
    }
    x = np.asarray(inputs["L0_attn_x"])
    per_core = [{"xT": np.ascontiguousarray(x[b].T)} for b in range(x.shape[0])]
    return shared, per_core


def kernel(**inputs):
    L = 2048
    if "nc" not in _CACHE:
        _CACHE["nc"] = build(L)
    nc = _CACHE["nc"]
    shared, per_core = prep_inputs(inputs, L)
    in_maps = [dict(shared, **pc) for pc in per_core]
    res = run_bass_kernel_spmd(nc, in_maps, core_ids=list(range(NCORES)))
    outs = [np.ascontiguousarray(np.asarray(r["outT"]).T) for r in res.results]
    return np.stack(outs, axis=0)


# version 03
# speedup vs baseline: 1.0009x; 1.0009x over previous
import numpy as np
from contextlib import ExitStack

import concourse.bass as bass
import concourse.mybir as mb
from concourse.bass import AP
from concourse.bass_utils import run_bass_kernel_spmd

F32 = mb.dt.float32
BF16 = mb.dt.bfloat16
I32 = mb.dt.int32
OP = mb.AluOpType
AF = mb.ActivationFunctionType
AX = mb.AxisListType.X

NCORES = 8
DM = 1024
GD = DM // 128
DIN = 2048
GI = DIN // 128
NST = 16
DTR = 64
HID0 = 3072
G0 = HID0 // 128
NH1 = 16
HD = 64
NH2 = 8
INNER2 = 512
HID2 = 2730
F1 = 5460
WIN = 512
LNEPS = 1e-5

ENGINES = ("sync", "vector", "scalar", "tensor", "gpsimd")


def ts(i, n=128):
    return slice(i * n, (i + 1) * n)


class Sem:
    __slots__ = ("h", "n", "name")

    def __init__(self, h, name):
        self.h = h
        self.n = 0
        self.name = name


class Ctx:

    def __init__(self, nc, es, L):
        self.nc = nc
        self.es = es
        self.L = L
        self.streams = {e: [] for e in ENGINES}
        self.sems = {}

    _CANON = None

    def _canon(self, name):
        if name.startswith("bar_"):
            return "bar"
        for suf, canon in (("_v", "g_v"), ("_s", "g_s"), ("_t", "g_t"),
                           ("_ng", "g_dgB"), ("_rt", "g_dgB"), ("_mk", "g_dsB"),
                           ("_wv", "g_dgC"), ("_wo", "g_dgD"),
                           ("_kh", "g_dsC"), ("_khr", "g_khr"),

                           ("_pre", "g_dsE"), ("_preg", "g_dgE"),
                           ("_p", "g_dsE"), ("_pg", "g_dgE")):
            if name.endswith(suf):
                return canon
        return name

    def sem(self, name):
        name = self._canon(name)
        if name not in self.sems:
            h = self.es.enter_context(self.nc.semaphore(name))
            self.sems[name] = Sem(h, name)
        return self.sems[name]

    def emit(self, eng, fn):
        self.streams[eng].append(fn)

    def rec(self, eng, kind, sem, val):
        if not hasattr(self, "trace"):
            self.trace = {e: [] for e in ENGINES}
        self.trace[eng].append((kind, sem, val))

    def wait(self, eng, sem, val=None):
        v = sem.n if val is None else val
        if v <= 0:
            return
        self.rec(eng, "wait", sem.name, v)
        self.emit(eng, lambda e, s=sem.h, vv=v: e.wait_ge(s, vv))

    def op(self, eng, fn, inc=None):
        if inc is None:
            self.emit(eng, fn)
        else:
            sem, k = inc
            sem.n += k
            self.rec(eng, "inc", sem.name, k)
            self.emit(eng, lambda e, f=fn, s=sem.h, kk=k: f(e).then_inc(s, kk))

    def dma(self, eng, dst, src, sem):
        sem.n += 16
        self.rec(eng, "inc", sem.name, 16)
        self.emit(eng, lambda e, dd=dst, ss=src, h=sem.h: e.dma_start(dd, ss).then_inc(h, 16))

    def di(self, eng, sem, k=1):
        sem.n += k
        self.rec(eng, "inc", sem.name, k)
        self.emit(eng, lambda e, s=sem.h, kk=k: (e.drain(), e.sem_inc(s, kk)))

    def barrier(self, tag):
        s = self.sem("bar_" + tag)
        for e in ENGINES:
            s.n += 1
            self.rec(e, "inc", s.name, 1)
            self.emit(e, lambda en, h=s.h: (en.drain(), en.sem_inc(h, 1)))
        for e in ENGINES:
            self.wait(e, s)


class Ring:

    def __init__(self, c, name, bufs):
        self.c = c
        self.bufs = bufs
        self.sd = [c.sem(f"ring_d{i}") for i in range(len(bufs))]
        self.sr = c.sem("ring_r")
        self.iprod = 0
        self.icons = 0
        self.targets = []
        self.rbase = self.sr.n

    def load(self, src_ap, ncols, rows=128):
        i = self.iprod
        nb = len(self.bufs)
        if i >= nb:
            self.c.wait("gpsimd", self.sr, self.rbase + i - nb + 1)
            self.c.wait("gpsimd", self.sd[i % nb], self.targets[i - nb])
        self.c.dma("gpsimd", self.bufs[i % nb][0:rows, 0:ncols], src_ap, self.sd[i % nb])
        self.targets.append(self.sd[i % nb].n)
        self.iprod += 1

    def take(self, eng="tensor"):
        i = self.icons
        nb = len(self.bufs)
        self.c.wait(eng, self.sd[i % nb], self.targets[i])
        self.icons += 1
        return self.bufs[i % nb]

    def release(self, eng="tensor"):
        self.c.di(eng, self.sr)



def host_constants(L):
    import ml_dtypes
    c = {}
    c["ident"] = np.eye(128, dtype=np.float32)
    nkc = L // 128
    alibi = np.zeros((128, NH1 * nkc), np.float32)
    p = np.arange(128, dtype=np.float32)[:, None]
    for h in range(NH1):
        slope = 0.5 ** h
        for kc in range(nkc):
            alibi[:, h * nkc + kc : h * nkc + kc + 1] = slope * (kc * 128 + p)
    c["alibi"] = alibi
    half = np.arange(0, HD, 2, dtype=np.float64)
    inv_freq = 1.0 / 10000.0 ** (half / HD)
    tt = np.arange(L, dtype=np.float64)
    fr = tt[:, None] * inv_freq[None, :]
    fr = np.concatenate([fr, fr], axis=-1)
    base = (half + 0.4 * HD) / (1.4 * HD)
    power = (tt - L // 2) / float(WIN // 2)
    sc = base[None, :] ** power[:, None]
    sc = np.concatenate([sc, sc], axis=-1)
    cos, sin = np.cos(fr), np.sin(fr)
    t1q = (cos * sc).T
    t2q = (sin * sc).T
    t1k = (cos / sc).T
    t2k = (sin / sc).T
    for t2 in (t2q, t2k):
        t2[0:32, :] *= -1.0

    def dup(a):
        return np.ascontiguousarray(np.concatenate([a, a], axis=0)).astype(np.float32)

    c["rotq1"], c["rotq2"] = dup(t1q), dup(t2q)
    c["rotk1"], c["rotk2"] = dup(t1k), dup(t2k)
    NEG = np.float32(-3.0e38)
    a = np.arange(128)
    c["mask_own_kq"] = np.where(a[:, None] > a[None, :], NEG, 0).astype(np.float32)
    c["mask_prev_kq"] = np.where(a[:, None] < a[None, :], NEG, 0).astype(np.float32)
    c["mask_own_qk"] = np.ascontiguousarray(c["mask_own_kq"].T)
    c["mask_prev_qk"] = np.ascontiguousarray(c["mask_prev_kq"].T)
    return c



def ln_full(ctx, t, src_col, dst_col, tag, gamma=None, beta=None,
            update_x=False):
    c = ctx
    L = c.L
    T = min(512, L)
    sv = c.sem(tag + "_v")
    ss = c.sem(tag + "_s")
    st = c.sem(tag + "_t")
    for tc in range(L // T):
        sl = slice(tc * T, (tc + 1) * T)
        ps_sum = t.PW0[:, 0:T]
        ps_sq = t.PW0[:, 512 : 512 + T]
        c.wait("vector", st)
        for g in range(GD):
            c.op("vector", lambda v, g=g, sl=sl: v.tensor_copy(t.xb_col(g, sl), src_col(g, sl)))
        c.di("vector", sv)
        c.wait("tensor", sv)
        for g in range(GD):
            pp = t.ln_r2a if g % 2 == 0 else t.ln_r2b
            c.wait("scalar", st, st.n - 1 if g >= 2 else st.n)
            c.op("scalar", lambda s, g=g, pp=pp, sl=sl: s.activation(
                pp[:, 0:T], src_col(g, sl), AF.Square), inc=None)
            c.di("scalar", ss)
            c.wait("tensor", ss)
            first, last = g == 0, g == GD - 1
            c.op("tensor", lambda e, g=g, sl=sl, f=first, l=last: e.matmul(
                ps_sum, t.ones_b[:], t.xb_col(g, sl), start=f, stop=l))
            c.op("tensor", lambda e, g=g, pp=pp, f=first, l=last: e.matmul(
                ps_sq, t.ones_b[:], pp[:, 0:T], start=f, stop=l),
                inc=(st, 1))
        c.wait("vector", st)
        c.op("vector", lambda v: v.tensor_scalar_mul(t.ln_mu[:, 0:T], ps_sum, 1.0 / DM))
        c.di("vector", sv)
        c.wait("scalar", sv)
        c.op("scalar", lambda s: s.activation(t.ln_m2[:, 0:T], t.ln_mu[:, 0:T], AF.Square))
        c.di("scalar", ss)
        c.wait("vector", ss)
        c.op("vector", lambda v: v.scalar_tensor_tensor(
            t.ln_var[:, 0:T], ps_sq, 1.0 / DM, t.ln_m2[:, 0:T], OP.mult, OP.subtract))
        c.di("vector", sv)
        c.wait("scalar", sv)
        c.op("scalar", lambda s: s.activation(t.ln_var[:, 0:T], t.ln_var[:, 0:T], AF.Ln, bias=t.ln_eps_ap()))
        c.emit("scalar", lambda s: s.drain())
        c.op("scalar", lambda s: s.activation(t.ln_rs[:, 0:T], t.ln_var[:, 0:T], AF.Exp, scale=-0.5))
        c.di("scalar", ss)
        c.wait("vector", ss)
        c.op("vector", lambda v: v.tensor_mul(t.ln_m2[:, 0:T], t.ln_mu[:, 0:T], t.ln_rs[:, 0:T]))
        c.emit("vector", lambda v: v.drain())
        for g in range(GD):
            c.op("vector", lambda v, g=g, sl=sl: v.tensor_mul(
                t.ln_t1[:, 0:T], src_col(g, sl), t.ln_rs[:, 0:T]))
            c.emit("vector", lambda v: v.drain())
            if gamma is None:
                c.op("vector", lambda v, g=g, sl=sl: v.tensor_sub(
                    dst_col(g, sl), t.ln_t1[:, 0:T], t.ln_m2[:, 0:T]))
                c.emit("vector", lambda v: v.drain())
            else:
                c.op("vector", lambda v, g=g: v.tensor_sub(
                    t.ln_t1[:, 0:T], t.ln_t1[:, 0:T], t.ln_m2[:, 0:T]))
                c.di("vector", sv)
                c.wait("scalar", sv)
                c.op("scalar", lambda s, g=g, sl=sl: s.activation(
                    dst_col(g, sl), t.ln_t1[:, 0:T], AF.Identity,
                    bias=beta[:, g : g + 1], scale=gamma[:, g : g + 1]))
                c.di("scalar", ss)
                c.wait("vector", ss)
        if update_x:
            for g in range(GD):
                c.op("vector", lambda v, g=g, sl=sl: v.tensor_copy(
                    t.xb_col(g, sl), t.x_col(g, sl)))
            c.emit("vector", lambda v: v.drain())
        c.di("vector", sv)
        c.wait("tensor", sv)
    c.barrier(tag)



def ph_mamba(ctx, d, t, m):
    c = ctx
    L = c.L
    T = min(512, L)
    NTC = L // T
    sv = c.sem("mb_v")
    ss = c.sem("mb_s")
    stt = c.sem("mb_t")
    sdw = [c.sem("ring_d0"), c.sem("ring_d1")]
    s_scan = c.sem("mb_scan")
    svw = c.sem("ring_r")
    svw_base = svw.n
    sdw_targets = []

    s_mp = c.sem("mb_pre")
    s_mpg = c.sem("mb_preg")
    for g in range(GI):
        c.dma("sync", m.convw[:, g * 4 : g * 4 + 4], d["convw"][ts(g), :], s_mp)
        c.dma("sync", m.convb[:, g : g + 1], d["convb"][ts(g), :], s_mp)
        c.dma("sync", m.dtpb[:, g : g + 1], d["dtprojb"][ts(g), :], s_mp)
        c.dma("sync", m.negA[:, g * NST : (g + 1) * NST], d["alog"][ts(g), :], s_mp)
        c.dma("sync", m.D[:, g : g + 1], d["dvec"][ts(g), :], s_mp)
    for g in range(GD):
        c.dma("sync", m.lng[:, g : g + 1], d["mlng"][ts(g), :], s_mp)
        c.dma("sync", m.lnb[:, g : g + 1], d["mlnb"][ts(g), :], s_mp)
    for g in range(GI):
        c.dma("gpsimd", m.xproj[:, g * 96 : (g + 1) * 96], d["xproj"][ts(g), :], s_mpg)
    c.dma("gpsimd", m.dtw[:], d["dtproj"][:], s_mpg)
    c.wait("scalar", s_mp)
    c.wait("scalar", s_mpg)
    c.op("scalar", lambda s: s.activation(m.negA[:], m.negA[:], AF.Exp))
    c.di("scalar", ss)
    c.wait("vector", ss)
    c.op("vector", lambda v: v.tensor_scalar_mul(m.negA[:], m.negA[:], -1.0))
    c.op("vector", lambda v: v.memset(m.states[:], 0.0))
    c.op("vector", lambda v: v.memset(m.hist[:], 0.0))
    c.di("vector", sv)
    c.wait("tensor", sv)
    c.wait("tensor", s_mp)
    c.wait("tensor", s_mpg)

    slab_emit_i = [0]

    def emit_slab_load(kind, ob):
        i = slab_emit_i[0]
        slab_emit_i[0] += 1
        if i >= 2:
            c.wait("gpsimd", svw, svw_base + i - 1)
            c.wait("gpsimd", sdw[i % 2], sdw_targets[i - 2])
        buf = m.wslab[i % 2]
        sdwi = sdw[i % 2]
        if kind == "in":
            srcap = AP(d["inproj"].tensor, ob * 512,
                       [[2 * DIN, 128], [128 * 2 * DIN, GD], [1, 512]])
            dstap = buf[:, 0 : GD * 512]
        else:
            srcap = AP(d["outproj"].tensor, ob * 256,
                       [[DM, 128], [128 * DM, GI], [1, 256]])
            dstap = buf[:, 0 : GI * 256]
        c.dma("gpsimd", dstap, srcap, sdwi)
        sdw_targets.append(sdwi.n)

    slab_i = [0]

    def next_slab():
        i = slab_i[0]
        slab_i[0] += 1
        c.wait("tensor", sdw[i % 2], sdw_targets[i])
        return m.wslab[i % 2], i

    def release_slab(i):
        c.di("tensor", svw)

    for tc in range(NTC):
        sl = slice(tc * T, (tc + 1) * T)
        for ob in range(8):
            emit_slab_load("in", ob)

        ps_sum = t.PW0[:, 0:T]
        ps_sq = t.PW0[:, 512 : 512 + T]
        for g in range(GD):
            pp = t.ln_r2a if g % 2 == 0 else t.ln_r2b
            c.wait("scalar", stt)
            c.op("scalar", lambda s, g=g, pp=pp, sl=sl: s.activation(
                pp[:, 0:T], t.x_col(g, sl), AF.Square))
            c.di("scalar", ss)
            c.wait("tensor", ss)
            first, last = g == 0, g == GD - 1
            c.op("tensor", lambda e, g=g, sl=sl, f=first, l=last: e.matmul(
                ps_sum, t.ones_b[:], t.xb_col(g, sl), start=f, stop=l))
            c.op("tensor", lambda e, pp=pp, f=first, l=last: e.matmul(
                ps_sq, t.ones_b[:], pp[:, 0:T], start=f, stop=l), inc=(stt, 1))
        c.wait("vector", stt)
        c.op("vector", lambda v: v.tensor_scalar_mul(t.ln_mu[:, 0:T], ps_sum, 1.0 / DM))
        c.di("vector", sv)
        c.wait("scalar", sv)
        c.op("scalar", lambda s: s.activation(t.ln_m2[:, 0:T], t.ln_mu[:, 0:T], AF.Square))
        c.di("scalar", ss)
        c.wait("vector", ss)
        c.op("vector", lambda v: v.scalar_tensor_tensor(
            t.ln_var[:, 0:T], ps_sq, 1.0 / DM, t.ln_m2[:, 0:T], OP.mult, OP.subtract))
        c.di("vector", sv)
        c.wait("scalar", sv)
        c.op("scalar", lambda s: s.activation(t.ln_var[:, 0:T], t.ln_var[:, 0:T], AF.Ln, bias=t.ln_eps_ap()))
        c.emit("scalar", lambda s: s.drain())
        c.op("scalar", lambda s: s.activation(t.ln_rs[:, 0:T], t.ln_var[:, 0:T], AF.Exp, scale=-0.5))
        c.di("scalar", ss)
        c.wait("vector", ss)
        c.op("vector", lambda v: v.tensor_mul(t.ln_m2[:, 0:T], t.ln_mu[:, 0:T], t.ln_rs[:, 0:T]))
        c.emit("vector", lambda v: v.drain())
        for g in range(GD):
            c.wait("vector", ss)
            c.op("vector", lambda v, g=g, sl=sl: v.tensor_mul(
                t.ln_t1[:, 0:T], t.x_col(g, sl), t.ln_rs[:, 0:T]))
            c.emit("vector", lambda v: v.drain())
            c.op("vector", lambda v, g=g: v.tensor_sub(
                t.ln_t1[:, 0:T], t.ln_t1[:, 0:T], t.ln_m2[:, 0:T]))
            c.di("vector", sv)
            c.wait("scalar", sv)
            c.op("scalar", lambda s, g=g: s.activation(
                m.xn[:, ts(g, T)], t.ln_t1[:, 0:T], AF.Identity,
                bias=m.lnb[:, g : g + 1], scale=m.lng[:, g : g + 1]))
            c.di("scalar", ss)
        c.wait("tensor", ss)
        c.wait("vector", ss)

        for ob in range(8):
            buf, si = next_slab()
            for oc in range(4):
                o_global = ob * 4 + oc
                ps = t.PW1[:, (oc % 2) * 512 : (oc % 2) * 512 + T]
                c.wait("tensor", sv)
                c.wait("tensor", ss)
                for g in range(GD):
                    first, last = g == 0, g == GD - 1
                    c.op("tensor", lambda e, ps=ps, buf=buf, g=g, oc=oc, f=first, l=last:
                         e.matmul(ps, buf[:, g * 512 + oc * 128 : g * 512 + oc * 128 + 128],
                                  m.xn[:, ts(g, T)], start=f, stop=l),
                         inc=(stt, 1) if last else None)
                dc = o_global if ob < 4 else o_global - GI
                if ob < 4:
                    c.wait("vector", stt)
                    c.op("vector", lambda v, dc=dc: v.tensor_copy(m.upre[:, 0:3], m.hist[:, dc * 3 : dc * 3 + 3]))
                    c.op("vector", lambda v, ps=ps: v.tensor_copy(m.upre[:, 3 : 3 + T], ps))
                    c.emit("vector", lambda v: v.drain())
                    c.op("vector", lambda v, dc=dc: v.tensor_copy(m.hist[:, dc * 3 : dc * 3 + 3], m.upre[:, T : T + 3]))
                    c.emit("vector", lambda v: v.drain())
                    c.op("vector", lambda v, dc=dc: v.tensor_scalar_mul(
                        m.cacc[:, 0:T], m.upre[:, 0:T], m.convw[:, dc * 4 : dc * 4 + 1]))
                    c.emit("vector", lambda v: v.drain())
                    for j in (1, 2, 3):
                        src_acc = m.cacc if j % 2 == 1 else m.cacc2
                        dst_acc = m.cacc2 if j % 2 == 1 else m.cacc
                        c.op("vector", lambda v, dc=dc, j=j, sa=src_acc, da=dst_acc:
                             v.scalar_tensor_tensor(
                                 da[:, 0:T], m.upre[:, j : j + T],
                                 m.convw[:, dc * 4 + j : dc * 4 + j + 1],
                                 sa[:, 0:T], OP.mult, OP.add))
                        c.emit("vector", lambda v: v.drain())
                    c.di("vector", sv)
                    c.wait("scalar", sv)
                    c.op("scalar", lambda s, dc=dc: s.activation(
                        m.u[:, ts(dc, T)], m.cacc2[:, 0:T], AF.Silu,
                        bias=m.convb[:, dc : dc + 1]))
                    c.di("scalar", ss)
                else:
                    c.wait("scalar", stt)
                    c.op("scalar", lambda s, ps=ps, dc=dc: s.activation(
                        m.sz[:, ts(dc, T)], ps, AF.Silu))
                    c.di("scalar", ss)
            release_slab(si)
        c.wait("tensor", ss)

        psd = t.PW0[0:96, 1024 : 1024 + T]
        for g in range(GI):
            first, last = g == 0, g == GI - 1
            c.op("tensor", lambda e, g=g, f=first, l=last: e.matmul(
                psd, m.xproj[:, g * 96 : (g + 1) * 96], m.u[:, ts(g, T)],
                start=f, stop=l), inc=(stt, 1) if last else None)
        c.wait("scalar", stt)
        c.op("scalar", lambda s: s.activation(m.dtb[0:64, 0:T], t.PW0[0:64, 1024 : 1024 + T], AF.Copy))
        c.di("scalar", ss)
        c.wait("vector", stt)
        c.op("vector", lambda v: v.tensor_copy(m.sbc[0:32, 0:T], t.PW0[64:96, 1024 : 1024 + T]))
        c.di("vector", sv)
        sbcd = c.sem("mb_sbcd")
        c.wait("sync", sv)
        c.dma("sync", d["scr_bc"][tc * 32 : (tc + 1) * 32, :][:, 0:T], m.sbc[0:32, 0:T], sbcd)
        c.wait("gpsimd", sbcd, sbcd.n)
        for ph in range(3):
            cnt = len(range(ph, 32, 3))
            srcap = AP(d["scr_bc"].tensor, (tc * 32 + ph) * 512, [[3 * 512, cnt], [1, T]])
            c.dma("gpsimd", m.sbc2[ph * 32 : ph * 32 + 1, 0 : cnt * T], srcap, c.sem("mb_sbcg"))
        sbcg = c.sem("mb_sbcg")
        for ob in range(4):
            emit_slab_load("out", ob)
        c.wait("tensor", sbcd)
        c.wait("tensor", c.sem("mb_sbcg"))
        c.wait("tensor", sv)
        c.wait("tensor", ss)

        for dc in range(GI):
            pdt = t.PW0[:, 1536 : 1536 + T]
            c.op("tensor", lambda e, dc=dc: e.matmul(
                pdt, m.dtw[:, ts(dc)], m.dtb[0:64, 0:T], start=True, stop=True),
                inc=(stt, 1))
            c.wait("scalar", stt)
            c.op("scalar", lambda s, dc=dc: s.activation(
                m.delta[:, 0:T], pdt, AF.Exp, bias=m.dtpb[:, dc : dc + 1]))
            c.emit("scalar", lambda s: s.drain())
            c.op("scalar", lambda s: s.activation(m.delta[:, 0:T], m.delta[:, 0:T], AF.Ln, bias=t.one_ap()))
            c.di("scalar", ss)
            c.wait("vector", ss)
            c.op("vector", lambda v, dc=dc: v.tensor_mul(m.du[:, 0:T], m.delta[:, 0:T], m.u[:, ts(dc, T)]))
            c.di("vector", sv)
            c.wait("tensor", sv)
            for n in range(NST):
                idx = dc * NST + n
                slot = idx % 2
                pb = t.PW1[:, 1024 + slot * 512 : 1024 + slot * 512 + T]
                pcc = t.PW0[:, slot * 512 : slot * 512 + T]
                c.wait("tensor", s_scan, s_scan.n - 1)
                rb, rc = n, NST + n
                pb_b, pc_b = (rb % 3) * 32, (rc % 3) * 32
                bap = m.sbc2[pb_b : pb_b + 1, (rb // 3) * T : (rb // 3) * T + T]
                cap = m.sbc2[pc_b : pc_b + 1, (rc // 3) * T : (rc // 3) * T + T]
                c.op("tensor", lambda e, pb=pb, bap=bap, pb_b=pb_b: e.matmul(
                    pb, t.ones_b[pb_b : pb_b + 1, 0:128], bap, start=True, stop=True))
                c.op("tensor", lambda e, pcc=pcc, cap=cap, pc_b=pc_b: e.matmul(
                    pcc, t.ones_b[pc_b : pc_b + 1, 0:128], cap, start=True, stop=True), inc=(stt, 1))
                c.wait("scalar", s_scan, s_scan.n - 1)
                c.op("scalar", lambda s, dc=dc, n=n, slot=slot: s.activation(
                    m.dA[:, slot * T : slot * T + T], m.delta[:, 0:T], AF.Exp,
                    scale=m.negA[:, dc * NST + n : dc * NST + n + 1]))
                c.di("scalar", ss)
                c.wait("vector", stt)
                c.wait("vector", ss)
                c.op("vector", lambda v, pb=pb: v.tensor_mul(m.dBu[:, 0:T], m.du[:, 0:T], pb))
                c.emit("vector", lambda v: v.drain())
                c.op("vector", lambda v, dc=dc, n=n, slot=slot: v.tensor_tensor_scan(
                    m.h[:, 0:T], m.dA[:, slot * T : slot * T + T], m.dBu[:, 0:T],
                    m.states[:, dc * NST + n : dc * NST + n + 1], OP.mult, OP.add),
                    inc=(s_scan, 1))
                c.emit("vector", lambda v: v.drain())
                c.op("vector", lambda v, dc=dc, n=n: v.tensor_copy(
                    m.states[:, dc * NST + n : dc * NST + n + 1], m.h[:, T - 1 : T]))
                c.op("vector", lambda v, pcc=pcc: v.tensor_mul(m.hC[:, 0:T], m.h[:, 0:T], pcc))
                c.di("vector", sv)
                c.wait("tensor", sv)
                psy = t.PW1[:, 0:T]
                c.op("tensor", lambda e, psy=psy, n=n: e.matmul(
                    psy, t.ident_b[:], m.hC[:, 0:T], start=(n == 0), stop=(n == NST - 1)),
                    inc=(stt, 1) if n == NST - 1 else None)
            c.wait("vector", stt)
            c.op("vector", lambda v, dc=dc: v.scalar_tensor_tensor(
                m.y1[:, 0:T], m.u[:, ts(dc, T)], m.D[:, dc : dc + 1], t.PW1[:, 0:T],
                OP.mult, OP.add))
            c.emit("vector", lambda v: v.drain())
            c.op("vector", lambda v, dc=dc: v.tensor_mul(
                m.u[:, ts(dc, T)], m.y1[:, 0:T], m.sz[:, ts(dc, T)]))
            c.di("vector", sv)
            c.wait("tensor", sv)

        for ob in range(4):
            buf, si = next_slab()
            for oc in range(2):
                mg = ob * 2 + oc
                ps = t.PW0[:, (oc % 2) * 512 : (oc % 2) * 512 + T]
                c.wait("tensor", sv)
                for g in range(GI):
                    first, last = g == 0, g == GI - 1
                    c.op("tensor", lambda e, ps=ps, buf=buf, g=g, oc=oc, f=first, l=last:
                         e.matmul(ps, buf[:, g * 256 + oc * 128 : g * 256 + oc * 128 + 128],
                                  m.u[:, ts(g, T)], start=f, stop=l),
                         inc=(stt, 1) if last else None)
                c.wait("vector", stt)
                c.op("vector", lambda v, ps=ps, mg=mg, sl=sl: v.scalar_tensor_tensor(
                    t.x_col(mg, sl), t.x_col(mg, sl), 2.0, ps, OP.mult, OP.add))
                c.di("vector", sv)
            release_slab(si)
        c.wait("tensor", sv)



def ph_ffn0(ctx, d, t, f):
    c = ctx
    L = c.L
    TH = min(1024, L)
    sv = c.sem("f0_v")
    ss = c.sem("f0_s")
    stt = c.sem("f0_t")
    ring = Ring(c, "f0", f.wslab)

    for th in range(L // TH):
        for m in range(G0):
            for wname in ("wgT", "wuT"):
                srcap = AP(d[wname].tensor, m * 128,
                           [[HID0, 128], [128 * HID0, GD], [1, 128]])
                ring.load(srcap, GD * 128)
        for mc in range(GD):
            srcap = AP(d["wdT"].tensor, mc * 128,
                       [[DM, 128], [128 * DM, G0], [1, 128]])
            ring.load(srcap, G0 * 128)

    for th in range(L // TH):
        for m in range(G0):
            wg = ring.take()
            wu = ring.take()
            for t2 in range(TH // 512):
                sl = slice(th * TH + t2 * 512, th * TH + t2 * 512 + 512)
                hcol = m * TH + t2 * 512
                psg = t.PW1[:, 0:512]
                psu = t.PW1[:, 512:1024]
                c.wait("tensor", ss)
                c.wait("tensor", sv)
                for g in range(GD):
                    fi, la = g == 0, g == GD - 1
                    c.op("tensor", lambda e, psg=psg, wg=wg, g=g, sl=sl, fi=fi, la=la:
                         e.matmul(psg, wg[:, g * 128 : g * 128 + 128], t.xb_col(g, sl),
                                  start=fi, stop=la))
                for g in range(GD):
                    fi, la = g == 0, g == GD - 1
                    c.op("tensor", lambda e, psu=psu, wu=wu, g=g, sl=sl, fi=fi, la=la:
                         e.matmul(psu, wu[:, g * 128 : g * 128 + 128], t.xb_col(g, sl),
                                  start=fi, stop=la),
                         inc=(stt, 1) if la else None)
                c.wait("scalar", stt)
                c.op("scalar", lambda s, psg=psg, t2=t2: s.activation(
                    f.gl[:, t2 * 512 : t2 * 512 + 512], psg, AF.Silu, scale=1.0 / 127.0))
                c.di("scalar", ss)
                c.wait("vector", ss)
                c.op("vector", lambda v, psu=psu, m=m, t2=t2, hcol=hcol: v.tensor_mul(
                    f.h[:, hcol : hcol + 512], f.gl[:, t2 * 512 : t2 * 512 + 512], psu))
                c.di("vector", sv)
            ring.release()
            ring.release()
        c.wait("tensor", sv)
        for mc in range(GD):
            wd = ring.take()
            for t2 in range(TH // 512):
                sl = slice(th * TH + t2 * 512, th * TH + t2 * 512 + 512)
                ps = t.PW0[:, (t2 % 2) * 512 : (t2 % 2) * 512 + 512]
                c.wait("tensor", sv)
                for hm in range(G0):
                    fi, la = hm == 0, hm == G0 - 1
                    c.op("tensor", lambda e, ps=ps, wd=wd, hm=hm, t2=t2, fi=fi, la=la:
                         e.matmul(ps, wd[:, hm * 128 : hm * 128 + 128],
                                  f.h[:, hm * TH + t2 * 512 : hm * TH + t2 * 512 + 512],
                                  start=fi, stop=la),
                         inc=(stt, 1) if la else None)
                c.wait("vector", stt)
                c.op("vector", lambda v, ps=ps, mc=mc, sl=sl: v.scalar_tensor_tensor(
                    t.x_col(mc, sl), ps, 1.0 / 16129.0, t.x_col(mc, sl), OP.mult, OP.add))
                c.di("vector", sv)
            ring.release()
        c.wait("tensor", sv)




def ph_attn1(ctx, d, t, a):
    c = ctx
    import os as _os0
    delta = int(_os0.environ.get("A1_DELTA", "99"))
    if getattr(c, "a1_stop", 99) <= -1:
        if delta >= 1:
            c.di("vector", c.sem("a1_v"))
        if delta >= 2:
            c.wait("gpsimd", c.sem("a1_v"))
        if delta >= 3:
            c.wait("tensor", c.sem("a1_v"))
        return
    L = c.L
    NKC = L // 128
    NQB = L // 512
    nkc = NKC
    sv = c.sem("a1_v")
    ss = c.sem("a1_s")
    stt = c.sem("a1_t")
    sp = c.sem("a1_p")
    spg = c.sem("a1_pg")
    ring = Ring(c, "a1", a.wslab)

    import os
    mask = int(os.environ.get("A1_PRE_MASK", "127"))
    if mask & 1:
        for nm, dst in (("bq", a.bq), ("bk", a.bk), ("bo", a.bo)):
            c.dma("sync", dst[:], d[nm][:], sp)
        c.dma("sync", a.alibi[:], d["alibi"][:], sp)
    if mask & 2:
        c.dma("gpsimd", a.rez[:], AP(d["rezero"].tensor, 0, [[0, 128], [1, 1]]), spg)
    if mask & 8:
        c.dma("gpsimd", a.Aq[:], AP(d["AqT"].tensor, 0, [[8, 128], [128 * 8, GD], [1, 8]]), spg)
    import os as _os
    if int(_os.environ.get("A1_MS", "1")):
        c.op("vector", lambda v: v.memset(a.Akv[:], 0.0))
        c.op("vector", lambda v: v.memset(a.Ao[:], 0.0))
    c.di("vector", sv)
    c.wait("gpsimd", sv)
    if mask & 16:
        c.dma("gpsimd", AP(a.Akv, 0, [[GD * 72, 128], [72, GD], [1, 8]]),
              AP(d["AkT"].tensor, 0, [[8, 128], [128 * 8, GD], [1, 8]]), spg)
        c.dma("gpsimd", AP(a.Akv, 64, [[GD * 72, 128], [72, GD], [1, 8]]),
              AP(d["AvT"].tensor, 0, [[8, 128], [128 * 8, GD], [1, 8]]), spg)
        c.dma("gpsimd", AP(a.Ao, 64, [[NH1 * 72, 64], [72, NH1], [1, 8]]),
              AP(d["AoT"].tensor, 0, [[8, 64], [64 * 8, NH1], [1, 8]]), spg)
    if mask & 32:
        for nm, r0, c0 in (("BkT", 0, 0), ("BvT", 64, 0), ("BqT", 0, DM), ("BoT", 64, DM)):
            c.dma("gpsimd", a.Bpack[r0 : r0 + 8, c0 : c0 + DM], d[nm][:], spg)
        c.dma("gpsimd", a.Bpack[72:73, 0:DM], d["bvrow"][:], spg)
    c.wait("vector", sp)
    if mask & 64:
        c.op("vector", lambda v: v.tensor_scalar_mul(a.bk8[:], a.bk[:], 0.125))
    if int(_os.environ.get("A1_MS", "1")):
        c.op("vector", lambda v: v.memset(a.negm[:], 0.0))
    c.op("vector", lambda v: v.memset(a.lrap[64:73, :], 1.0))
    c.di("vector", sv)

    def wslab_src(wname, oc):
        return AP(d[wname].tensor, oc * 128, [[DM, 128], [128 * DM, GD], [1, 128]])

    swv = c.sem("a1_wv")

    if getattr(c, "a1_stop", 99) <= 0:
        c.wait("tensor", spg)
        c.wait("tensor", sp)
        c.wait("tensor", sv)
        return
    c.wait("tensor", spg)
    c.wait("tensor", sv)
    for i2 in range(L // 512):
        sl = slice(i2 * 512, (i2 + 1) * 512)
        ps = t.PW0[0:72, 1536:2048]
        for g in range(GD):
            fi, la = g == 0, g == GD - 1
            c.op("tensor", lambda e, ps=ps, g=g, sl=sl, fi=fi, la=la:
                 e.matmul(ps, a.Akv[:, g * 72 : g * 72 + 72], t.xb_col(g, sl), start=fi, stop=la),
                 inc=(stt, 1) if la else None)
        c.wait("scalar", stt)
        c.op("scalar", lambda s, ps=ps, sl=sl: s.activation(
            a.lrap[0:72, sl], ps, AF.Identity, scale=0.125))
        c.di("scalar", ss)
        c.wait("tensor", ss)

    if getattr(c, "a1_stop", 99) <= 1:
        return
    skd = [c.sem("spill_d0"), c.sem("spill_d1")]
    skd_t = []
    for oc in range(GD):
        ring.load(wslab_src("WkT", oc), GD * 128)
    for oc in range(GD):
        wk = ring.take()
        for i2 in range(L // 512):
            sl = slice(i2 * 512, (i2 + 1) * 512)
            ps = t.PW1[:, (i2 % 2) * 512 : (i2 % 2) * 512 + 512]
            ik = oc * (L // 512) + i2
            kst = a.kst[ik % 2]
            c.wait("tensor", ss)
            for g in range(GD):
                fi = g == 0
                c.op("tensor", lambda e, ps=ps, wk=wk, g=g, sl=sl, fi=fi:
                     e.matmul(ps, wk[:, g * 128 : g * 128 + 128], t.xb_col(g, sl),
                              start=fi, stop=False))
            c.op("tensor", lambda e, ps=ps, oc=oc, sl=sl: e.matmul(
                ps, a.Bpack[0:8, oc * 128 : oc * 128 + 128], a.lrap[0:8, sl],
                start=False, stop=True), inc=(stt, 1))
            c.wait("scalar", stt)
            if ik >= 2:
                c.wait("scalar", skd[ik % 2], skd_t[ik - 2])
            c.op("scalar", lambda s, ps=ps, oc=oc, kst=kst: s.activation(
                kst[:], ps, AF.Identity, bias=a.bk8[:, oc : oc + 1], scale=0.125))
            c.di("scalar", ss)
            c.wait("sync", ss)
            c.dma("sync", AP(d["kspill"].tensor, oc * 128 * L + sl.start,
                             [[L, 128], [1, 512]]), kst[:], skd[ik % 2])
            skd_t.append(skd[ik % 2].n)
            c.wait("scalar", ss, ss.n)
            c.op("scalar", lambda s, kst=kst: s.activation(a.sq[:, 0:512], kst[:], AF.Square))
            c.di("scalar", ss)
            c.wait("tensor", ss)
            psn = t.PW0[:, 1536:2048]
            for par in range(2):
                p0 = par * 64
                c.op("tensor", lambda e, psn=psn, p0=p0: e.matmul(
                    psn, t.ones_b[p0 : p0 + 64, 0:128], a.sq[p0 : p0 + 64, 0:512],
                    start=True, stop=True), inc=(stt, 1))
                c.wait("vector", stt)
                h = 2 * oc + par
                c.op("vector", lambda v, psn=psn, i2=i2: v.reduce_max(
                    a.mx[:, 0:1], psn, AX))
                c.emit("vector", lambda v: v.drain())
                if i2 == 0:
                    c.op("vector", lambda v, h=h: v.tensor_copy(a.kn2[:, h : h + 1], a.mx[:, 0:1]))
                else:
                    c.op("vector", lambda v, h=h: v.tensor_max(
                        a.kn2[:, h : h + 1], a.kn2[:, h : h + 1], a.mx[:, 0:1]))
                c.di("vector", sv)
                c.wait("tensor", sv)
        ring.release()
    c.wait("scalar", sv)
    c.op("scalar", lambda s: s.activation(a.negmk[:], a.kn2[:], AF.Sqrt))
    c.di("scalar", ss)
    c.wait("vector", ss)
    c.op("vector", lambda v: v.tensor_scalar_mul(a.negmk[:], a.negmk[:], -1.0))
    c.di("vector", sv)

    if getattr(c, "a1_stop", 99) <= 2:
        return
    c.dma("gpsimd", a.wv_ao[:, 0 : GD * 1024],
          AP(d["WvT"].tensor, 0, [[DM, 128], [128 * DM, GD], [1, DM]]), swv)
    c.wait("tensor", swv)
    c.wait("tensor", sv)
    c.op("vector", lambda v: v.memset(a.vt[:], 0.0))
    c.emit("vector", lambda v: v.drain())
    for kt in range(NKC):
        for h in range(NH1):
            c.op("vector", lambda v, kt=kt, h=h: v.memset(
                a.vt[:, kt * 1040 + h * 65 + 64 : kt * 1040 + h * 65 + 65], 1.0))
    c.di("vector", sv)
    c.wait("tensor", sv)
    for kt in range(NKC):
        for ob2 in range(2):
            ps = t.PW1[:, (ob2 % 2) * 512 : (ob2 % 2) * 512 + 512]
            c.wait("tensor", sv)
            for g in range(GD):
                c.op("tensor", lambda e, ps=ps, kt=kt, g=g, ob2=ob2: e.matmul(
                    ps, t.xb_col(g, slice(kt * 128, kt * 128 + 128)),
                    a.wv_ao[:, g * 1024 + ob2 * 512 : g * 1024 + ob2 * 512 + 512],
                    start=(g == 0), stop=False))
            c.op("tensor", lambda e, ps=ps, kt=kt, ob2=ob2: e.matmul(
                ps, a.lrap[64:73, kt * 128 : kt * 128 + 128],
                a.Bpack[64:73, ob2 * 512 : ob2 * 512 + 512],
                start=False, stop=True), inc=(stt, 1))
            c.wait("vector", stt)
            for h2 in range(8):
                h = ob2 * 8 + h2
                c.op("vector", lambda v, ps=ps, kt=kt, h=h, h2=h2: v.tensor_copy(
                    a.vt[:, kt * 1040 + h * 65 : kt * 1040 + h * 65 + 64],
                    ps[:, h2 * 64 : h2 * 64 + 64]))
            c.di("vector", sv)
    c.wait("tensor", sv)

    if getattr(c, "a1_stop", 99) <= 3:
        return
    for qb in range(NQB):
        qsl = slice(qb * 512, (qb + 1) * 512)
        ps = t.PW0[0:8, 1536:2048]
        c.wait("tensor", ss)
        c.wait("tensor", sv)
        for g in range(GD):
            fi, la = g == 0, g == GD - 1
            c.op("tensor", lambda e, ps=ps, g=g, fi=fi, la=la, qsl=qsl: e.matmul(
                ps, a.Aq[:, g * 8 : g * 8 + 8], t.xb_col(g, qsl), start=fi, stop=la),
                inc=(stt, 1) if la else None)
        c.wait("scalar", stt)
        c.op("scalar", lambda s, ps=ps: s.activation(a.lraq[0:8, 0:512], ps, AF.Identity, scale=0.125))
        c.di("scalar", ss)
        c.wait("tensor", ss)
        for oc in range(GD):
            ring.load(wslab_src("WqT", oc), GD * 128)
        for oc in range(GD):
            wq = ring.take()
            ps = t.PW1[:, (oc % 2) * 512 : (oc % 2) * 512 + 512]
            c.wait("tensor", ss)
            for g in range(GD):
                fi = g == 0
                c.op("tensor", lambda e, ps=ps, wq=wq, g=g, fi=fi, qsl=qsl: e.matmul(
                    ps, wq[:, g * 128 : g * 128 + 128], t.xb_col(g, qsl), start=fi, stop=False))
            c.op("tensor", lambda e, ps=ps, oc=oc: e.matmul(
                ps, a.Bpack[0:8, DM + oc * 128 : DM + oc * 128 + 128], a.lraq[0:8, 0:512],
                start=False, stop=True), inc=(stt, 1))
            ring.release()
            c.wait("scalar", stt)
            c.op("scalar", lambda s, ps=ps, oc=oc: s.activation(
                a.qf[:, oc * 512 : oc * 512 + 512], ps, AF.Identity,
                bias=a.bq[:, oc : oc + 1]))
            c.di("scalar", ss)
        c.wait("tensor", ss)
        if getattr(c, "a1_stop", 99) <= 4:
            continue
        c.wait("sync", skd[0])
        c.wait("sync", skd[1])
        skh = [c.sem("pf_d0"), c.sem("pf_d1")]
        if not hasattr(c, "pf_targets"):
            c.pf_targets = {}
        kh_targets = []
        khr_base = c.sem("a1_khr").n
        for h in range(NH1):
            p0h = (h % 2) * 64
            if khr_base + h - 1 > 0:
                c.wait("sync", c.sem("a1_khr"), khr_base + h - 1)
            prev = c.pf_targets.get(h % 2)
            if prev is not None:
                c.wait("sync", skh[h % 2], prev)
            c.dma("sync", a.khb[h % 2][p0h : p0h + 64, :],
                  AP(d["kspill"].tensor, h * 64 * L, [[L, 64], [1, L]]), skh[h % 2])
            kh_targets.append(skh[h % 2].n)
            c.pf_targets[h % 2] = skh[h % 2].n
        for h in range(NH1):
            g = h // 2
            p0 = (h % 2) * 64
            p1 = 64 - p0
            c.wait("tensor", skh[h % 2], kh_targets[h])
            c.wait("scalar", sv)
            c.wait("scalar", stt)
            c.op("scalar", lambda s, g=g, p0=p0: s.activation(
                a.sq[p0 : p0 + 64, 0:512], a.qf[p0 : p0 + 64, g * 512 : g * 512 + 512], AF.Square))
            c.di("scalar", ss)
            c.wait("tensor", ss)
            psn = t.PW0[:, 1536:2048]
            c.op("tensor", lambda e, psn=psn, p0=p0: e.matmul(
                psn, t.ones_b[p0 : p0 + 64, 0:128], a.sq[p0 : p0 + 64, 0:512],
                start=True, stop=True), inc=(stt, 1))
            c.wait("scalar", stt)
            c.op("scalar", lambda s, psn=psn: s.activation(a.sqr[:, 0:512], psn, AF.Sqrt))
            c.di("scalar", ss)
            c.wait("vector", ss)
            c.wait("vector", stt)
            slope2047 = -(0.5 ** h) * (L - 1.0)
            c.op("vector", lambda v, h=h, p1=p1, s2=slope2047: v.tensor_scalar(
                a.negm[p1 : p1 + 1, 0:512], a.sqr[p1 : p1 + 1, 0:512],
                a.negmk[p1 : p1 + 1, h : h + 1], s2, OP.mult, OP.add))
            c.di("vector", sv)
            c.wait("tensor", sv)
            for kc in range(NKC):
                slot = kc % 2
                pss = t.PW0[:, slot * 512 : slot * 512 + 512]
                c.wait("tensor", ss, ss.n - 1)
                c.op("tensor", lambda e, pss=pss, h=h, p0=p0, kc=kc: e.matmul(
                    pss, a.khb[h % 2][p0 : p0 + 64, kc * 128 : kc * 128 + 128],
                    a.qf[p0 : p0 + 64, (h // 2) * 512 : (h // 2) * 512 + 512],
                    start=True, stop=False))
                c.op("tensor", lambda e, pss=pss, p1=p1: e.matmul(
                    pss, t.ones_b[p1 : p1 + 1, 0:128], a.negm[p1 : p1 + 1, 0:512],
                    start=False, stop=True), inc=(stt, 1))
                c.wait("scalar", stt)
                c.op("scalar", lambda s, pss=pss, h=h, kc=kc, slot=slot: s.activation(
                    a.pt[:, slot * 512 : slot * 512 + 512], pss, AF.Exp,
                    bias=a.alibi[:, h * nkc + kc : h * nkc + kc + 1]), inc=(ss, 1))
                c.wait("tensor", ss)
                pso = t.PW1[0:65, 1024:1536]
                c.op("tensor", lambda e, pso=pso, kc=kc, h=h, slot=slot: e.matmul(
                    pso, a.vt[:, kc * 1040 + h * 65 : kc * 1040 + h * 65 + 65],
                    a.pt[:, slot * 512 : slot * 512 + 512],
                    start=(kc == 0), stop=(kc == NKC - 1)),
                    inc=(stt, 1) if kc == NKC - 1 else None)
            c.di("tensor", c.sem("a1_khr"))
            c.wait("vector", stt)
            c.op("vector", lambda v: v.tensor_scalar_max(a.lrow,
                                                         t.PW1[64:65, 1024:1536], 1e-30))
            c.emit("vector", lambda v: v.drain())
            c.op("vector", lambda v: v.reciprocal(a.linv, a.lrow))
            c.di("vector", sv)
            c.wait("tensor", sv)
            psl = t.PW1[:, 1536:2048]
            c.op("tensor", lambda e, psl=psl: e.matmul(
                psl, t.ones_f[64:65, 0:128], a.linv, start=True, stop=True),
                inc=(stt, 1))
            c.wait("vector", stt)
            c.op("vector", lambda v, h=h: v.tensor_copy(
                a.ao2[0:64, h * 512 : h * 512 + 512], t.PW1[0:64, 1024:1536]))
            c.emit("vector", lambda v: v.drain())
            c.op("vector", lambda v, h=h: v.tensor_mul(
                a.ao2[0:64, h * 512 : h * 512 + 512], a.ao2[0:64, h * 512 : h * 512 + 512],
                t.PW1[0:64, 1536:2048]))
            c.di("vector", sv)
            c.wait("tensor", sv)
        if getattr(c, "a1_stop", 99) <= 5:
            continue
        ps = t.PW0[0:72, 1536:2048]
        for h in range(NH1):
            fi, la = h == 0, h == NH1 - 1
            c.op("tensor", lambda e, ps=ps, h=h, fi=fi, la=la: e.matmul(
                ps, a.Ao[0:64, h * 72 : h * 72 + 72], a.ao2[0:64, h * 512 : h * 512 + 512],
                start=fi, stop=la), inc=(stt, 1) if la else None)
        c.wait("scalar", stt)
        c.op("scalar", lambda s, ps=ps: s.activation(a.lraq[64:72, 0:512],
                                                     t.PW0[64:72, 1536:2048], AF.Identity, scale=0.125))
        c.di("scalar", ss)
        c.wait("tensor", ss)
        for oc in range(GD):
            srcap = AP(d["WoT"].tensor, oc * 128, [[DM, 64], [64 * DM, NH1], [1, 128]])
            ring.load(srcap, NH1 * 128, rows=64)
        for oc in range(GD):
            wo = ring.take()
            ps = t.PW0[:, (oc % 2) * 512 : (oc % 2) * 512 + 512]
            c.wait("tensor", sv)
            c.wait("tensor", ss)
            for h in range(NH1):
                fi = h == 0
                c.op("tensor", lambda e, ps=ps, wo=wo, h=h, fi=fi: e.matmul(
                    ps, wo[0:64, h * 128 : h * 128 + 128],
                    a.ao2[0:64, h * 512 : h * 512 + 512], start=fi, stop=False))
            c.op("tensor", lambda e, ps=ps, oc=oc: e.matmul(
                ps, a.Bpack[64:72, DM + oc * 128 : DM + oc * 128 + 128], a.lraq[64:72, 0:512],
                start=False, stop=True), inc=(stt, 1))
            ring.release()
            c.wait("vector", stt)
            c.op("vector", lambda v, ps=ps, oc=oc: v.tensor_scalar(
                a.sqr[:, 0:512], ps, a.bo[:, oc : oc + 1], a.rez[:, 0:1], OP.add, OP.mult))
            c.emit("vector", lambda v: v.drain())
            c.op("vector", lambda v, oc=oc, qsl=qsl: v.scalar_tensor_tensor(
                t.x_col(oc, qsl), t.x_col(oc, qsl), 2.0, a.sqr[:, 0:512], OP.mult, OP.add))
            c.di("vector", sv)
        c.wait("tensor", sv)




def ph_ffn1_mix(ctx, d, t, b, l):
    c = ctx
    L = c.L
    NKC = L // 128
    NW = L // WIN
    tag = f"m{l}"
    sv = c.sem(tag + "_v")
    ss = c.sem(tag + "_s")
    stt = c.sem(tag + "_t")
    sng = c.sem(tag + "_ng")
    ring = Ring(c, tag, b.wslab)

    c.wait("vector", c.sem("b_pre"))
    s_rt = c.sem(tag + "_rt")
    for nm, dst in (("rotq1", b.rotq1), ("rotq2", b.rotq2),
                    ("rotk1", b.rotk1), ("rotk2", b.rotk2)):
        c.dma("gpsimd", dst[:], d[nm][:], s_rt)
    s_mk = c.sem(tag + "_mk")
    for nm, dst in (("mask_own_kq", b.mokq), ("mask_prev_kq", b.mpkq),
                    ("mask_own_qk", b.moqk), ("mask_prev_qk", b.mpqk)):
        c.dma("sync", dst[:], d[nm][:], s_mk)
    c.wait("vector", s_rt)
    c.wait("vector", s_mk)
    for g in range(GD):
        c.op("vector", lambda v, g=g: v.tensor_copy(t.XB[:, ts(g, L)], t.X[:, ts(g, L)]))
    c.emit("vector", lambda v: v.drain())
    for g in range(GD // 2, GD):
        gq = g - GD // 2
        c.op("vector", lambda v, gq=gq: v.memset(b.xsh0[:, gq * 512 : gq * 512 + 1], 0.0))
        c.op("vector", lambda v, g=g, gq=gq: v.tensor_copy(
            b.xsh0[:, gq * 512 + 1 : gq * 512 + 512], t.XB[:, g * L : g * L + 511]))
    c.di("vector", sv)

    def qkv_src(oc):
        return AP(d["wqkv"].tensor, l * DM * 1536 + oc * 128,
                  [[1536, 128], [128 * 1536, GD], [1, 128]])

    for oc in range(8):
        ring.load(qkv_src(oc), GD * 128)
    swv = c.sem(tag + "_wv")
    c.dma("gpsimd", b.wv[:, 0 : GD * 512],
          AP(d["wqkv"].tensor, l * DM * 1536 + 1024, [[1536, 128], [128 * 1536, GD], [1, 512]]), swv)
    swo = c.sem(tag + "_wo")
    c.dma("gpsimd", b.wo_all[0:64, 0:8192],
          AP(d["wo"].tensor, l * INNER2 * DM,
             [[DM, 64], [128, GD], [64 * DM, NH2], [1, 128]]), swo)

    def shifted_mm(e, ps, g, sl, fi, la, stationary_ap):
        if g < GD // 2:
            return e.matmul(ps[:, 0 : sl.stop - sl.start], stationary_ap, t.xb_col(g, sl), start=fi, stop=la)
        elif sl.start == 0:
            gq = g - GD // 2
            return e.matmul(ps[:, 0 : sl.stop], stationary_ap,
                            b.xsh0[:, gq * 512 : gq * 512 + sl.stop], start=fi, stop=la)
        else:
            return e.matmul(ps[:, 0 : sl.stop - sl.start], stationary_ap,
                            t.XB[:, g * L + sl.start - 1 : g * L + sl.stop - 1], start=fi, stop=la)

    sqk = [c.sem("spill_d0"), c.sem("spill_d1")]
    sqk_t = []
    qk_i = [0]
    for part, dst in (("q", None), ("k", None)):
        base_oc = 0 if part == "q" else 4
        for oc2 in range(4):
            oc = base_oc + oc2
            wslab = ring.take()
            for i2 in range(L // 512):
                sl = slice(i2 * 512, (i2 + 1) * 512)
                ps = t.PW1[:, (i2 % 2) * 512 : (i2 % 2) * 512 + 512]
                c.wait("tensor", ss)
                c.wait("tensor", sv)
                for g in range(GD):
                    fi, la = g == 0, g == GD - 1
                    stat = wslab[:, g * 128 : g * 128 + 128]
                    c.op("tensor", lambda e, ps=ps, g=g, sl=sl, fi=fi, la=la, stat=stat:
                         shifted_mm(e, ps, g, sl, fi, la, stat),
                         inc=(stt, 1) if la else None)
                if getattr(c, "dbg_raw", False):
                    c.wait("scalar", stt)
                    c.op("scalar", lambda s, ps=ps, dst=dst, oc2=oc2, sl=sl: s.activation(
                        dst[:, oc2 * L + sl.start : oc2 * L + sl.stop], ps, AF.Identity))
                    c.di("scalar", ss)
                    c.wait("tensor", ss)
                    continue
                c.wait("scalar", stt)
                c.op("scalar", lambda s, ps=ps: s.activation(b.sq[:, 0:512], ps, AF.Square))
                c.emit("scalar", lambda s: s.drain())
                c.op("scalar", lambda s, ps=ps: s.activation(b.raw[:, 0:512], ps, AF.Identity))
                c.di("scalar", ss)
                c.wait("tensor", ss)
                psn = t.PW0[:, 1024:1536]
                c.op("tensor", lambda e, psn=psn: e.matmul(
                    psn[0:64, :], t.ones_b[0:64, 0:64], b.sq[0:64, 0:512],
                    start=True, stop=True))
                c.op("tensor", lambda e, psn=psn: e.matmul(
                    psn[64:128, :], t.ones_b[64:128, 64:128], b.sq[64:128, 0:512],
                    start=True, stop=True), inc=(stt, 1))
                c.wait("vector", stt)
                c.op("vector", lambda v, psn=psn: v.tensor_scalar_max(b.rs[:, 0:512], psn, 1e-24))
                c.di("vector", sv)
                c.wait("scalar", sv)
                c.op("scalar", lambda s: s.activation(b.rs[:, 0:512], b.rs[:, 0:512], AF.Ln))
                c.emit("scalar", lambda s: s.drain())
                c.op("scalar", lambda s: s.activation(b.rs[:, 0:512], b.rs[:, 0:512], AF.Exp, scale=-0.5))
                c.di("scalar", ss)
                c.wait("vector", ss)
                qsc = b.qs8 if part == "q" else b.ks
                c.op("vector", lambda v, qsc=qsc: v.tensor_scalar_mul(
                    b.rs[:, 0:512], b.rs[:, 0:512], qsc[:, l : l + 1]))
                c.emit("vector", lambda v: v.drain())
                c.op("vector", lambda v: v.tensor_mul(b.t0[:, 0:512], b.raw[:, 0:512], b.rs[:, 0:512]))
                c.emit("vector", lambda v: v.drain())
                if getattr(c, "dbg_norot", False):
                    c.op("vector", lambda v, dst=dst, oc2=oc2, sl=sl: v.tensor_copy(
                        dst[:, oc2 * L + sl.start : oc2 * L + sl.stop], b.t0[:, 0:512]))
                    c.di("vector", sv)
                    c.wait("tensor", sv)
                    continue
                for (dp, sp) in ((0, 32), (32, 0), (64, 96), (96, 64)):
                    c.op("vector", lambda v, dp=dp, sp=sp: v.tensor_copy(
                        b.t2s[dp : dp + 32, 0:512], b.t0[sp : sp + 32, 0:512]))
                t1t = b.rotq1 if part == "q" else b.rotk1
                t2t = b.rotq2 if part == "q" else b.rotk2
                c.op("vector", lambda v, oc2=oc2, sl=sl, t1t=t1t: v.tensor_mul(
                    b.t1o[:, 0:512], b.t0[:, 0:512], t1t[:, sl]))
                c.emit("vector", lambda v: v.drain())
                c.op("vector", lambda v, oc2=oc2, sl=sl, t2t=t2t: v.tensor_mul(
                    b.t2o[:, 0:512], b.t2s[:, 0:512], t2t[:, sl]))
                c.emit("vector", lambda v: v.drain())
                iq = qk_i[0]
                qk_i[0] += 1
                qkst = b.qkst[iq % 2]
                if iq >= 2:
                    c.wait("vector", sqk[iq % 2], sqk_t[iq - 2])
                c.op("vector", lambda v, qkst=qkst: v.tensor_add(
                    qkst[:], b.t1o[:, 0:512], b.t2o[:, 0:512]))
                c.di("vector", sv)
                c.wait("sync", sv)
                rowbase = (512 if part == "q" else 0) + oc2 * 128
                c.dma("sync", AP(d["kspill"].tensor, rowbase * L + sl.start,
                                 [[L, 128], [1, 512]]), qkst[:], sqk[iq % 2])
                sqk_t.append(sqk[iq % 2].n)
                c.wait("tensor", sv)
            ring.release()
    c.wait("tensor", sv)
    c.wait("tensor", ss)

    if getattr(c, "dbg_qk", False):
        sdbg = c.sem("dbg_qk")
        c.wait("gpsimd", sv)
        c.wait("gpsimd", ss)
        for g2 in range(4):
            c.dma("gpsimd", d["outT"][ts(g2), :], b.qh[:, g2 * L : (g2 + 1) * L], sdbg)
            c.dma("gpsimd", d["outT"][ts(g2 + 4), :], b.kh[:, g2 * L : (g2 + 1) * L], sdbg)
        c.wait("sync", sdbg)
        c.dbg_stop = True
        return

    c.wait("tensor", swv)
    c.op("vector", lambda v: v.memset(b.vt2[:], 0.0))
    c.emit("vector", lambda v: v.drain())
    for kt in range(NKC):
        for h in range(NH2):
            c.op("vector", lambda v, kt=kt, h=h: v.memset(
                b.vt2[:, kt * 520 + h * 65 + 64 : kt * 520 + h * 65 + 65], 1.0))
    c.di("vector", sv)
    c.wait("tensor", sv)
    for kt in range(NKC):
        ps = t.PW1[:, (kt % 2) * 512 : (kt % 2) * 512 + 512]
        c.wait("tensor", sv)
        for g in range(GD):
            fi, la = g == 0, g == GD - 1
            if g < GD // 2:
                stat = t.XB[:, g * L + kt * 128 : g * L + kt * 128 + 128]
            elif kt == 0:
                stat = b.xsh0[:, (g - GD // 2) * 512 : (g - GD // 2) * 512 + 128]
            else:
                stat = t.XB[:, g * L + kt * 128 - 1 : g * L + kt * 128 + 127]
            c.op("tensor", lambda e, ps=ps, stat=stat, g=g, fi=fi, la=la: e.matmul(
                ps, stat, b.wv[:, g * 512 : g * 512 + 512], start=fi, stop=la),
                inc=(stt, 1) if la else None)
        c.wait("vector", stt)
        for h in range(NH2):
            c.op("vector", lambda v, ps=ps, kt=kt, h=h: v.tensor_copy(
                b.vt2[:, kt * 520 + h * 65 : kt * 520 + h * 65 + 64],
                ps[:, h * 64 : h * 64 + 64]))
        c.di("vector", sv)
    c.wait("tensor", sv)

    skh = [c.sem("pf_d0"), c.sem("pf_d1")]
    skhr = c.sem(tag + "_khr")
    c.wait("sync", sqk[0])
    c.wait("sync", sqk[1])
    for w in range(NW):
        kbase_ch = max(0, 4 * w - 4)
        nkw = (4 * w + 4 - kbase_ch) * 128
        if not hasattr(c, "pf_targets"):
            c.pf_targets = {}
        kh2_targets = []
        khr_base2 = skhr.n
        for h in range(NH2):
            p0h = (h % 2) * 64
            if w * NH2 + h >= 2:
                c.wait("sync", skhr, skhr.n + (h - 2) + 1 if False else khr_base2 + h - 1)
            prev = c.pf_targets.get(h % 2)
            if prev is not None:
                c.wait("sync", skh[h % 2], prev)
            c.dma("sync", b.khb[h % 2][p0h : p0h + 64, 0:nkw],
                  AP(d["kspill"].tensor, h * 64 * L + kbase_ch * 128,
                     [[L, 64], [1, nkw]]), skh[h % 2])
            c.dma("sync", b.qhb[h % 2][p0h : p0h + 64, 0:512],
                  AP(d["kspill"].tensor, (512 + h * 64) * L + w * WIN,
                     [[L, 64], [1, 512]]), skh[h % 2])
            kh2_targets.append(skh[h % 2].n)
            c.pf_targets[h % 2] = skh[h % 2].n
        for h in range(NH2):
            g2 = h // 2
            p0 = (h % 2) * 64
            p1 = 64 - p0
            c.wait("tensor", skh[h % 2], kh2_targets[h])
            khb = b.khb[h % 2]
            qhb = b.qhb[h % 2]
            for i4 in range(4):
                i = w * 4 + i4
                c0 = max(0, i - 4)
                ncc = i - c0 + 1
                ps1 = t.PW0[:, 0:1024]
                c.wait("tensor", sv)
                c.wait("tensor", ss)
                nk = ncc * 128
                nmm = (nk + 511) // 512
                i_l = (i - 4 * w) * 128
                c0_l = (c0 - kbase_ch) * 128
                for mi in range(nmm):
                    w0 = mi * 512
                    wlen = min(512, nk - w0)
                    fi, la = mi == 0, mi == nmm - 1
                    c.op("tensor", lambda e, ps1=ps1, p0=p0, i_l=i_l, c0_l=c0_l, w0=w0,
                         wlen=wlen, fi=fi, la=la, khb=khb, qhb=qhb:
                         e.matmul(ps1[:, w0 : w0 + wlen],
                                  qhb[p0 : p0 + 64, i_l : i_l + 128],
                                  khb[p0 : p0 + 64, c0_l + w0 : c0_l + w0 + wlen],
                                  start=True, stop=True),
                         inc=(stt, 1) if la else None)
                c.wait("vector", stt)
                if i >= 4:
                    c.op("vector", lambda v, ps1=ps1: v.tensor_add(
                        ps1[:, 0:128], ps1[:, 0:128], b.mpqk[:]))
                c.op("vector", lambda v, ps1=ps1, nk=nk: v.tensor_add(
                    ps1[:, nk - 128 : nk], ps1[:, nk - 128 : nk], b.moqk[:]))
                c.emit("vector", lambda v: v.drain())
                c.op("vector", lambda v, ps1=ps1, nk=nk, i4=i4: v.reduce_max(
                    b.mcols[:, i4 : i4 + 1], ps1[:, 0:nk], AX, negate=True))
                c.di("vector", sv)
                c.wait("tensor", sv)
            pstr = t.PW0[0:4, 1536:1664]
            c.op("tensor", lambda e, pstr=pstr: e.transpose(pstr, b.mcols[:, 0:4], t.ident[:]),
                 inc=(stt, 1))
            c.wait("vector", stt)
            c.op("vector", lambda v, pstr=pstr: v.tensor_copy(b.mrow[0:4, 0:128], pstr))
            c.di("vector", sv)
            c.wait("gpsimd", sv)
            c.dma("gpsimd", b.negm[0:1, 0:512],
                  AP(b.mrow, 0, [[128, 4], [1, 128]]), sng)
            c.dma("gpsimd", b.negm[64:65, 0:512],
                  AP(b.mrow, 0, [[128, 4], [1, 128]]), sng)
            c.wait("tensor", sng)
            cc0 = max(0, 4 * w - 4)
            for cc in range(cc0, 4 * w + 4):
                ilo = max(cc, 4 * w)
                ihi = min(cc + 4, 4 * w + 3)
                nq = (ihi - ilo + 1) * 128
                q0 = ilo * 128
                ps2 = t.PW0[:, 0:1024]
                c.wait("tensor", ss)
                c.wait("tensor", sv)
                nmm = (nq + 511) // 512
                cc_l = (cc - kbase_ch) * 128
                q0_l = q0 - w * WIN
                for mi in range(nmm):
                    w0 = mi * 512
                    wlen = min(512, nq - w0)
                    c.op("tensor", lambda e, ps2=ps2, p0=p0, cc_l=cc_l, q0_l=q0_l, w0=w0,
                         wlen=wlen, khb=khb, qhb=qhb:
                         e.matmul(ps2[:, w0 : w0 + wlen],
                                  khb[p0 : p0 + 64, cc_l : cc_l + 128],
                                  qhb[p0 : p0 + 64, q0_l + w0 : q0_l + w0 + wlen],
                                  start=True, stop=False))
                c.op("tensor", lambda e, ps2=ps2, p1=p1, nq=nq, q0=q0, w=w: e.matmul(
                    ps2[:, 0:nq], t.ones_b[p1 : p1 + 1, 0:128],
                    b.negm[p1 : p1 + 1, q0 - w * WIN : q0 - w * WIN + nq],
                    start=False, stop=True), inc=(stt, 1))
                c.wait("vector", stt)
                if cc == ilo:
                    c.op("vector", lambda v, ps2=ps2: v.tensor_add(
                        ps2[:, 0:128], ps2[:, 0:128], b.mokq[:]))
                if cc + 4 == ihi:
                    c.op("vector", lambda v, ps2=ps2, nq=nq: v.tensor_add(
                        ps2[:, nq - 128 : nq], ps2[:, nq - 128 : nq], b.mpkq[:]))
                c.di("vector", sv)
                c.wait("scalar", sv)
                c.op("scalar", lambda s, ps2=ps2, nq=nq: s.activation(
                    b.pt[:, 0:nq], ps2[:, 0:nq], AF.Exp))
                c.di("scalar", ss)
                c.wait("tensor", ss)
                for i in range(ilo, ihi + 1):
                    fi = cc == max(0, i - 4)
                    la = cc == i
                    i4 = i - 4 * w
                    c.op("tensor", lambda e, cc=cc, h=h, i=i, i4=i4, ilo=ilo, fi=fi, la=la:
                         e.matmul(t.PW1[0:65, i4 * 512 : i4 * 512 + 128],
                                  b.vt2[:, cc * 520 + h * 65 : cc * 520 + h * 65 + 65],
                                  b.pt[:, (i - ilo) * 128 : (i - ilo) * 128 + 128],
                                  start=fi, stop=la),
                         inc=(stt, 1) if la else None)
            c.wait("vector", stt)
            for i4 in range(4):
                c.op("vector", lambda v, i4=i4: v.tensor_scalar_max(
                    b.lrow[0:1, i4 * 128 : i4 * 128 + 128],
                    t.PW1[64:65, i4 * 512 : i4 * 512 + 128], 1e-30))
            c.emit("vector", lambda v: v.drain())
            c.op("vector", lambda v: v.reciprocal(b.linv[0:1, 0:512], b.lrow[0:1, 0:512]))
            c.di("vector", sv)
            c.wait("tensor", sv)
            psl = t.PW0[:, 1024:1536]
            c.op("tensor", lambda e, psl=psl: e.matmul(
                psl, t.ones_f[0:1, 0:128], b.linv[0:1, 0:512], start=True, stop=True),
                inc=(stt, 1))
            c.wait("vector", stt)
            for i4 in range(4):
                c.op("vector", lambda v, h=h, i4=i4: v.tensor_copy(
                    b.aol[0:64, h * 512 + i4 * 128 : h * 512 + i4 * 128 + 128],
                    t.PW1[0:64, i4 * 512 : i4 * 512 + 128]))
            c.emit("vector", lambda v: v.drain())
            c.op("vector", lambda v, h=h: v.tensor_mul(
                b.aol[0:64, h * 512 : h * 512 + 512], b.aol[0:64, h * 512 : h * 512 + 512],
                t.PW0[0:64, 1024:1536]))
            c.di("vector", sv)
            c.wait("tensor", sv)
            c.di("tensor", skhr)
        c.wait("tensor", swo)
        for oc in range(GD):
            ps = t.PW0[:, (oc % 2) * 512 : (oc % 2) * 512 + 512]
            c.wait("tensor", sv)
            for h in range(NH2):
                fi, la = h == 0, h == NH2 - 1
                c.op("tensor", lambda e, ps=ps, oc=oc, h=h, fi=fi, la=la: e.matmul(
                    ps, b.wo_all[0:64, oc * 1024 + h * 128 : oc * 1024 + h * 128 + 128],
                    b.aol[0:64, h * 512 : h * 512 + 512], start=fi, stop=la),
                    inc=(stt, 1) if la else None)
            c.wait("vector", stt)
            c.op("vector", lambda v, ps=ps, oc=oc, w=w: v.tensor_add(
                t.x_col(oc, slice(w * WIN, (w + 1) * WIN)),
                t.x_col(oc, slice(w * WIN, (w + 1) * WIN)), ps))
            c.di("vector", sv)
        c.wait("tensor", sv)


def ph_ffn1_ffn(ctx, d, t, b2, l):
    c = ctx
    L = c.L
    tag = f"g{l}"
    sv = c.sem(tag + "_v")
    ss = c.sem(tag + "_s")
    stt = c.sem(tag + "_t")
    ring = Ring(c, tag, b2.wslab)
    NCH = 22

    for i2 in range(L // 512):
        for j in range(NCH):
            for wnm in ("w1a", "w1g"):
                srcap = AP(d[wnm].tensor, l * DM * 2816 + j * 128,
                           [[2816, 128], [128 * 2816, GD], [1, 128]])
                ring.load(srcap, GD * 128)
        for mc in range(GD):
            srcap = AP(d["w2"].tensor, l * 2816 * DM + mc * 128,
                       [[DM, 128], [128 * DM, NCH], [1, 128]])
            ring.load(srcap, NCH * 128)

    for i2 in range(L // 512):
        sl = slice(i2 * 512, (i2 + 1) * 512)
        ps_sum = t.PW0[:, 0:512]
        ps_sq = t.PW0[:, 512:1024]
        c.wait("vector", stt)
        c.wait("vector", ss)
        for g in range(GD):
            c.op("vector", lambda v, g=g, sl=sl: v.tensor_copy(t.xb_col(g, sl), t.x_col(g, sl)))
        c.emit("vector", lambda v: v.drain())
        if sl.start == 0:
            for g in range(GD // 2, GD):
                gq = g - GD // 2
                c.op("vector", lambda v, gq=gq: v.memset(b2.fsh0[:, gq * 512 : gq * 512 + 1], 0.0))
                c.op("vector", lambda v, g=g, gq=gq: v.tensor_copy(
                    b2.fsh0[:, gq * 512 + 1 : gq * 512 + 512], t.XB[:, g * L : g * L + 511]))
        c.di("vector", sv)
        c.wait("scalar", sv)
        for g in range(GD):
            pp = t.ln_r2a if g % 2 == 0 else t.ln_r2b
            if g < GD // 2:
                srcap = t.x_col(g, sl)
                sumap = t.xb_col(g, sl)
            elif sl.start == 0:
                gq = g - GD // 2
                srcap = b2.fsh0[:, gq * 512 : gq * 512 + 512]
                sumap = srcap
            else:
                srcap = t.XB[:, g * L + sl.start - 1 : g * L + sl.stop - 1]
                sumap = srcap
            c.wait("scalar", stt, stt.n - 1 if g >= 2 else stt.n)
            c.op("scalar", lambda s, pp=pp, srcap=srcap: s.activation(pp[:, 0:512], srcap, AF.Square))
            c.di("scalar", ss)
            c.wait("tensor", ss)
            fi, la = g == 0, g == GD - 1
            c.op("tensor", lambda e, sumap=sumap, fi=fi, la=la: e.matmul(
                ps_sum, t.ones_b[:], sumap, start=fi, stop=la))
            c.op("tensor", lambda e, pp=pp, fi=fi, la=la: e.matmul(
                ps_sq, t.ones_b[:], pp[:, 0:512], start=fi, stop=la),
                inc=(stt, 1))
        c.wait("vector", stt)
        c.op("vector", lambda v: v.tensor_scalar_mul(t.ln_mu[:, 0:512], ps_sum, 1.0 / DM))
        c.di("vector", sv)
        c.wait("scalar", sv)
        c.op("scalar", lambda s: s.activation(t.ln_m2[:, 0:512], t.ln_mu[:, 0:512], AF.Square))
        c.di("scalar", ss)
        c.wait("vector", ss)
        c.op("vector", lambda v: v.scalar_tensor_tensor(
            t.ln_var[:, 0:512], ps_sq, 1.0 / DM, t.ln_m2[:, 0:512], OP.mult, OP.subtract))
        c.di("vector", sv)
        c.wait("scalar", sv)
        c.op("scalar", lambda s: s.activation(t.ln_var[:, 0:512], t.ln_var[:, 0:512], AF.Ln, bias=t.ln_eps_ap()))
        c.emit("scalar", lambda s: s.drain())
        c.op("scalar", lambda s: s.activation(t.ln_rs[:, 0:512], t.ln_var[:, 0:512], AF.Exp, scale=-0.5))
        c.di("scalar", ss)
        c.wait("vector", ss)
        c.op("vector", lambda v: v.tensor_mul(t.ln_m2[:, 0:512], t.ln_mu[:, 0:512], t.ln_rs[:, 0:512]))
        c.emit("vector", lambda v: v.drain())
        for g in range(GD):
            c.wait("vector", ss)
            if g < GD // 2:
                srcap = t.x_col(g, sl)
            elif sl.start == 0:
                srcap = b2.fsh0[:, (g - GD // 2) * 512 : (g - GD // 2) * 512 + 512]
            else:
                srcap = t.XB[:, g * L + sl.start - 1 : g * L + sl.stop - 1]
            c.op("vector", lambda v, srcap=srcap: v.tensor_mul(
                t.ln_t1[:, 0:512], srcap, t.ln_rs[:, 0:512]))
            c.emit("vector", lambda v: v.drain())
            c.op("vector", lambda v, g=g: v.tensor_sub(
                t.ln_t1[:, 0:512], t.ln_t1[:, 0:512], t.ln_m2[:, 0:512]))
            c.di("vector", sv)
            c.wait("scalar", sv)
            c.op("scalar", lambda s, g=g: s.activation(
                b2.xnl[:, ts(g, 512)], t.ln_t1[:, 0:512], AF.Identity,
                bias=b2.lnb[:, l * GD + g : l * GD + g + 1],
                scale=b2.lng[:, l * GD + g : l * GD + g + 1]))
            c.di("scalar", ss)
        c.wait("tensor", ss)
        for j in range(NCH):
            wa = ring.take()
            wg = ring.take()
            cj = 128
            psa = t.PW1[:, 0:512]
            psg = t.PW1[:, 512:1024]
            c.wait("tensor", sv)
            c.wait("tensor", ss)
            for g in range(GD):
                fi, la = g == 0, g == GD - 1
                c.op("tensor", lambda e, psa=psa, wa=wa, g=g, cj=cj, fi=fi, la=la: e.matmul(
                    psa[0:cj, :], wa[:, g * cj : g * cj + cj], b2.xnl[:, ts(g, 512)],
                    start=fi, stop=la))
            for g in range(GD):
                fi, la = g == 0, g == GD - 1
                c.op("tensor", lambda e, psg=psg, wg=wg, g=g, cj=cj, fi=fi, la=la: e.matmul(
                    psg[0:cj, :], wg[:, g * cj : g * cj + cj], b2.xnl[:, ts(g, 512)],
                    start=fi, stop=la), inc=(stt, 1) if la else None)
            ring.release()
            ring.release()
            c.wait("scalar", stt)
            c.op("scalar", lambda s, psg=psg, j=j, cj=cj: s.activation(
                b2.gl[0:cj, 0:512], psg[0:cj, :], AF.Gelu, bias=b2.b1g[0:cj, j + 22 * l : j + 22 * l + 1]))
            c.di("scalar", ss)
            c.wait("vector", ss)
            c.op("vector", lambda v, psa=psa, j=j, cj=cj: v.scalar_tensor_tensor(
                b2.h[0:cj, j * 512 : j * 512 + 512], psa[0:cj, :],
                b2.b1a[0:cj, j + 22 * l : j + 22 * l + 1],
                b2.gl[0:cj, 0:512], OP.add, OP.mult))
            c.di("vector", sv)
            c.wait("tensor", sv)
        for mc in range(GD):
            w2s = ring.take()
            ps = t.PW0[:, 1024 + (mc % 2) * 512 : 1024 + (mc % 2) * 512 + 512]
            c.wait("tensor", sv)
            for j in range(NCH):
                fi, la = j == 0, j == NCH - 1
                c.op("tensor", lambda e, ps=ps, w2s=w2s, j=j, fi=fi, la=la: e.matmul(
                    ps, w2s[:, j * 128 : j * 128 + 128],
                    b2.h[:, j * 512 : j * 512 + 512], start=fi, stop=la),
                    inc=(stt, 1) if la else None)
            ring.release()
            c.wait("vector", stt)
            c.op("vector", lambda v, ps=ps, mc=mc, sl=sl: v.scalar_tensor_tensor(
                t.x_col(mc, sl), ps, b2.b2[:, l * GD + mc : l * GD + mc + 1],
                t.x_col(mc, sl), OP.add, OP.add))
            c.di("vector", sv)
        c.wait("tensor", sv)



def build(L=2048, upto=99):
    nc = bass.Bass("TRN2", target_bir_lowering=False)
    es = ExitStack()
    ctx = Ctx(nc, es, L)

    d = {}

    def din(name, shape, dt=F32):
        d[name] = nc.dram_tensor(name, list(shape), dt, kind="ExternalInput").ap()

    din("xT", (DM, L))
    din("inproj", (DM, 2 * DIN))
    din("convw", (DIN, 4))
    din("convb", (DIN, 1))
    din("xproj", (DIN, DTR + 2 * NST))
    din("dtproj", (DTR, DIN))
    din("dtprojb", (DIN, 1))
    din("alog", (DIN, NST))
    din("dvec", (DIN, 1))
    din("outproj", (DIN, DM))
    din("mlng", (DM, 1))
    din("mlnb", (DM, 1))
    din("WqT", (DM, DM)); din("WkT", (DM, DM)); din("WvT", (DM, DM)); din("WoT", (DM, DM))
    din("AqT", (DM, 8)); din("AkT", (DM, 8)); din("AvT", (DM, 8)); din("AoT", (DM, 8))
    din("BqT", (8, DM)); din("BkT", (8, DM)); din("BvT", (8, DM)); din("BoT", (8, DM))
    din("bq", (128, GD)); din("bk", (128, GD)); din("bv", (128, GD)); din("bo", (128, GD))
    din("bvrow", (1, DM))
    din("rezero", (1, 1))
    din("wqkv", (2 * DM, 1536))
    din("wo", (2 * INNER2, DM))
    din("w1a", (2 * DM, 2816))
    din("w1g", (2 * DM, 2816))
    din("w2", (2 * 2816, DM))
    din("qscale", (128, 2))
    din("kscale", (128, 2))
    din("l1lng", (128, 2 * GD))
    din("l1lnb", (128, 2 * GD))
    din("b1a", (128, 2 * 22))
    din("b1g", (128, 2 * 22))
    din("l1b2", (128, 2 * GD))
    din("rotq1", (128, L))
    din("rotq2", (128, L))
    din("rotk1", (128, L))
    din("rotk2", (128, L))
    din("mask_own_kq", (128, 128))
    din("mask_prev_kq", (128, 128))
    din("mask_own_qk", (128, 128))
    din("mask_prev_qk", (128, 128))
    din("alibi", (128, NH1 * (L // 128)))
    din("wgT", (DM, HID0), I32)
    din("wuT", (DM, HID0), I32)
    din("wdT", (HID0, DM), I32)
    din("ident", (128, 128))
    din("fconst", (128, 8))
    d["outT"] = nc.dram_tensor("outT", [DM, L], F32, kind="ExternalOutput").ap()
    d["scr_x"] = nc.dram_tensor("scr_x", [128, GD * L], F32).ap()
    d["kspill"] = nc.dram_tensor("kspill", [DM, L], BF16).ap()
    d["scr_bc"] = nc.dram_tensor("scr_bc", [(L // 512 if L >= 512 else 1) * 32, 512], F32).ap()

    class T:
        pass

    t = T()
    perm = bass.AutoArena(nc)
    t.X = perm.sbuf_tensor("X", [128, GD * L], F32)
    t.XB = perm.sbuf_tensor("XB", [128, GD * L], BF16)
    t.ident = perm.sbuf_tensor("identS", [128, 128], F32)
    t.ident_b = perm.sbuf_tensor("identB", [128, 128], BF16)
    t.ones_f = perm.sbuf_tensor("ones_f", [128, 128], F32)
    t.ones_b = perm.sbuf_tensor("ones_b", [128, 128], BF16)
    t.fc = perm.sbuf_tensor("fconst", [128, 8], F32)

    t.ln_r2a = perm.sbuf_tensor("ln_r2a", [128, 512], BF16)
    t.ln_r2b = perm.sbuf_tensor("ln_r2b", [128, 512], BF16)
    t.ln_mu = perm.sbuf_tensor("ln_mu", [128, 512], F32)
    t.ln_m2 = perm.sbuf_tensor("ln_m2", [128, 512], F32)
    t.ln_var = perm.sbuf_tensor("ln_var", [128, 512], F32)
    t.ln_rs = perm.sbuf_tensor("ln_rs", [128, 512], F32)
    t.ln_t1 = perm.sbuf_tensor("ln_t1", [128, 512], F32)
    perm.allocate("left")

    t.ln_eps_ap = lambda: t.fc[:, 0:1]
    t.one_ap = lambda: t.fc[:, 1:2]

    def x_col(g, sl):
        return t.X[:, g * L + sl.start : g * L + sl.stop]

    def xb_col(g, sl):
        return t.XB[:, g * L + sl.start : g * L + sl.stop]

    t.x_col = x_col
    t.xb_col = xb_col

    pw0 = es.enter_context(nc.psum_tensor("PW0", [128, 2048], F32))
    pw1 = es.enter_context(nc.psum_tensor("PW1", [128, 2048], F32))
    t.PW0 = pw0
    t.PW1 = pw1

    s_pre = ctx.sem("d_pre")
    ctx.dma("sync", t.ident[:], d["ident"][:], s_pre)
    ctx.dma("sync", t.fc[:], d["fconst"][:], s_pre)

    s_preg = ctx.sem("d_preg")
    ctx.dma("gpsimd", t.ident_b[:], d["ident"][:], s_preg)
    ctx.wait("vector", s_pre)
    ctx.wait("vector", s_preg)
    ctx.op("vector", lambda v: v.memset(t.ones_f[:], 1.0))
    ctx.op("vector", lambda v: v.memset(t.ones_b[:], 1.0))
    ctx.di("vector", ctx.sem("v_init"))

    s_in = ctx.sem("d_xin")
    for g in range(GD):
        ctx.dma("sync", t.X[:, ts(g, L)], d["xT"][ts(g), :], s_in)
    ctx.wait("vector", s_in)
    ctx.wait("vector", ctx.sem("v_init"))
    for g in range(GD):
        ctx.op("vector", lambda v, g=g: v.tensor_copy(t.XB[:, ts(g, L)], t.X[:, ts(g, L)]))
    ctx.di("vector", ctx.sem("v_init"))
    ctx.barrier("load")

    if upto >= 1:
        mar = bass.AutoArena(nc)

        class M:
            pass

        m = M()
        T_ = min(512, L)
        m.xn = mar.sbuf_tensor("m_xn", [128, GD * T_], BF16)
        m.u = mar.sbuf_tensor("m_u", [128, GI * T_], BF16)
        m.sz = mar.sbuf_tensor("m_sz", [128, GI * T_], BF16)
        m.wslab = [mar.sbuf_tensor(f"m_ws{i}", [128, 4096], BF16) for i in range(2)]
        m.upre = mar.sbuf_tensor("m_upre", [128, T_ + 3], F32)
        m.hist = mar.sbuf_tensor("m_hist", [128, GI * 3], F32)
        m.cacc = mar.sbuf_tensor("m_cacc", [128, T_], F32)
        m.dtb = mar.sbuf_tensor("m_dtb", [64, T_], BF16)
        m.sbc = mar.sbuf_tensor("m_sbc", [32, T_], F32)
        m.sbc2 = mar.sbuf_tensor("m_sbc2", [65, 11 * T_], BF16)
        m.delta = mar.sbuf_tensor("m_delta", [128, T_], F32)
        m.du = mar.sbuf_tensor("m_du", [128, T_], F32)
        m.dA = mar.sbuf_tensor("m_dA", [128, 2 * T_], F32)
        m.h = mar.sbuf_tensor("m_h", [128, T_], F32)
        m.hC = mar.sbuf_tensor("m_hC", [128, T_], BF16)
        m.y1 = mar.sbuf_tensor("m_y1", [128, T_], F32)
        m.states = mar.sbuf_tensor("m_states", [128, GI * NST], F32)
        m.cacc2 = m.y1
        m.dBu = m.cacc
        m.convw = mar.sbuf_tensor("mm_convw", [128, GI * 4], F32)
        m.convb = mar.sbuf_tensor("mm_convb", [128, GI], F32)
        m.dtpb = mar.sbuf_tensor("mm_dtpb", [128, GI], F32)
        m.negA = mar.sbuf_tensor("mm_negA", [128, GI * NST], F32)
        m.D = mar.sbuf_tensor("mm_D", [128, GI], F32)
        m.xproj = mar.sbuf_tensor("mm_xproj", [128, GI * 96], BF16)
        m.dtw = mar.sbuf_tensor("mm_dtw", [64, DIN], BF16)
        m.lng = mar.sbuf_tensor("mm_lng", [128, GD], F32)
        m.lnb = mar.sbuf_tensor("mm_lnb", [128, GD], F32)
        mar.allocate("right")
        ph_mamba(ctx, d, t, m)
        ctx.barrier("mamba")
        mar.free()
        ln_full(ctx, t, t.x_col, t.x_col, "pln0", update_x=True)

    if upto >= 2:
        far = bass.AutoArena(nc)

        class F:
            pass

        f = F()
        TH = min(1024, L)
        f.h = far.sbuf_tensor("f_h", [128, G0 * TH], BF16)
        f.gl = far.sbuf_tensor("f_gl", [128, TH], F32)
        f.wslab = [far.sbuf_tensor(f"f_ws{i}", [128, G0 * 128], BF16) for i in range(3)]
        far.allocate("right")
        ph_ffn0(ctx, d, t, f)
        ctx.barrier("ffn0")
        far.free()
        ln_full(ctx, t, t.x_col, t.x_col, "pln1", update_x=True)

    if upto >= 3 or 29 <= upto <= 36:
        if 29 <= upto <= 35:
            ctx.a1_stop = upto - 30
        aar = bass.AutoArena(nc)

        class A:
            pass

        a = A()
        NKC = L // 128
        a.kst = [aar.sbuf_tensor(f"a_kst{i}", [128, 512], BF16) for i in range(2)]
        a.khb = [aar.sbuf_tensor(f"a_khb{i}", [128, L], BF16) for i in range(2)]
        a.vt = aar.sbuf_tensor("a_vt", [128, NKC * 1040], BF16)
        a.wv_ao = aar.sbuf_tensor("a_wvao", [128, 8192], BF16)
        a.ao2 = a.wv_ao
        a.qf = aar.sbuf_tensor("a_qf", [128, GD * 512], BF16)
        a.negm = aar.sbuf_tensor("a_negm", [128, 512], BF16)
        a.lrap = aar.sbuf_tensor("a_lrap", [73, L], BF16)
        a.lraq = aar.sbuf_tensor("a_lraq", [72, 512], BF16)
        a.sq = aar.sbuf_tensor("a_sq", [128, 512], BF16)
        a.sqr = aar.sbuf_tensor("a_sqr", [128, 512], F32)
        a.mx = aar.sbuf_tensor("a_mx", [128, 4], F32)
        a.negmk = aar.sbuf_tensor("a_negmk", [128, NH1], F32)
        a.kn2 = aar.sbuf_tensor("a_kn2", [128, NH1], F32)
        a.pt = aar.sbuf_tensor("a_pt", [128, 1024], BF16)
        a.wslab = [aar.sbuf_tensor(f"a_ws{i}", [128, 2048], BF16) for i in range(2)]
        a.Aq = aar.sbuf_tensor("a_Aq", [128, GD * 8], BF16)
        a.Akv = aar.sbuf_tensor("a_Akv", [128, GD * 72], BF16)
        a.Ao = aar.sbuf_tensor("a_Ao", [64, NH1 * 72], BF16)
        a.Bpack = aar.sbuf_tensor("a_Bp", [73, 2 * DM], BF16)
        a.alibi = aar.sbuf_tensor("a_alibi", [128, NH1 * (L // 128)], F32)
        a.bq = aar.sbuf_tensor("a_bq", [128, GD], F32)
        a.bk = aar.sbuf_tensor("a_bk", [128, GD], F32)
        a.bk8 = aar.sbuf_tensor("a_bk8", [128, GD], F32)
        a.bo = aar.sbuf_tensor("a_bo", [128, GD], F32)
        a.rez = aar.sbuf_tensor("a_rez", [128, 1], F32)
        a.lrow = a.sqr[0:1, :]
        a.linv = a.sqr[64:65, :]
        aar.allocate("right")
        ph_attn1(ctx, d, t, a)
        ctx.barrier("attn1")
        aar.free()
        if upto != 36:
            ln_full(ctx, t, t.x_col, t.x_col, "pln2", update_x=True)

    if upto >= 4 and not (21 <= upto <= 36):
        s_xs = ctx.sem("d_xsave")
        ctx.dma("sync", d["scr_x"][:], t.X[:, 0 : GD * L], s_xs)
        ctx.wait("sync", s_xs)
        ctx.barrier("xsave")

        bpre_g = ctx.sem("b_preg")
        bpre = ctx.sem("b_pred")
        bar0 = bass.AutoArena(nc)

        class BS:
            pass

        bs = BS()
        bs.qs8 = bar0.sbuf_tensor("b_qs8", [128, 2], F32)
        bs.ks = bar0.sbuf_tensor("b_ks", [128, 2], F32)
        bs.lng = bar0.sbuf_tensor("b_lng", [128, 2 * GD], F32)
        bs.lnb = bar0.sbuf_tensor("b_lnb", [128, 2 * GD], F32)
        bs.b1a = bar0.sbuf_tensor("b_b1a", [128, 2 * 22], F32)
        bs.b1g = bar0.sbuf_tensor("b_b1g", [128, 2 * 22], F32)
        bs.b2 = bar0.sbuf_tensor("b_b2", [128, 2 * GD], F32)
        bar0.allocate("left")
        ctx.dma("sync", bs.qs8[:], d["qscale"][:], bpre)
        ctx.dma("sync", bs.ks[:], d["kscale"][:], bpre)
        ctx.dma("sync", bs.lng[:], d["l1lng"][:], bpre)
        ctx.dma("sync", bs.lnb[:], d["l1lnb"][:], bpre)
        ctx.dma("sync", bs.b1a[:], d["b1a"][:], bpre)
        ctx.dma("sync", bs.b1g[:], d["b1g"][:], bpre)
        ctx.dma("sync", bs.b2[:], d["l1b2"][:], bpre)
        sb_pre = ctx.sem("b_pre")
        ctx.wait("vector", bpre)
        ctx.op("vector", lambda v: v.tensor_scalar_mul(bs.qs8[:], bs.qs8[:], 8.0))
        ctx.di("vector", sb_pre)

        if upto == 21:
            ctx.dbg_qk = True
        if upto == 22:
            ctx.dbg_qk = True
            ctx.dbg_raw = True
        if upto == 23:
            ctx.dbg_qk = True
            ctx.dbg_norot = True
        nsub = {11: 1, 12: 2, 13: 3, 21: 1, 22: 1, 23: 1}.get(upto, 4)
        subi = 0
        for l in range(2):
            if subi >= nsub:
                break
            bar = bass.AutoArena(nc)

            class B:
                pass

            b = B()
            NKC = L // 128
            b.qkst = [bar.sbuf_tensor(f"b_qkst{i}", [128, 512], BF16) for i in range(2)]
            b.khb = [bar.sbuf_tensor(f"b_khb{i}", [128, 1024], BF16) for i in range(2)]
            b.qhb = [bar.sbuf_tensor(f"b_qhb{i}", [128, 512], BF16) for i in range(2)]
            b.vt2 = bar.sbuf_tensor("b_vt2", [128, NKC * 520], BF16)
            b.wv = bar.sbuf_tensor("b_wv", [128, GD * 512], BF16)
            b.wo_all = bar.sbuf_tensor("b_woall", [64, 8192], BF16)
            b.xsh0 = bar.sbuf_tensor("b_xsh0", [128, 4 * 512], BF16)
            b.rotq1 = bar.sbuf_tensor("bb_rq1", [128, L], BF16)
            b.rotq2 = bar.sbuf_tensor("bb_rq2", [128, L], BF16)
            b.rotk1 = bar.sbuf_tensor("bb_rk1", [128, L], BF16)
            b.rotk2 = bar.sbuf_tensor("bb_rk2", [128, L], BF16)
            b.mokq = bar.sbuf_tensor("bb_mokq", [128, 128], F32)
            b.mpkq = bar.sbuf_tensor("bb_mpkq", [128, 128], F32)
            b.moqk = bar.sbuf_tensor("bb_moqk", [128, 128], F32)
            b.mpqk = bar.sbuf_tensor("bb_mpqk", [128, 128], F32)
            b.sq = bar.sbuf_tensor("b_sq", [128, 512], BF16)
            b.rs = bar.sbuf_tensor("b_rs", [128, 512], F32)
            b.t0 = bar.sbuf_tensor("b_t0", [128, 512], F32)
            b.t2s = bar.sbuf_tensor("b_t2s", [128, 512], BF16)
            b.t1o = bar.sbuf_tensor("b_t1o", [128, 512], BF16)
            b.t2o = bar.sbuf_tensor("b_t2o", [128, 512], BF16)
            b.mcols = bar.sbuf_tensor("b_mcols", [128, 4], F32)
            b.mrow = bar.sbuf_tensor("b_mrow", [4, 128], F32)
            b.negm = bar.sbuf_tensor("b_negm", [128, 512], BF16)
            b.pt = bar.sbuf_tensor("b_pt", [128, 640], BF16)
            b.lrow = bar.sbuf_tensor("b_lrow", [1, 512], F32)
            b.linv = bar.sbuf_tensor("b_linv", [1, 512], F32)
            b.aol = bar.sbuf_tensor("b_aol", [64, NH2 * 512], BF16)
            b.wslab = [bar.sbuf_tensor(f"b_ws{l}{i}", [128, 1024], BF16) for i in range(2)]
            b.qs8 = bs.qs8
            b.ks = bs.ks
            b.raw = t.ln_t1
            bar.allocate("right")
            ph_ffn1_mix(ctx, d, t, b, l)
            ctx.barrier(f"mix{l}")
            bar.free()
            subi += 1
            if getattr(ctx, "dbg_stop", False):
                break
            if subi >= nsub:
                break

            bar2 = bass.AutoArena(nc)

            class B2:
                pass

            b2 = B2()
            b2.xnl = bar2.sbuf_tensor("b2_xnl", [128, GD * 512], BF16)
            b2.h = bar2.sbuf_tensor("b2_h", [128, 22 * 512], BF16)
            b2.gl = bar2.sbuf_tensor("b2_gl", [128, 512], F32)
            b2.fsh0 = bar2.sbuf_tensor("b2_fsh0", [128, 4 * 512], BF16)
            b2.wslab = [bar2.sbuf_tensor(f"b2_ws{l}{i}", [128, 2816], BF16) for i in range(3)]
            b2.b1a = bs.b1a
            b2.b1g = bs.b1g
            b2.b2 = bs.b2
            b2.lng = bs.lng
            b2.lnb = bs.lnb
            bar2.allocate("right")
            ph_ffn1_ffn(ctx, d, t, b2, l)
            ctx.barrier(f"ffn{l}")
            bar2.free()
            subi += 1
        bar0.free()

        if nsub >= 4:
            s_xr = ctx.sem("d_xread")
            s_rv = ctx.sem("v_xres")
            for g in range(GD):
                for i2 in range(L // 512):
                    sl = slice(i2 * 512, (i2 + 1) * 512)
                    ctx.wait("sync", s_rv)
                    ctx.dma("sync", t.ln_t1[:, 0:512],
                            d["scr_x"][:, g * L + sl.start : g * L + sl.stop], s_xr)
                    ctx.wait("vector", s_xr)
                    ctx.op("vector", lambda v, g=g, sl=sl: v.tensor_add(
                        t.x_col(g, sl), t.x_col(g, sl), t.ln_t1[:, 0:512]))
                    ctx.di("vector", s_rv)
            ctx.barrier("xres")
            ln_full(ctx, t, t.x_col, t.x_col, "pln3", update_x=False)

    if getattr(ctx, "dbg_stop", False):
        with nc.Block() as block:
            for eng in ENGINES:
                def make(eng):
                    def f(e):
                        for fn in ctx.streams[eng]:
                            fn(e)
                    return f
                getattr(block, eng)(make(eng))
        es.close()
        return nc
    s_out = ctx.sem("d_out")
    for g in range(GD):
        ctx.dma("sync", d["outT"][ts(g), :], t.X[:, ts(g, L)], s_out)
    ctx.wait("sync", s_out)

    with nc.Block() as block:
        for eng in ENGINES:
            def make(eng):
                def f(e):
                    for fn in ctx.streams[eng]:
                        fn(e)
                return f
            getattr(block, eng)(make(eng))

    es.close()
    build.last_ctx = ctx
    return nc



_CACHE = {}


def _pad_cols(w):
    out = np.zeros((2, DM, 2816), np.float32)
    out[:, :, : w.shape[2]] = w
    return out.reshape(2 * DM, 2816)


def _pad_rows(w):
    out = np.zeros((2, 2816, DM), np.float32)
    out[:, : w.shape[1], :] = w
    return out.reshape(2 * 2816, DM)


def _pack_b1(b):
    out = np.zeros((128, 2 * 22), np.float32)
    for l in range(2):
        bp = np.zeros(22 * 128, np.float32)
        bp[: b.shape[1]] = b[l]
        out[:, l * 22 : (l + 1) * 22] = bp.reshape(22, 128).T
    return out


def prep_inputs(inputs, L):
    consts = host_constants(L)
    shared = {
        "inproj": np.asarray(inputs["L0_attn_in_proj_w"]),
        "convw": np.asarray(inputs["L0_attn_conv_w"]),
        "convb": np.asarray(inputs["L0_attn_conv_b"]).reshape(DIN, 1),
        "xproj": np.asarray(inputs["L0_attn_x_proj_w"]),
        "dtproj": np.asarray(inputs["L0_attn_dt_proj_w"]),
        "dtprojb": np.asarray(inputs["L0_attn_dt_proj_b"]).reshape(DIN, 1),
        "alog": np.asarray(inputs["L0_attn_A_log"]),
        "dvec": np.asarray(inputs["L0_attn_D"]).reshape(DIN, 1),
        "outproj": np.asarray(inputs["L0_attn_out_proj_w"]),
        "mlng": np.asarray(inputs["L0_attn_ln_g"]).reshape(DM, 1),
        "mlnb": np.asarray(inputs["L0_attn_ln_b"]).reshape(DM, 1),
        "wgT": np.ascontiguousarray(np.asarray(inputs["L0_ffn_w_gate"]).T),
        "wuT": np.ascontiguousarray(np.asarray(inputs["L0_ffn_w_up"]).T),
        "wdT": np.ascontiguousarray(np.asarray(inputs["L0_ffn_w_down"]).T),
        "WqT": np.ascontiguousarray(np.asarray(inputs["L1_attn_Wq"]).T),
        "WkT": np.ascontiguousarray(np.asarray(inputs["L1_attn_Wk"]).T),
        "WvT": np.ascontiguousarray(np.asarray(inputs["L1_attn_Wv"]).T),
        "WoT": np.ascontiguousarray(np.asarray(inputs["L1_attn_Wo"]).T),
        "AqT": np.ascontiguousarray(np.asarray(inputs["L1_attn_Aq"]).T),
        "AkT": np.ascontiguousarray(np.asarray(inputs["L1_attn_Ak"]).T),
        "AvT": np.ascontiguousarray(np.asarray(inputs["L1_attn_Av"]).T),
        "AoT": np.ascontiguousarray(np.asarray(inputs["L1_attn_Ao"]).T),
        "BqT": np.ascontiguousarray(np.asarray(inputs["L1_attn_Bq"]).T),
        "BkT": np.ascontiguousarray(np.asarray(inputs["L1_attn_Bk"]).T),
        "BvT": np.ascontiguousarray(np.asarray(inputs["L1_attn_Bv"]).T),
        "BoT": np.ascontiguousarray(np.asarray(inputs["L1_attn_Bo"]).T),
        "bq": np.ascontiguousarray(np.asarray(inputs["L1_attn_bq"]).reshape(GD, 128).T),
        "bk": np.ascontiguousarray(np.asarray(inputs["L1_attn_bk"]).reshape(GD, 128).T),
        "bv": np.ascontiguousarray(np.asarray(inputs["L1_attn_bv"]).reshape(GD, 128).T),
        "bo": np.ascontiguousarray(np.asarray(inputs["L1_attn_bo"]).reshape(GD, 128).T),
        "bvrow": np.asarray(inputs["L1_attn_bv"]).reshape(1, DM),
        "rezero": np.asarray(inputs["L1_attn_rezero"]).reshape(1, 1),
        "alibi": consts["alibi"],
        "wqkv": np.asarray(inputs["L1_ffn_wqkv"]).reshape(2 * DM, 1536),
        "wo": np.asarray(inputs["L1_ffn_wo"]).reshape(2 * INNER2, DM),
        "w1a": _pad_cols(np.asarray(inputs["L1_ffn_w1"])[:, :, :HID2]),
        "w1g": _pad_cols(np.asarray(inputs["L1_ffn_w1"])[:, :, HID2:]),
        "w2": _pad_rows(np.asarray(inputs["L1_ffn_w2"])),
        "qscale": np.ascontiguousarray(
            np.tile(np.asarray(inputs["L1_ffn_q_scale"]).T, (2, 1))),
        "kscale": np.ascontiguousarray(
            np.tile(np.asarray(inputs["L1_ffn_k_scale"]).T, (2, 1))),
        "l1lng": np.ascontiguousarray(
            np.asarray(inputs["L1_ffn_ln_g"]).reshape(2, GD, 128)
            .transpose(2, 0, 1).reshape(128, 2 * GD)),
        "l1lnb": np.ascontiguousarray(
            np.asarray(inputs["L1_ffn_ln_b"]).reshape(2, GD, 128)
            .transpose(2, 0, 1).reshape(128, 2 * GD)),
        "b1a": _pack_b1(np.asarray(inputs["L1_ffn_b1"])[:, :HID2]),
        "b1g": _pack_b1(np.asarray(inputs["L1_ffn_b1"])[:, HID2:]),
        "l1b2": np.ascontiguousarray(
            np.asarray(inputs["L1_ffn_b2"]).reshape(2, GD, 128)
            .transpose(2, 0, 1).reshape(128, 2 * GD)),
        "rotq1": consts["rotq1"], "rotq2": consts["rotq2"],
        "rotk1": consts["rotk1"], "rotk2": consts["rotk2"],
        "mask_own_kq": consts["mask_own_kq"], "mask_prev_kq": consts["mask_prev_kq"],
        "mask_own_qk": consts["mask_own_qk"], "mask_prev_qk": consts["mask_prev_qk"],
        "ident": consts["ident"],
        "fconst": np.broadcast_to(np.array([LNEPS, 1.0, 0, 0, 0, 0, 0, 0], np.float32), (128, 8)).copy(),
    }
    x = np.asarray(inputs["L0_attn_x"])
    per_core = [{"xT": np.ascontiguousarray(x[b].T)} for b in range(x.shape[0])]
    return shared, per_core


def kernel(**inputs):
    L = 2048
    if "nc" not in _CACHE:
        _CACHE["nc"] = build(L)
    nc = _CACHE["nc"]
    shared, per_core = prep_inputs(inputs, L)
    in_maps = [dict(shared, **pc) for pc in per_core]
    res = run_bass_kernel_spmd(nc, in_maps, core_ids=list(range(NCORES)))
    outs = [np.ascontiguousarray(np.asarray(r["outT"]).T) for r in res.results]
    return np.stack(outs, axis=0)
